# version 01
# baseline (speedup 1.0000x reference)
import numpy as np

"""Raw-bass program-building framework with automatic semaphore edges.

Instructions are emitted EAGERLY through Prog.op / Prog.dma:
- eng in {'pe','dve','act','pool','sp'}; DMA queues {'sp','act','pool'}.
- reads/writes are resource keys: either a tensor handle or (handle, tag).
- The builder tracks last-writer/readers per resource and emits wait_ge
  edges (same-engine included), deduped with per-engine vector clocks.
"""
import contextlib
import concourse.bass as bass
import concourse.mybir as mybir
from concourse.alu_op_type import AluOpType as OP

F32 = mybir.dt.float32
BF16 = mybir.dt.bfloat16
I32 = mybir.dt.int32
U32 = mybir.dt.uint32
I16 = mybir.dt.int16
U16 = mybir.dt.uint16
U8 = mybir.dt.uint8
AF = mybir.ActivationFunctionType
AX = mybir.AxisListType

ENGS = ('pe', 'dve', 'act', 'pool', 'sp')
ENG_MAP = {'pe': 'tensor', 'dve': 'vector', 'act': 'scalar',
           'pool': 'gpsimd', 'sp': 'sync'}
DMAQ = {'sp': 'dsp', 'act': 'dact', 'pool': 'dpool', 'dve': 'ddve'}
NDMAG = 4
DMA_SEMS = [f"{q}{g}" for q in DMAQ.values() for g in range(NDMAG)]


class _Res:
    __slots__ = ('whole_w', 'whole_r', 'tag_w', 'tag_r')

    def __init__(self):
        self.whole_w = None
        self.whole_r = {}
        self.tag_w = {}
        self.tag_r = {}


class Prog:
    def __init__(self, name="prog"):
        self.nc = bass.Bass("TRN2", target_bir_lowering=False)
        self.stack = contextlib.ExitStack()
        self.stack.__enter__()
        self.extra_inputs = {}
        self.out_names = []
        self.counts = dict.fromkeys(list(ENGS) + DMA_SEMS, 0)
        self.dma_idx = dict.fromkeys(DMAQ.values(), 0)
        self.clock = {e: {} for e in ENGS}
        self.handles = {}
        self._names = set()
        self.scopes = {}
        self.fence_need = {}
        self.sems = {s: self.stack.enter_context(self.nc.semaphore(s))
                     for s in list(ENGS) + DMA_SEMS}
        self.block = self.stack.enter_context(self.nc.Block())
        self.finalized = False
        self.n_ops = 0

    def _uname(self, n):
        i = 0
        nn = n
        while nn in self._names:
            i += 1
            nn = f"{n}_{i}"
        self._names.add(nn)
        return nn

    def sbuf(self, name, shape, dt=F32, scope=None):
        st = self.scopes[scope] if scope else self.stack
        side = 'right' if scope else 'left'
        return st.enter_context(self.nc.sbuf_tensor(self._uname(name), list(shape), dt, side=side))

    def psum(self, name, shape, dt=F32, scope=None):
        st = self.scopes[scope] if scope else self.stack
        side = 'right' if scope else 'left'
        return st.enter_context(self.nc.psum_tensor(self._uname(name), list(shape), dt, side=side))

    def open_scope(self, name):
        assert name not in self.scopes
        self.scopes[name] = contextlib.ExitStack()
        self.scopes[name].__enter__()

    def close_scope(self, name):
        self.scopes.pop(name).__exit__(None, None, None)
        self.fence()

    def fence(self):
        self.fence_need = {s: c for s, c in self.counts.items() if c > 0}

    def dram_in(self, name, shape, dt=F32):
        return self.nc.dram_tensor(name, list(shape), dt, kind="ExternalInput").ap()

    def dram_out(self, name, shape, dt=F32):
        self.out_names.append(name)
        return self.nc.dram_tensor(name, list(shape), dt, kind="ExternalOutput").ap()

    def dram_tmp(self, name, shape, dt=F32):
        return self.nc.dram_tensor(self._uname(name), list(shape), dt, kind="Internal").ap()

    def const(self, name, arr, dt=F32):
        name = self._uname(name)
        npdt = {F32: np.float32, I32: np.int32, I16: np.int16,
                U32: np.uint32, U16: np.uint16, U8: np.uint8}.get(dt)
        if npdt is not None:
            arr = np.ascontiguousarray(np.asarray(arr).astype(npdt))
        self.extra_inputs[name] = arr
        return self.nc.dram_tensor(name, list(arr.shape), dt, kind="ExternalInput").ap()

    def _get_res(self, h):
        k = id(h)
        r = self.handles.get(k)
        if r is None:
            r = self.handles[k] = _Res()
        return r

    @staticmethod
    def _hk(r):
        return (r[0], r[1]) if isinstance(r, tuple) else (r, None)

    def _dep_read(self, r, toks):
        h, tag = self._hk(r)
        res = self.handles.get(id(h))
        if res is None:
            return
        if res.whole_w:
            toks.append(res.whole_w)
        if tag is not None:
            t = res.tag_w.get(tag)
            if t:
                toks.append(t)
        else:
            toks.extend(res.tag_w.values())

    def _dep_write(self, r, toks):
        h, tag = self._hk(r)
        res = self.handles.get(id(h))
        if res is None:
            return
        if res.whole_w:
            toks.append(res.whole_w)
        toks.extend(res.whole_r.items())
        if tag is not None:
            t = res.tag_w.get(tag)
            if t:
                toks.append(t)
            toks.extend(res.tag_r.get(tag, {}).items())
        else:
            toks.extend(res.tag_w.values())
            for d in res.tag_r.values():
                toks.extend(d.items())

    def _note_read(self, r, tok):
        h, tag = self._hk(r)
        res = self._get_res(h)
        sname, val = tok
        if tag is None:
            if res.whole_r.get(sname, 0) < val:
                res.whole_r[sname] = val
        else:
            d = res.tag_r.setdefault(tag, {})
            if d.get(sname, 0) < val:
                d[sname] = val

    def _note_write(self, r, tok):
        h, tag = self._hk(r)
        res = self._get_res(h)
        if tag is None:
            res.whole_w = tok
            res.whole_r = {}
            res.tag_w = {}
            res.tag_r = {}
        else:
            res.tag_w[tag] = tok
            res.tag_r[tag] = {}

    def _emit(self, eng, sem_name, inc, fn, reads, writes, self_serialize=False):
        toks = []
        if self_serialize and self.counts[sem_name] > 0:
            toks.append((sem_name, self.counts[sem_name]))
        for r in reads:
            self._dep_read(r, toks)
        for w in writes:
            self._dep_write(w, toks)
        need = {}
        for (sname, val) in toks:
            if need.get(sname, -1) < val:
                need[sname] = val
        for sname, val in self.fence_need.items():
            if need.get(sname, -1) < val:
                need[sname] = val
        clk = self.clock[eng]
        waits = []
        for sname, val in need.items():
            if clk.get(sname, 0) >= val:
                continue
            clk[sname] = val
            waits.append((sname, val))
        self.counts[sem_name] += inc
        tok = (sem_name, self.counts[sem_name])
        sems = self.sems
        section = getattr(self.block, ENG_MAP[eng])

        def body(e):
            for (sname, val) in waits:
                e.wait_ge(sems[sname], val)
            instr = fn(e)
            if instr is None:
                raise RuntimeError(f"op fn returned None on {eng}")
            instr.then_inc(sems[sem_name], inc)
        section(body)
        self.n_ops += 1
        for r in reads:
            self._note_read(r, tok)
        for w in writes:
            self._note_write(w, tok)
        return tok

    def op(self, eng, fn, reads=(), writes=()):
        assert eng in ('pe', 'dve', 'act', 'pool')
        return self._emit(eng, eng, 1, fn, reads, writes)

    def dma(self, queue, fn, reads=(), writes=()):
        assert queue in ('sp', 'act', 'pool', 'dve')
        base = DMAQ[queue]
        g = self.dma_idx[base] % NDMAG
        self.dma_idx[base] += 1
        return self._emit(queue, f"{base}{g}", 16, fn, reads, writes,
                          self_serialize=True)

    def raw(self, eng, fn):
        section = getattr(self.block, ENG_MAP[eng])
        section(lambda e: fn(e))

    def finalize(self):
        assert not self.finalized
        self.finalized = True
        sems = self.sems
        counts = dict(self.counts)

        @self.block.sync
        def _(sync):
            for sname, cnt in counts.items():
                if cnt > 0 and sname != 'sp':
                    sync.wait_ge(sems[sname], cnt)
        self.stack.__exit__(None, None, None)
        return self.nc

    def run(self, per_core_inputs, n_cores=8, trace=False):
        from concourse.bass_utils import run_bass_kernel_spmd
        if not self.finalized:
            self.finalize()
        in_maps = []
        for c in range(n_cores):
            m = dict(per_core_inputs[c])
            m.update(self.extra_inputs)
            in_maps.append(m)
        return run_bass_kernel_spmd(self.nc, in_maps, core_ids=list(range(n_cores)), trace=trace)

    def run_sim(self, inputs_core0):
        from concourse.bass_interp import MultiCoreSim
        if not self.finalized:
            self.finalize()
        sim = MultiCoreSim(self.nc, num_cores=1)
        core = sim.cores[0]
        m = dict(inputs_core0)
        m.update(self.extra_inputs)
        for k, v in m.items():
            t = core.tensor(k)
            t[...] = np.asarray(v).reshape(t.shape)
        sim.simulate()
        return {name: core.tensor(name).copy() for name in self.out_names}


def ceil_div(a, b):
    return (a + b - 1) // b


class Ctx:
    def __init__(self, p: Prog):
        self.p = p
        self.ident = {}
        self.ones_col = None

    def get_ident(self, dt=BF16):
        if dt not in self.ident:
            t = self.p.sbuf("ident", [128, 128], dt)
            ap = self.p.const("ident_c", np.eye(128, dtype=np.float32), F32)
            if dt == F32:
                self.p.dma('sp', lambda e: e.dma_start(out=t[:], in_=ap[:, :]), writes=[t])
            else:
                self.p.dma('pool', lambda e: e.dma_start(out=t[:], in_=ap[:, :]), writes=[t])
            self.ident[dt] = t
        return self.ident[dt]

    def get_ones_col(self, dt=F32):
        if self.ones_col is None:
            self.ones_col = {}
        if dt not in self.ones_col:
            t = self.p.sbuf("ones_col", [128, 1], dt)
            self.p.op('pool', lambda e: e.memset(t[:], 1.0), writes=[t])
            self.ones_col[dt] = t
        return self.ones_col[dt]

    def get_ident_block(self, hd, dt=BF16):
        key = ('blk', hd, dt)
        if key not in self.ident:
            t = self.p.sbuf(f"identb{hd}", [128, hd], dt)
            arr = np.tile(np.eye(hd, dtype=np.float32), (128 // hd, 1))
            ap = self.p.const(f"identb{hd}_c", arr, F32)
            self.p.dma('pool', lambda e: e.dma_start(out=t[:], in_=ap[:, :]), writes=[t])
            self.ident[key] = t
        return self.ident[key]

    def cval(self, val):
        if not hasattr(self, '_cvals'):
            self._cvals = {}
        if val not in self._cvals:
            t = self.p.sbuf(f"cv", [128, 1], F32)
            self.p.op('pool', lambda e: e.memset(t[:], float(val)), writes=[t])
            self._cvals[val] = t
        return self._cvals[val]

"""Common sublayer scaffolding: input staging + LN, streamed matmul, output path."""



def col_of(vec_np):
    n = vec_np.shape[-1]
    return np.ascontiguousarray(vec_np.reshape(n // 128, 128).T)


def load_col_vec(p: Prog, name, dram_ap, n, dt=F32):
    nc_ = ceil_div(n, 128)
    t = p.sbuf(name, [128, nc_], dt)
    p.dma('sp', lambda e: e.dma_start(out=t[:], in_=dram_ap[:, :]), writes=[t])
    return t


def stage_input_ln(p: Prog, ctx: Ctx, x_dram, T=2048, D=1024, keep_f32=False,
                   keep_x_fm=False, eps=1e-5, f32_scope=None, gb=None, eps_inner=None):
    nd = D // 128
    nt = T // 128
    ln_bf = [p.sbuf(f"lnbf{c}", [128, T], BF16) for c in range(nd)]
    ln_f32 = [p.sbuf(f"lnf{c}", [128, T], F32, scope=f32_scope) for c in range(nd)] if keep_f32 else None
    x_fm = [p.sbuf(f"xfm{c}", [128, T], F32) for c in range(nd)] if keep_x_fm else None

    ident = ctx.get_ident(F32)
    ones = ctx.get_ones_col()
    SCP = p._uname('_lnstage')
    p.open_scope(SCP)
    xt_tm = [p.sbuf(f"xt_tm{i}", [128, D], F32, scope=SCP) for i in range(2)]
    xw = [p.sbuf(f"xw{i}", [128, 128], F32, scope=SCP) for i in range(3)]
    sq = p.sbuf("sq", [128, 128], F32, scope=SCP)
    tp_ps = [p.psum(f"tp_ps{i}", [128, 128], F32, scope=SCP) for i in range(2)]
    sx_ps = p.psum("sx_ps", [128, 512], F32, scope=SCP)
    sq_ps = p.psum("sq_ps", [128, 512], F32, scope=SCP)
    bc_ps = [p.psum(f"bc_ps{i}", [128, 128], F32, scope=SCP) for i in range(2)]
    rowbuf = p.sbuf("rowbuf", [128, 512], F32, scope=SCP)
    RA, RB, RC, RD = (slice(0, 128), slice(128, 256), slice(256, 384), slice(384, 512))

    for ti in range(nt):
        tmt = xt_tm[ti % 2]
        p.dma('sp', lambda e: e.dma_start(out=tmt[:], in_=x_dram[ti * 128:(ti + 1) * 128, :]),
              writes=[tmt])

        def get_fm(c, dest_w):
            ps = tp_ps[c % 2]
            p.op('pe', lambda e: e.transpose(ps[:], tmt[:, c * 128:(c + 1) * 128], ident[:]),
                 reads=[tmt, ident], writes=[ps])
            p.op('dve', lambda e: e.tensor_copy(dest_w[:], ps[:]),
                 reads=[ps], writes=[dest_w])

        for c in range(nd):
            w = xw[c % 2]
            get_fm(c, w)
            if keep_x_fm:
                p.op('act', lambda e: e.copy(x_fm[c][:, ti * 128:(ti + 1) * 128], w[:]),
                     reads=[w], writes=[(x_fm[c], ti)])
            p.op('pe', lambda e: e.matmul(sx_ps[0:1, 0:128], ones[:], w[:],
                                          start=(c == 0), stop=(c == nd - 1)),
                 reads=[w, ones], writes=[(sx_ps, 'acc')])
            p.op('act', lambda e: e.activation(sq[:], w[:], AF.Square),
                 reads=[w], writes=[sq])
            p.op('pe', lambda e: e.matmul(sq_ps[0:1, 0:128], ones[:], sq[:],
                                          start=(c == 0), stop=(c == nd - 1)),
                 reads=[sq, ones], writes=[(sq_ps, 'acc')])
        p.op('dve', lambda e: e.tensor_scalar(rowbuf[0:1, RC], sx_ps[0:1, 0:128],
                                              1.0 / D, None, OP.mult, OP.bypass),
             reads=[(sx_ps, 'acc')], writes=[(rowbuf, 'C')])
        p.op('dve', lambda e: e.tensor_scalar(rowbuf[0:1, RD], sq_ps[0:1, 0:128],
                                              1.0 / D, None, OP.mult, OP.bypass),
             reads=[(sq_ps, 'acc')], writes=[(rowbuf, 'D')])
        p.op('dve', lambda e: e.tensor_tensor(rowbuf[0:1, RA], rowbuf[0:1, RC],
                                              rowbuf[0:1, RC], OP.mult),
             reads=[(rowbuf, 'C')], writes=[(rowbuf, 'A')])
        p.op('dve', lambda e: e.tensor_tensor(rowbuf[0:1, RB], rowbuf[0:1, RD],
                                              rowbuf[0:1, RA], OP.subtract),
             reads=[(rowbuf, 'D'), (rowbuf, 'A')], writes=[(rowbuf, 'B')])
        epsb = ctx.cval(eps if eps_inner is None else eps_inner * eps)
        if eps_inner is None:
            p.op('act', lambda e: e.activation(rowbuf[0:1, RA], rowbuf[0:1, RB], AF.Ln,
                                               bias=epsb[0:1, :]),
                 reads=[(rowbuf, 'B'), epsb], writes=[(rowbuf, 'A')])
        else:
            p.op('act', lambda e: e.activation(rowbuf[0:1, RA], rowbuf[0:1, RB], AF.Ln,
                                               bias=epsb[0:1, :], scale=1.0 + eps_inner),
                 reads=[(rowbuf, 'B'), epsb], writes=[(rowbuf, 'A')])
        p.op('act', lambda e: e.activation(rowbuf[0:1, RA], rowbuf[0:1, RA], AF.Exp, scale=-0.5),
             reads=[(rowbuf, 'A')], writes=[(rowbuf, 'A')])
        p.op('dve', lambda e: e.scalar_tensor_tensor(rowbuf[0:1, RB], rowbuf[0:1, RC],
                                                     -1.0, rowbuf[0:1, RA], OP.mult, OP.mult),
             reads=[(rowbuf, 'C'), (rowbuf, 'A')], writes=[(rowbuf, 'B')])
        p.op('pe', lambda e: e.matmul(bc_ps[0][:, 0:128], ones[0:1, :].broadcast_to((1, 128)),
                                      rowbuf[0:1, RA], start=True, stop=True),
             reads=[(rowbuf, 'A'), ones], writes=[bc_ps[0]])
        p.op('pe', lambda e: e.matmul(bc_ps[1][:, 0:128], ones[0:1, :].broadcast_to((1, 128)),
                                      rowbuf[0:1, RB], start=True, stop=True),
             reads=[(rowbuf, 'B'), ones], writes=[bc_ps[1]])
        for c in range(nd):
            if keep_x_fm:
                src_ap = x_fm[c][:, ti * 128:(ti + 1) * 128]
                rtag = [(x_fm[c], ti)]
            else:
                w = xw[c % 2]
                get_fm(c, w)
                src_ap = w[:]
                rtag = [w]
            tmp = xw[2]
            p.op('dve', lambda e: e.tensor_tensor(tmp[:], src_ap, bc_ps[0][:, 0:128], OP.mult),
                 reads=rtag + [bc_ps[0]], writes=[tmp])
            if keep_f32:
                p.op('dve', lambda e: e.tensor_tensor(
                    ln_f32[c][:, ti * 128:(ti + 1) * 128], tmp[:], bc_ps[1][:, 0:128], OP.add),
                    reads=[tmp, bc_ps[1]], writes=[(ln_f32[c], ti)])
                p.op('act', lambda e: e.copy(ln_bf[c][:, ti * 128:(ti + 1) * 128],
                                             ln_f32[c][:, ti * 128:(ti + 1) * 128]),
                     reads=[(ln_f32[c], ti)], writes=[(ln_bf[c], ti)])
            elif gb is not None:
                gcol, bcol = gb
                tmp2 = xw[0] if keep_x_fm else xw[(c + 1) % 2]
                p.op('dve', lambda e: e.tensor_tensor(tmp2[:], tmp[:], bc_ps[1][:, 0:128],
                                                      OP.add),
                     reads=[tmp, bc_ps[1]], writes=[tmp2])
                p.op('dve', lambda e: e.tensor_scalar(
                    ln_bf[c][:, ti * 128:(ti + 1) * 128], tmp2[:], gcol[:, c:c + 1],
                    bcol[:, c:c + 1], OP.mult, OP.add),
                    reads=[tmp2, gcol, bcol], writes=[(ln_bf[c], ti)])
            else:
                p.op('dve', lambda e: e.tensor_tensor(
                    ln_bf[c][:, ti * 128:(ti + 1) * 128], tmp[:], bc_ps[1][:, 0:128], OP.add),
                    reads=[tmp, bc_ps[1]], writes=[(ln_bf[c], ti)])
    p.close_scope(SCP)
    return dict(ln_bf=ln_bf, ln_f32=ln_f32, x_fm=x_fm)


class WStream:

    def __init__(self, p: Prog, name, dram_ap, nbuf=3, slice_w=128, dt=BF16):
        self.p = p
        self.dram = dram_ap
        self.bufs = [p.sbuf(f"{name}_b{i}", [128, slice_w], dt) for i in range(nbuf)]
        self.i = 0

    def fetch(self, r0, c0, rows=128, cols=128, queue='pool'):
        b = self.bufs[self.i % len(self.bufs)]
        self.i += 1
        self.p.dma(queue, lambda e: e.dma_start(
            out=b[0:rows, 0:cols], in_=self.dram[r0:r0 + rows, c0:c0 + cols]),
            writes=[b])
        return b


def stream_matmul(p: Prog, name, w_dram, din, dout, x_chunks, T, epilogue,
                  x_tag_whole=True, queue='pool', psums=None, nbuf=3,
                  x_slice_of=None):
    nk = ceil_div(din, 128)
    nm = ceil_div(dout, 128)
    ws = WStream(p, name + "_w", w_dram, nbuf=nbuf)
    if psums is None:
        psums = [p.psum(f"{name}_ps{i}", [128, min(T, 2048)], F32) for i in range(2)]
    NS = 512
    for mc in range(nm):
        ps = psums[mc % len(psums)]
        for kc in range(nk):
            rows = min(128, din - kc * 128)
            cols = min(128, dout - mc * 128)
            wb = ws.fetch(kc * 128, mc * 128, rows, cols, queue=queue)
            xt = x_chunks[kc]
            xr = xt if x_tag_whole else (xt, 'all')
            for n0 in range(0, T, NS):
                nn = min(NS, T - n0)
                xap = xt[0:rows, n0:n0 + nn] if x_slice_of is None else x_slice_of(kc, rows, n0, nn)
                p.op('pe', lambda e: e.matmul(
                    ps[0:cols, n0:n0 + nn], wb[0:rows, 0:cols], xap,
                    start=(kc == 0), stop=(kc == nk - 1)),
                    reads=[wb, xr], writes=[(ps, n0)])
        epilogue(mc, ps)


def store_residual_out(p: Prog, ctx: Ctx, f_fm, x_dram, y_dram, T=2048, D=1024, f_dt=F32):
    nd = D // 128
    nt = T // 128
    ident = ctx.get_ident(f_dt)
    SCP = p._uname('_store')
    p.open_scope(SCP)
    xt = [p.sbuf(f"sr_x{i}", [128, D], F32, scope=SCP) for i in range(2)]
    ot = [p.sbuf(f"sr_o{i}", [128, D], F32, scope=SCP) for i in range(2)]
    tps = [p.psum(f"sr_ps{i}", [128, 128], f_dt, scope=SCP) for i in range(2)]
    for ti in range(nt):
        x_t = xt[ti % 2]
        o_t = ot[ti % 2]
        p.dma('sp', lambda e: e.dma_start(out=x_t[:], in_=x_dram[ti * 128:(ti + 1) * 128, :]),
              writes=[x_t])
        for c in range(nd):
            ps = tps[c % 2]
            p.op('pe', lambda e: e.transpose(
                ps[:], f_fm[c][:, ti * 128:(ti + 1) * 128], ident[:]),
                reads=[(f_fm[c], ti), ident], writes=[ps])
            p.op('dve', lambda e: e.tensor_tensor(
                o_t[:, c * 128:(c + 1) * 128], ps[:], x_t[:, c * 128:(c + 1) * 128], OP.add),
                reads=[ps, x_t], writes=[(o_t, c)])
        p.dma('sp', lambda e: e.dma_start(out=y_dram[ti * 128:(ti + 1) * 128, :], in_=o_t[:]),
              reads=[(o_t, c) for c in range(nd)])
    p.close_scope(SCP)


def proj_fm(p: Prog, name, w_dram, x_chunks, T, din, dout, bias_col=None,
            out_dt=BF16, act=None, act_scale=1.0, queue='pool', out_chunks=None,
            fetch_w=512, x_reads=None, epilogue=None, bias_off=0, x_off=0):
    nk = ceil_div(din, 128)
    nm = ceil_div(dout, 128)
    if out_chunks is None and epilogue is None:
        out_chunks = [p.sbuf(f"{name}_o{mc}", [128, T], out_dt) for mc in range(nm)]
    SCP = p._uname(f'_proj_{name}')
    p.open_scope(SCP)
    bw = fetch_w // 128
    wstrip = [[p.sbuf(f"{name}_w{b}_{kc}", [128, fetch_w], BF16, scope=SCP)
               for kc in range(nk)] for b in range(2)]
    psums = [p.psum(f"{name}_ps{i}", [128, min(T, 2048)], F32, scope=SCP) for i in range(2)]
    NS = 512
    mc = 0
    blki = 0
    while mc < nm:
        blk = min(bw, nm - mc)
        ws = wstrip[blki % 2]
        blki += 1
        for kc in range(nk):
            p.dma(queue, lambda e: e.dma_start(
                out=ws[kc][0:128, 0:blk * 128],
                in_=w_dram[kc * 128:(kc + 1) * 128, mc * 128:(mc + blk) * 128]),
                writes=[ws[kc]])
        for j in range(blk):
            ps = psums[(mc + j) % 2]
            for kc in range(nk):
                xr = [x_chunks[kc]] if x_reads is None else x_reads(kc)
                for n0 in range(0, T, NS):
                    nn = min(NS, T - n0)
                    p.op('pe', lambda e: e.matmul(
                        ps[:, n0:n0 + nn], ws[kc][:, j * 128:(j + 1) * 128],
                        x_chunks[kc][:, x_off + n0:x_off + n0 + nn],
                        start=(kc == 0), stop=(kc == nk - 1)),
                        reads=[ws[kc]] + xr, writes=[(ps, n0)])
            ptags = [(ps, n0) for n0 in range(0, T, NS)]
            mcj = mc + j
            if epilogue is not None:
                epilogue(mcj, ps, ptags)
            elif act is not None:
                if bias_col is not None:
                    p.op('act', lambda e: e.activation(
                        out_chunks[mcj][:], ps[:, 0:T], act, scale=act_scale,
                        bias=bias_col[:, bias_off + mcj:bias_off + mcj + 1]),
                        reads=ptags + [bias_col], writes=[out_chunks[mcj]])
                else:
                    p.op('act', lambda e: e.activation(
                        out_chunks[mcj][:], ps[:, 0:T], act, scale=act_scale),
                        reads=ptags, writes=[out_chunks[mcj]])
            elif bias_col is not None:
                p.op('dve', lambda e: e.tensor_scalar(
                    out_chunks[mcj][:], ps[:, 0:T], bias_col[:, bias_off + mcj:bias_off + mcj + 1], None,
                    OP.add, OP.bypass),
                    reads=ptags + [bias_col], writes=[out_chunks[mcj]])
            else:
                p.op('dve', lambda e: e.tensor_copy(out_chunks[mcj][:], ps[:, 0:T]),
                     reads=ptags, writes=[out_chunks[mcj]])
        mc += blk
    p.close_scope(SCP)
    return out_chunks


def ln_fm_sbuf(p, ctx, src, out_chunks, T, D, gcol=None, bcol=None, eps=1e-5,
               src_reads=None, ti0=0, ti1=None, out_off0=False):
    ones = ctx.get_ones_col()
    onesb = ctx.get_ones_col(BF16)
    SCP = p._uname('_lnfm')
    p.open_scope(SCP)
    nd = D // 128
    nt = T // 128
    if ti1 is None:
        ti1 = nt
    sq = p.sbuf("sq", [128, 128], F32, scope=SCP)
    sx_ps = p.psum("sx_ps", [128, 512], F32, scope=SCP)
    sq_ps = p.psum("sq_ps", [128, 512], F32, scope=SCP)
    bc_ps = [p.psum(f"bc_ps{i}", [128, 128], F32, scope=SCP) for i in range(2)]
    rowbuf = p.sbuf("rowbuf", [128, 512], F32, scope=SCP)
    tmp = p.sbuf("tmp", [128, 128], F32, scope=SCP)
    RA, RB, RC, RD = (slice(0, 128), slice(128, 256), slice(256, 384), slice(384, 512))
    epsb = ctx.cval(eps)
    for ti in range(ti0, ti1):
        for c in range(nd):
            sl = src[c][:, ti * 128:(ti + 1) * 128]
            rd = [src[c]] if src_reads is None else src_reads(c, ti)
            p.op('pe', (lambda sl_, c_: lambda e: e.matmul(
                sx_ps[0:1, 0:128], onesb[:], sl_, start=(c_ == 0), stop=(c_ == nd - 1)))(sl, c),
                reads=rd + [onesb], writes=[(sx_ps, 'a')])
            p.op('act', (lambda sl_: lambda e: e.activation(sq[:], sl_, AF.Square))(sl),
                 reads=rd, writes=[sq])
            p.op('pe', (lambda c_: lambda e: e.matmul(
                sq_ps[0:1, 0:128], ones[:], sq[:], start=(c_ == 0), stop=(c_ == nd - 1)))(c),
                reads=[sq, ones], writes=[(sq_ps, 'a')])
        p.op('dve', lambda e: e.tensor_scalar(rowbuf[0:1, RC], sx_ps[0:1, 0:128],
                                              1.0 / D, None, OP.mult, OP.bypass),
             reads=[(sx_ps, 'a')], writes=[(rowbuf, 'C')])
        p.op('dve', lambda e: e.tensor_scalar(rowbuf[0:1, RD], sq_ps[0:1, 0:128],
                                              1.0 / D, None, OP.mult, OP.bypass),
             reads=[(sq_ps, 'a')], writes=[(rowbuf, 'D')])
        p.op('dve', lambda e: e.tensor_tensor(rowbuf[0:1, RA], rowbuf[0:1, RC],
                                              rowbuf[0:1, RC], OP.mult),
             reads=[(rowbuf, 'C')], writes=[(rowbuf, 'A')])
        p.op('dve', lambda e: e.tensor_tensor(rowbuf[0:1, RB], rowbuf[0:1, RD],
                                              rowbuf[0:1, RA], OP.subtract),
             reads=[(rowbuf, 'D'), (rowbuf, 'A')], writes=[(rowbuf, 'B')])
        p.op('act', lambda e: e.activation(rowbuf[0:1, RA], rowbuf[0:1, RB], AF.Ln,
                                           bias=epsb[0:1, :]),
             reads=[(rowbuf, 'B'), epsb], writes=[(rowbuf, 'A')])
        p.op('act', lambda e: e.activation(rowbuf[0:1, RA], rowbuf[0:1, RA], AF.Exp,
                                           scale=-0.5),
             reads=[(rowbuf, 'A')], writes=[(rowbuf, 'A')])
        p.op('dve', lambda e: e.scalar_tensor_tensor(rowbuf[0:1, RB], rowbuf[0:1, RC],
                                                     -1.0, rowbuf[0:1, RA], OP.mult, OP.mult),
             reads=[(rowbuf, 'C'), (rowbuf, 'A')], writes=[(rowbuf, 'B')])
        p.op('pe', lambda e: e.matmul(bc_ps[0][:, 0:128], ones[0:1, :].broadcast_to((1, 128)),
                                      rowbuf[0:1, RA], start=True, stop=True),
             reads=[(rowbuf, 'A'), ones], writes=[bc_ps[0]])
        p.op('pe', lambda e: e.matmul(bc_ps[1][:, 0:128], ones[0:1, :].broadcast_to((1, 128)),
                                      rowbuf[0:1, RB], start=True, stop=True),
             reads=[(rowbuf, 'B'), ones], writes=[bc_ps[1]])
        for c in range(nd):
            sl = src[c][:, ti * 128:(ti + 1) * 128]
            rd = [src[c]] if src_reads is None else src_reads(c, ti)
            p.op('dve', (lambda sl_: lambda e: e.tensor_tensor(
                tmp[:], sl_, bc_ps[0][:, 0:128], OP.mult))(sl),
                reads=rd + [bc_ps[0]], writes=[tmp])
            p.op('dve', lambda e: e.tensor_tensor(tmp[:], tmp[:], bc_ps[1][:, 0:128], OP.add),
                 reads=[tmp, bc_ps[1]], writes=[tmp])
            to = (ti - ti0) if out_off0 else ti
            if gcol is not None:
                p.op('dve', (lambda c_, to_: lambda e: e.tensor_scalar(
                    out_chunks[c_][:, to_ * 128:(to_ + 1) * 128], tmp[:],
                    gcol[:, c_:c_ + 1], bcol[:, c_:c_ + 1], OP.mult, OP.add))(c, to),
                    reads=[tmp, gcol, bcol], writes=[(out_chunks[c], to)])
            else:
                p.op('act', (lambda c_, to_: lambda e: e.copy(
                    out_chunks[c_][:, to_ * 128:(to_ + 1) * 128], tmp[:]))(c, to),
                    reads=[tmp], writes=[(out_chunks[c], to)])
    p.close_scope(SCP)



"""Multi-head attention cores, feature-major layout."""



def attention_heads(p: Prog, ctx: Ctx, name, heads, get_q, get_k, get_v, put_out,
                    T=2048, hd_qk=64, hd_v=64, scale=1.0, kv_len=None,
                    kblock_range=None, mask_bias=None, prepare=None,
                    sps_holder=None):
    kv_len = kv_len or T
    ident = ctx.get_ident(BF16)
    identb = ctx.get_ident_block(hd_v, BF16)
    SCP = p._uname(f'_attn_{name}')
    p.open_scope(SCP)
    nq = T // 128
    nk = kv_len // 128

    vT = p.sbuf(f"{name}_vT", [128, nk * hd_v], BF16, scope=SCP)
    tps = p.psum(f"{name}_tps", [128, 128], BF16, scope=SCP)
    s_ps = p.psum(f"{name}_sps", [128, 2048], F32, scope=SCP)
    o_ps = p.psum(f"{name}_ops", [128, 128], F32, scope=SCP)
    pt_ps = p.psum(f"{name}_ptps", [128, 128], BF16, scope=SCP)
    pbf = p.sbuf(f"{name}_pbf", [128, 2048], BF16, scope=SCP)
    ptb = [p.sbuf(f"{name}_ptb{i}", [128, 128], BF16, scope=SCP) for i in range(2)]
    mrow = p.sbuf(f"{name}_mrow", [128, 8], F32, scope=SCP)
    if sps_holder is not None:
        sps_holder['sps'] = s_ps

    NS = 512
    for h in range(heads):
        if prepare is not None:
            prepare(h)
        qt, qr0, qrd = get_q(h)
        kt, kr0, krd = get_k(h)
        vt, vr0, vrd = get_v(h)
        for kc in range(nk):
            p.op('pe', lambda e: e.transpose(
                tps[0:128, 0:hd_v], vt[vr0:vr0 + hd_v, kc * 128:(kc + 1) * 128],
                identb[vr0:vr0 + hd_v, 0:hd_v]),
                reads=vrd + [identb], writes=[tps])
            p.op('dve', lambda e: e.tensor_copy(vT[:, kc * hd_v:(kc + 1) * hd_v],
                                                tps[0:128, 0:hd_v]),
                 reads=[tps], writes=[(vT, kc)])
        for ti in range(nq):
            kb0, kb1 = (0, nk) if kblock_range is None else kblock_range(ti)
            kw = (kb1 - kb0) * 128
            for n0 in range(kb0 * 128, kb1 * 128, NS):
                nn = min(NS, kb1 * 128 - n0)
                p.op('pe', lambda e: e.matmul(
                    s_ps[:, n0 - kb0 * 128:n0 - kb0 * 128 + nn],
                    qt[qr0:qr0 + hd_qk, ti * 128:(ti + 1) * 128],
                    kt[kr0:kr0 + hd_qk, n0:n0 + nn], start=True, stop=True),
                    reads=qrd + krd, writes=[(s_ps, n0)])
            sp_tags = [(s_ps, n0) for n0 in range(kb0 * 128, kb1 * 128, NS)]
            if mask_bias is not None:
                for kc in range(kb0, kb1):
                    m = mask_bias(ti, kc)
                    if m is None:
                        continue
                    ap_, hdl = m
                    c0 = (kc - kb0) * 128
                    p.op('dve', (lambda ap2, c0_: lambda e: e.tensor_tensor(
                        s_ps[:, c0_:c0_ + 128], s_ps[:, c0_:c0_ + 128], ap2, OP.add))(ap_, c0),
                        reads=sp_tags + [hdl], writes=sp_tags)
            p.op('dve', lambda e: e.tensor_reduce(mrow[:, 0:1], s_ps[:, 0:kw],
                                                  AX.X, OP.max),
                 reads=sp_tags, writes=[(mrow, 'm')])
            p.op('dve', lambda e: e.tensor_scalar(mrow[:, 1:2], mrow[:, 0:1],
                                                  -scale, None, OP.mult, OP.bypass),
                 reads=[(mrow, 'm')], writes=[(mrow, 'nm')])
            p.op('act', lambda e: e.activation(pbf[:, 0:kw], s_ps[:, 0:kw], AF.Exp,
                                               bias=mrow[:, 1:2], scale=scale,
                                               accum_out=mrow[:, 2:3]),
                 reads=sp_tags + [(mrow, 'nm')], writes=[pbf, (mrow, 's')])
            p.op('dve', lambda e: e.reciprocal(mrow[:, 3:4], mrow[:, 2:3]),
                 reads=[(mrow, 's')], writes=[(mrow, 'r')])
            p.op('dve', lambda e: e.tensor_scalar(pbf[:, 0:kw], pbf[:, 0:kw],
                                                  mrow[:, 3:4], None, OP.mult, OP.bypass),
                 reads=[pbf, (mrow, 'r')], writes=[pbf])
            nkc = kb1 - kb0
            for j, kc in enumerate(range(kb0, kb1)):
                pt = ptb[j % 2]
                p.op('pe', (lambda kc_: lambda e: e.transpose(
                    pt_ps[:], pbf[:, (kc_ - kb0) * 128:(kc_ - kb0 + 1) * 128], ident[:]))(kc),
                    reads=[pbf, ident], writes=[pt_ps])
                p.op('dve', (lambda pt_: lambda e: e.tensor_copy(pt_[:], pt_ps[:]))(pt),
                     reads=[pt_ps], writes=[pt])
                p.op('pe', (lambda kc_, pt_, j_: lambda e: e.matmul(
                    o_ps[0:hd_v, 0:128], vT[:, kc_ * hd_v:(kc_ + 1) * hd_v], pt_[:],
                    start=(j_ == 0), stop=(j_ == nkc - 1)))(kc, pt, j),
                    reads=[(vT, kc), pt], writes=[o_ps])
            put_out(h, ti, o_ps[0:hd_v, 0:128], [o_ps])
    p.close_scope(SCP)


def attention_full(p: Prog, ctx: Ctx, name, qf, kf, vf, out_chunks, T=2048,
                   heads=16, hd=64, scale=1.0, kv_len=None,
                   kblock_range=None, mask_bias=None):
    per_chunk = 128 // hd

    def mk(lst):
        def g(h):
            ch, r0 = h // per_chunk, (h % per_chunk) * hd
            return lst[ch], r0, [lst[ch]]
        return g

    def put_out(h, ti, ps_ap, reads):
        ch, r0 = h // per_chunk, (h % per_chunk) * hd
        p.op('dve', lambda e: e.tensor_copy(
            out_chunks[ch][r0:r0 + hd, ti * 128:(ti + 1) * 128], ps_ap),
            reads=reads, writes=[(out_chunks[ch], (h, ti))])

    attention_heads(p, ctx, name, heads, mk(qf), mk(kf), mk(vf), put_out,
                    T=T, hd_qk=hd, hd_v=hd, scale=scale, kv_len=kv_len,
                    kblock_range=kblock_range, mask_bias=mask_bias)

"""L0_attn_a: MHA with RoPE (16 heads, hd=64), biases on q/k/v/o, scale 1/8.
Weights host-transposed (in,out); biases col_of.
"""




def rope_tables(T, hd=64):
    inv_freq = 1.0 / 10000.0 ** (np.arange(0, hd, 2, dtype=np.float32) / hd)
    t = np.arange(T, dtype=np.float32)[:, None]
    fr = t * inv_freq[None, :]
    emb = np.concatenate([fr, fr], 1)
    return np.cos(emb).T.copy(), np.sin(emb).T.copy()


def apply_rope(p, ctx, name, chunks, T, cosf, sinf_signed, swap_st):
    SCP = p._uname(f'_rope_{name}')
    p.open_scope(SCP)
    sw_ps = p.psum(f"{name}_swps", [128, 2048], F32, scope=SCP)
    tmp = p.sbuf(f"{name}_tmp", [128, T], BF16, scope=SCP)
    NS = 512
    for c, q in enumerate(chunks):
        for n0 in range(0, T, NS):
            nn = min(NS, T - n0)
            p.op('pe', lambda e: e.matmul(sw_ps[:, n0:n0 + nn], swap_st[:],
                                          q[:, n0:n0 + nn], start=True, stop=True),
                 reads=[q, swap_st], writes=[(sw_ps, n0)])
        stags = [(sw_ps, n0) for n0 in range(0, T, NS)]
        p.op('dve', lambda e: e.tensor_tensor(tmp[:], sw_ps[:, 0:T], sinf_signed[:], OP.mult),
             reads=stags + [sinf_signed], writes=[tmp])
        p.op('dve', lambda e: e.tensor_tensor(q[:], q[:], cosf[:], OP.mult),
             reads=[q, cosf], writes=[q])
        p.op('dve', lambda e: e.tensor_tensor(q[:], q[:], tmp[:], OP.add),
             reads=[q, tmp], writes=[q])
    p.close_scope(SCP)


def build_l0_attn_a(T=2048):
    p = Prog("l0a")
    ctx = Ctx(p)
    D = 1024
    x = p.dram_in('x', (T, D), F32)
    y = p.dram_out('y', (T, D), F32)
    qw = p.dram_in('qw', (D, D), F32)
    kw = p.dram_in('kw', (D, D), F32)
    vw = p.dram_in('vw', (D, D), F32)
    ow = p.dram_in('ow', (D, D), F32)
    qb = p.dram_in('qb', (128, 8), F32)
    kb = p.dram_in('kb', (128, 8), F32)
    vb = p.dram_in('vb', (128, 8), F32)
    ob = p.dram_in('ob', (128, 8), F32)

    cos64, sin64 = rope_tables(T)
    cos_full = np.tile(cos64, (2, 1))
    sin_sgn = np.concatenate([-sin64[:32], sin64[32:]], 0)
    sin_full = np.tile(sin_sgn, (2, 1))
    sw = np.zeros((128, 128), np.float32)
    for m in range(128):
        base = (m // 64) * 64
        off = m % 64
        sw_src = base + ((off + 32) % 64)
        sw[sw_src, m] = 1.0
    cosf = p.sbuf("cosf", [128, T], BF16)
    sinf = p.sbuf("sinf", [128, T], BF16)
    swap_st = p.sbuf("swapst", [128, 128], BF16)
    cg = p.const('cosc', cos_full, F32)
    sg = p.const('sinc', sin_full, F32)
    swc = p.const('swapc', sw, F32)
    p.dma('pool', lambda e: e.dma_start(out=cosf[:], in_=cg[:, :]), writes=[cosf])
    p.dma('pool', lambda e: e.dma_start(out=sinf[:], in_=sg[:, :]), writes=[sinf])
    p.dma('pool', lambda e: e.dma_start(out=swap_st[:], in_=swc[:, :]), writes=[swap_st])

    stg = stage_input_ln(p, ctx, x, T=T)
    ln = stg['ln_bf']
    qbc = load_col_vec(p, 'qbc', qb, D)
    kbc = load_col_vec(p, 'kbc', kb, D)
    vbc = load_col_vec(p, 'vbc', vb, D)
    obc = load_col_vec(p, 'obc', ob, D)

    ao = [p.sbuf(f"ao{c}", [128, T], BF16) for c in range(8)]
    p.open_scope('qkv')
    qf = [p.sbuf(f"qf{c}", [128, T], BF16, scope='qkv') for c in range(8)]
    kf = [p.sbuf(f"kf{c}", [128, T], BF16, scope='qkv') for c in range(8)]
    vf = [p.sbuf(f"vf{c}", [128, T], BF16, scope='qkv') for c in range(8)]
    proj_fm(p, 'q', qw, ln, T, D, D, bias_col=qbc, out_chunks=qf)
    proj_fm(p, 'k', kw, ln, T, D, D, bias_col=kbc, out_chunks=kf)
    proj_fm(p, 'v', vw, ln, T, D, D, bias_col=vbc, out_chunks=vf)
    apply_rope(p, ctx, 'rq', qf, T, cosf, sinf, swap_st)
    apply_rope(p, ctx, 'rk', kf, T, cosf, sinf, swap_st)
    attention_full(p, ctx, 'at', qf, kf, vf, ao, T=T, heads=16, hd=64, scale=0.125)
    p.close_scope('qkv')

    out_fm = proj_fm(p, 'o', ow, ao, T, D, D, bias_col=obc, out_dt=BF16)
    store_residual_out(p, ctx, out_fm, x, y, T=T, D=D, f_dt=BF16)
    return p


def ref_l0_attn_a(x, qw, qb, kw, kb, vw, vb, ow, ob):
    xs = x.astype(np.float64)
    mu = xs.mean(-1, keepdims=True)
    z = (xs - mu) / np.sqrt(xs.var(-1, keepdims=True) + 1e-5)
    T, D = x.shape
    H, hd = 16, 64
    q = (z @ qw.T.astype(np.float64) + qb).reshape(T, H, hd)
    k = (z @ kw.T.astype(np.float64) + kb).reshape(T, H, hd)
    v = (z @ vw.T.astype(np.float64) + vb).reshape(T, H, hd)
    cos, sin = rope_tables(T)
    cos = cos.T[:, None, :]
    sin = sin.T[:, None, :]

    def rot(t):
        t1, t2 = t[..., :32], t[..., 32:]
        rh = np.concatenate([-t2, t1], -1)
        return t * cos + rh * sin
    q, k = rot(q), rot(k)
    q = q.transpose(1, 0, 2) / np.sqrt(hd)
    k = k.transpose(1, 0, 2)
    v = v.transpose(1, 0, 2)
    s = q @ k.transpose(0, 2, 1)
    e = np.exp(s - s.max(-1, keepdims=True))
    a = e / e.sum(-1, keepdims=True)
    ctx = (a @ v).transpose(1, 0, 2).reshape(T, D)
    return x + (ctx @ ow.T.astype(np.float64) + ob).astype(np.float32)



"""L0_attn_b: Transformer-XL relative attention.
Inner LN(gamma,beta) fused with outer LN. q,k,v,r projections (no bias);
AC = (q+u)k^T;  BD = (q+v_b)r^T rel-shifted via DRAM bounce;  softmax((AC+BDs)/8).
Host: wq/wk/wv/wr (1024, 1024) natural (x@W); wo (1024, 1024) natural (out@Wo);
bo col (128, 8); gcol/bcol (128, 8); ucol/vcol (128, 8); rel_pos_emb (10001, 1024).
"""



MAX_LEN = 5000


def build_l0_attn_b(T=2048):
    p = Prog("l0b")
    ctx = Ctx(p)
    D = 1024
    L = T
    x = p.dram_in('x', (T, D), F32)
    y = p.dram_out('y', (T, D), F32)
    wq = p.dram_in('wq', (D, D), F32)
    wk = p.dram_in('wk', (D, D), F32)
    wv = p.dram_in('wv', (D, D), F32)
    wr = p.dram_in('wr', (D, D), F32)
    wo = p.dram_in('wo', (D, D), F32)
    bo = p.dram_in('bo', (128, 8), F32)
    gcol_d = p.dram_in('gcol', (128, 8), F32)
    bcol_d = p.dram_in('bcol', (128, 8), F32)
    ucol_d = p.dram_in('ucol', (128, 8), F32)
    vcol_d = p.dram_in('vcol', (128, 8), F32)
    rpe = p.dram_in('rpe', (10001, D), F32)

    NREL = 2 * L
    WIN = L + 128
    PITCH = L + 256
    r0_idx = MAX_LEN - (L - 1)

    gcol = load_col_vec(p, 'gc', gcol_d, D)
    bcol = load_col_vec(p, 'bc', bcol_d, D)
    ucol = load_col_vec(p, 'uc', ucol_d, D)
    vcol = load_col_vec(p, 'vc', vcol_d, D)
    boc = load_col_vec(p, 'boc', bo, D)

    stg = stage_input_ln(p, ctx, x, T=T, gb=(gcol, bcol), eps_inner=1e-6)
    ln = stg['ln_bf']
    identf = ctx.get_ident(F32)
    identb = ctx.get_ident(BF16)
    identblk = ctx.get_ident_block(64, BF16)

    r_dram = p.dram_tmp('r_dram', (D, NREL), BF16)
    p.open_scope('rproj')
    re_tm = p.sbuf("re_tm", [128, D], F32, scope='rproj')
    re_fm = [p.sbuf(f"re_fm{c}", [128, 128], BF16, scope='rproj') for c in range(8)]
    t_ps = p.psum("t_ps", [128, 128], F32, scope='rproj')
    rp2 = [p.psum(f"rp2_{i}", [128, 128], F32, scope='rproj') for i in range(2)]
    r_sb = [p.sbuf(f"r_sb{i}", [128, 128], BF16, scope='rproj') for i in range(2)]
    wr_all = [p.sbuf(f"wr_a{kc}", [128, 1024], BF16, scope='rproj') for kc in range(8)]
    for kc in range(8):
        p.dma('pool', (lambda kc_: lambda e: e.dma_start(
            out=wr_all[kc_][:], in_=wr[kc_ * 128:(kc_ + 1) * 128, :]))(kc),
            writes=[wr_all[kc]])
    nrt = NREL // 128
    for rt in range(nrt):
        p.dma('sp', (lambda rt_: lambda e: e.dma_start(
            out=re_tm[:], in_=rpe[r0_idx + rt_ * 128:r0_idx + (rt_ + 1) * 128, :]))(rt),
            writes=[re_tm])
        for c in range(8):
            p.op('pe', (lambda c_: lambda e: e.transpose(
                t_ps[:], re_tm[:, c_ * 128:(c_ + 1) * 128], identf[:]))(c),
                reads=[re_tm, identf], writes=[t_ps])
            p.op('dve', (lambda c_: lambda e: e.tensor_copy(re_fm[c_][:], t_ps[:]))(c),
                 reads=[t_ps], writes=[re_fm[c]])
        for mc in range(8):
            ps = rp2[mc % 2]
            sb = r_sb[mc % 2]
            for kc in range(8):
                p.op('pe', (lambda mc_, kc_, ps_: lambda e: e.matmul(
                    ps_[:, 0:128], wr_all[kc_][:, mc_ * 128:(mc_ + 1) * 128],
                    re_fm[kc_][:], start=(kc_ == 0), stop=(kc_ == 7)))(mc, kc, ps),
                    reads=[wr_all[kc], re_fm[kc]], writes=[ps])
            p.op('act', (lambda ps_, sb_: lambda e: e.copy(sb_[:], ps_[:, 0:128]))(ps, sb),
                 reads=[ps], writes=[sb])
            p.dma('sp', (lambda mc_, rt_, sb_: lambda e: e.dma_start(
                out=r_dram[mc_ * 128:(mc_ + 1) * 128, rt_ * 128:(rt_ + 1) * 128],
                in_=sb_[:]))(mc, rt, sb),
                reads=[sb])
    p.close_scope('rproj')

    p.open_scope('QKV')
    qf = [p.sbuf(f"qf{c}", [128, T], BF16, scope='QKV') for c in range(8)]
    kf = [p.sbuf(f"kf{c}", [128, T], BF16, scope='QKV') for c in range(8)]
    vfc = [p.sbuf(f"vf{c}", [128, T], BF16, scope='QKV') for c in range(8)]
    proj_fm(p, 'q', wq, ln, T, D, D, out_chunks=qf, fetch_w=256)
    proj_fm(p, 'k', wk, ln, T, D, D, out_chunks=kf, fetch_w=256)
    proj_fm(p, 'v', wv, ln, T, D, D, out_chunks=vfc, fetch_w=256)
    for c in range(8):
        p.op('dve', (lambda c_: lambda e: e.tensor_scalar(
            qf[c_][:], qf[c_][:], ucol[:, c_:c_ + 1], None, OP.add, OP.bypass))(c),
            reads=[qf[c], ucol], writes=[qf[c]])
    dvu = p.sbuf("dvu", [128, 8], F32)
    p.op('dve', lambda e: e.tensor_tensor(dvu[:], vcol_d if False else vcol[:], ucol[:],
                                          OP.subtract),
         reads=[vcol, ucol], writes=[dvu])

    ao = [p.sbuf(f"ao{c}", [128, T], BF16) for c in range(8)]
    bd_dram = p.dram_tmp('bd_dram', (2, 128 * PITCH), BF16)

    p.open_scope('at')
    vT = p.sbuf("at_vT", [128, 16 * 64], BF16, scope='at')
    tps = p.psum("at_tps", [128, 128], BF16, scope='at')
    s_ps = p.psum("at_sps", [128, 2048], F32, scope='at')
    o_ps = p.psum("at_ops", [128, 128], F32, scope='at')
    pt_ps = p.psum("at_ptps", [128, 128], BF16, scope='at')
    bd2_ps = p.psum("at_bd2", [128, 128], F32, scope='at')
    pbf = p.sbuf("at_pbf", [128, 2048], BF16, scope='at')
    ptb = [p.sbuf(f"at_ptb{i}", [128, 128], BF16, scope='at') for i in range(2)]
    mrow = p.sbuf("at_mrow", [128, 8], F32, scope='at')
    qv_t = p.sbuf("at_qv", [64, 128], BF16, scope='at')
    bdtmp = p.sbuf("at_bdtmp", [128, PITCH], BF16, scope='at')
    bdsh = [p.sbuf(f"at_bdsh{i}", [128, T], BF16, scope='at') for i in range(2)]
    rwin = [p.sbuf(f"at_rw{i}", [64, WIN], BF16, scope='at') for i in range(2)]
    NS = min(512, T)
    nq = T // 128
    for h in range(16):
        ch, hr = h // 2, (h % 2) * 64
        for kc in range(nq):
            p.op('pe', (lambda kc_: lambda e: e.transpose(
                tps[0:128, 0:64], vfc[ch][hr:hr + 64, kc_ * 128:(kc_ + 1) * 128],
                identblk[hr:hr + 64, 0:64]))(kc),
                reads=[vfc[ch], identblk], writes=[tps])
            p.op('dve', (lambda kc_: lambda e: e.tensor_copy(
                vT[:, kc_ * 64:(kc_ + 1) * 64], tps[0:128, 0:64]))(kc),
                reads=[tps], writes=[(vT, kc)])
        for ti in range(nq):
            par = ti % 2
            w0 = (nq - 1) * 128 - ti * 128
            p.op('dve', (lambda ti_: lambda e: e.tensor_scalar(
                qv_t[0:64, :], qf[ch][hr:hr + 64, ti_ * 128:(ti_ + 1) * 128],
                dvu[hr:hr + 64, ch:ch + 1], None, OP.add, OP.bypass))(ti),
                reads=[qf[ch], dvu], writes=[qv_t])
            rw = rwin[par]
            p.dma('act', (lambda rw_, w0_: lambda e: e.dma_start(
                out=rw_[0:64, 0:WIN],
                in_=r_dram[h * 64:(h + 1) * 64, w0_:w0_ + WIN]))(rw, w0),
                writes=[rw])
            for n0 in range(0, T, NS):
                nn = min(NS, T - n0)
                p.op('pe', (lambda n0_, nn_, rw_: lambda e: e.matmul(
                    s_ps[:, n0_:n0_ + nn_], qv_t[0:64, :], rw_[0:64, n0_:n0_ + nn_],
                    start=True, stop=True))(n0, nn, rw),
                    reads=[qv_t, rw], writes=[(s_ps, n0)])
            p.op('pe', (lambda rw_: lambda e: e.matmul(
                bd2_ps[:, 0:128], qv_t[0:64, :], rw_[0:64, T:T + 128],
                start=True, stop=True))(rw),
                reads=[qv_t, rw], writes=[bd2_ps])
            p.op('act', lambda e: e.copy(bdtmp[:, 0:T], s_ps[:, 0:T]),
                 reads=[(s_ps, n0) for n0 in range(0, T, NS)], writes=[(bdtmp, 'a')])
            p.op('act', lambda e: e.copy(bdtmp[:, T:T + 128], bd2_ps[:, 0:128]),
                 reads=[bd2_ps], writes=[(bdtmp, 'b')])
            p.dma('sp', (lambda par_: lambda e: e.dma_start(
                out=bd_dram[par_:par_ + 1, :].rearrange(
                    "o (p c) -> (o p) c", p=128)[:, 0:T + 128],
                in_=bdtmp[:, 0:T + 128]))(par),
                reads=[(bdtmp, 'a'), (bdtmp, 'b')], writes=[(bd_dram, par)])
            for n0 in range(0, T, NS):
                p.op('pe', (lambda n0_, ti_: lambda e: e.matmul(
                    s_ps[:, n0_:n0_ + NS], qf[ch][hr:hr + 64, ti_ * 128:(ti_ + 1) * 128],
                    kf[ch][hr:hr + 64, n0_:n0_ + NS], start=True, stop=True))(n0, ti),
                    reads=[qf[ch], kf[ch]], writes=[(s_ps, n0)])
            bsh = bdsh[par]
            import concourse.bass as bass_mod
            sh_ap = bass_mod.AP(tensor=bd_dram.tensor, offset=par * 128 * PITCH + 127,
                                ap=[[PITCH - 1, 128], [1, T]])
            p.dma('sp', (lambda bsh_, sh_ap_: lambda e: e.dma_start(
                out=bsh_[:], in_=sh_ap_))(bsh, sh_ap),
                reads=[(bd_dram, par)], writes=[bsh])
            sp_tags = [(s_ps, n0) for n0 in range(0, T, NS)]
            p.op('dve', (lambda bsh_: lambda e: e.tensor_tensor(
                s_ps[:, 0:T], s_ps[:, 0:T], bsh_[:], OP.add))(bsh),
                reads=sp_tags + [bsh], writes=sp_tags)
            p.op('dve', lambda e: e.tensor_reduce(mrow[:, 0:1], s_ps[:, 0:T], AX.X, OP.max),
                 reads=sp_tags, writes=[(mrow, 'm')])
            p.op('dve', lambda e: e.tensor_scalar(mrow[:, 1:2], mrow[:, 0:1], -0.125,
                                                  None, OP.mult, OP.bypass),
                 reads=[(mrow, 'm')], writes=[(mrow, 'nm')])
            p.op('act', lambda e: e.activation(pbf[:, 0:T], s_ps[:, 0:T], AF.Exp,
                                               bias=mrow[:, 1:2], scale=0.125,
                                               accum_out=mrow[:, 2:3]),
                 reads=sp_tags + [(mrow, 'nm')], writes=[pbf, (mrow, 's')])
            p.op('dve', lambda e: e.reciprocal(mrow[:, 3:4], mrow[:, 2:3]),
                 reads=[(mrow, 's')], writes=[(mrow, 'r')])
            p.op('dve', lambda e: e.tensor_scalar(pbf[:, 0:T], pbf[:, 0:T], mrow[:, 3:4],
                                                  None, OP.mult, OP.bypass),
                 reads=[pbf, (mrow, 'r')], writes=[pbf])
            for j in range(nq):
                pt = ptb[j % 2]
                p.op('pe', (lambda j_: lambda e: e.transpose(
                    pt_ps[:], pbf[:, j_ * 128:(j_ + 1) * 128], identb[:]))(j),
                    reads=[pbf, identb], writes=[pt_ps])
                p.op('dve', (lambda pt_: lambda e: e.tensor_copy(pt_[:], pt_ps[:]))(pt),
                     reads=[pt_ps], writes=[pt])
                p.op('pe', (lambda j_, pt_: lambda e: e.matmul(
                    o_ps[0:64, 0:128], vT[:, j_ * 64:(j_ + 1) * 64], pt_[:],
                    start=(j_ == 0), stop=(j_ == nq - 1)))(j, pt),
                    reads=[(vT, j), pt], writes=[o_ps])
            p.op('dve', (lambda ti_: lambda e: e.tensor_copy(
                ao[ch][hr:hr + 64, ti_ * 128:(ti_ + 1) * 128], o_ps[0:64, 0:128]))(ti),
                reads=[o_ps], writes=[(ao[ch], (h, ti))])
    p.close_scope('at')
    p.close_scope('QKV')

    out_fm = proj_fm(p, 'o', wo, ao, T, D, D, bias_col=boc, out_dt=BF16, fetch_w=256)
    store_residual_out(p, ctx, out_fm, x, y, T=T, D=D, f_dt=BF16)
    return p


def ref_l0_attn_b(x, gamma, beta, Wq, Wk, Wv, Wr, u_bias, v_bias, Wo, bo, rpe):
    xs = x.astype(np.float64)
    mu = xs.mean(-1, keepdims=True)
    z = (xs - mu) / np.sqrt(xs.var(-1, keepdims=True) + 1e-5)
    mu2 = z.mean(-1, keepdims=True)
    v2 = ((z - mu2) ** 2).mean(-1, keepdims=True)
    xn = (z - mu2) / np.sqrt(v2 + 1e-6) * gamma + beta
    T, D = x.shape
    H, hd = 16, 64

    def heads(t):
        return t.reshape(T, H, hd).transpose(1, 0, 2)
    q = heads(xn @ Wq.astype(np.float64))
    k = heads(xn @ Wk.astype(np.float64))
    v = heads(xn @ Wv.astype(np.float64))
    idx = np.arange(2 * T - 1) - (T - 1) + MAX_LEN
    r_emb = rpe[idx].astype(np.float64)
    r = (r_emb @ Wr.astype(np.float64)).reshape(2 * T - 1, H, hd).transpose(1, 0, 2)
    AC = np.einsum('hld,hmd->hlm', q + u_bias[:, None, :], k)
    BD = np.einsum('hld,hrd->hlr', q + v_bias[:, None, :], r)
    Bp = np.pad(BD, ((0, 0), (0, 0), (1, 0)))
    Bp = Bp.reshape(H, 2 * T, T)
    BDs = Bp[:, 1:, :].reshape(H, T, 2 * T - 1)[:, :, :T]
    s = (AC + BDs) / np.sqrt(hd)
    e = np.exp(s - s.max(-1, keepdims=True))
    a = e / e.sum(-1, keepdims=True)
    out = np.einsum('hlm,hmd->hld', a, v).transpose(1, 0, 2).reshape(T, D)
    return x + (out @ Wo.astype(np.float64) + bo).astype(np.float32)



"""L0_ffn: SwiGLU MLP + product-key expert retrieval (PEER), exact dense-gate.
Host: wg/wu (1024, 4096) natural; wd (4096, 1024) natural; bg/bu cols (128, 32);
bd col (128, 8); wq (1024, 512) natural; keysx/keysy (4*64, 64) [keys[h,:,p,:]
transposed to (n, k) per head -> stacked (256, 64)]; det (1024, 4096)
[down_embed.T]; ue (4096, 1024) natural.
"""




def build_l0_ffn(T=2048):
    p = Prog("l0f")
    ctx = Ctx(p)
    D, HID, NE = 1024, 4096, 4096
    x = p.dram_in('x', (T, D), F32)
    y = p.dram_out('y', (T, D), F32)
    wg = p.dram_in('wg', (D, HID), F32)
    wu = p.dram_in('wu', (D, HID), F32)
    wd = p.dram_in('wd', (HID, D), F32)
    bgc_d = p.dram_in('bgc', (128, 32), F32)
    buc_d = p.dram_in('buc', (128, 32), F32)
    bdc_d = p.dram_in('bdc', (128, 8), F32)
    wq = p.dram_in('wq', (D, 512), F32)
    keysx = p.dram_in('keysx', (256, 64), F32)
    keysy = p.dram_in('keysy', (256, 64), F32)
    det = p.dram_in('det', (D, NE), F32)
    ue = p.dram_in('ue', (NE, D), F32)

    stg = stage_input_ln(p, ctx, x, T=T)
    ln = stg['ln_bf']
    identb = ctx.get_ident(BF16)
    bgc = load_col_vec(p, 'bg', bgc_d, HID)
    buc = load_col_vec(p, 'bu', buc_d, HID)
    bdc = load_col_vec(p, 'bd', bdc_d, D)

    kx = p.sbuf("kx", [128, 2 * 64], BF16)
    kxs = [p.sbuf(f"kxs{i}", [128, 64], BF16) for i in range(2)]
    kys = [p.sbuf(f"kys{i}", [128, 64], BF16) for i in range(2)]
    for i in range(2):
        p.dma('pool', (lambda i_: lambda e: e.dma_start(
            out=kxs[i_][:], in_=keysx[i_ * 128:(i_ + 1) * 128, :]))(i), writes=[kxs[i]])
        p.dma('pool', (lambda i_: lambda e: e.dma_start(
            out=kys[i_][:], in_=keysy[i_ * 128:(i_ + 1) * 128, :]))(i), writes=[kys[i]])

    qf = [p.sbuf(f"qf{c}", [128, T], BF16) for c in range(4)]
    proj_fm(p, 'q', wq, ln, T, D, 512, out_chunks=qf)

    out_fm = [p.sbuf(f"of{c}", [128, T], BF16) for c in range(8)]
    TB = min(512, T)
    ntb = TB // 128
    nh = HID // 128

    for tb0 in range(0, T, TB):
        p.open_scope('blk')
        hch = [p.sbuf(f"h{hc}", [128, TB], BF16, scope='blk') for hc in range(nh)]
        p.open_scope('gu')
        sil = p.sbuf("sil", [128, TB], BF16, scope='gu')

        def epi_g(m, ps, ptags):
            p.op('act', lambda e: e.activation(
                sil[:], ps[:, 0:TB], AF.Silu, bias=bgc[:, m:m + 1]),
                reads=ptags + [bgc], writes=[sil])

        def epi_u(m, ps, ptags):
            p.op('dve', lambda e: e.tensor_scalar(
                hch[m][:], ps[:, 0:TB], buc[:, m:m + 1], None, OP.add, OP.bypass),
                reads=ptags + [buc], writes=[(hch[m], 'u')])
            p.op('dve', lambda e: e.tensor_tensor(
                hch[m][:], hch[m][:], sil[:], OP.mult),
                reads=[(hch[m], 'u'), sil], writes=[(hch[m], 'h')])

        for hc in range(nh):
            proj_fm(p, f'pg{hc}', wg[0:D, hc * 128:(hc + 1) * 128], ln, TB, D, 128,
                    epilogue=lambda mcj, ps, ptags: epi_g(hc, ps, ptags),
                    x_off=tb0, fetch_w=128)
            proj_fm(p, f'pu{hc}', wu[0:D, hc * 128:(hc + 1) * 128], ln, TB, D, 128,
                    epilogue=lambda mcj, ps, ptags: epi_u(hc, ps, ptags),
                    x_off=tb0, fetch_w=128)
        p.close_scope('gu')
        p.open_scope('dn')

        def epi_d(m, ps, ptags):
            p.op('dve', lambda e: e.tensor_scalar(
                out_fm[m][:, tb0:tb0 + TB], ps[:, 0:TB], bdc[:, m:m + 1], None,
                OP.add, OP.bypass),
                reads=ptags + [bdc], writes=[(out_fm[m], tb0)])

        proj_fm(p, 'pd', wd, hch, TB, HID, D, epilogue=epi_d, fetch_w=128)
        p.close_scope('dn')
        p.close_scope('blk')

    for tb0 in range(0, T, TB):
        p.open_scope('pe_blk')
        gT = [p.sbuf(f"gT{ec}", [128, TB], BF16, scope='pe_blk') for ec in range(32)]
        p.open_scope('pe_ps')
        sim_ps = p.psum("sim_ps", [128, 512], F32, scope='pe_ps')
        w_ps = p.psum("w_ps", [128, 512], F32, scope='pe_ps')
        gt_ps = p.psum("gt_ps", [128, 128], BF16, scope='pe_ps')
        srow = p.sbuf("srow", [128, 64], F32, scope='pe_blk')
        sx8 = p.sbuf("sx8", [128, 8], F32, scope='pe_blk')
        sy8 = p.sbuf("sy8", [128, 8], F32, scope='pe_blk')
        gg8 = p.sbuf("gg8", [128, 8], F32, scope='pe_blk')
        g64 = p.sbuf("g64", [128, 64], F32, scope='pe_blk')
        gx = p.sbuf("gx", [128, 64], F32, scope='pe_blk')
        gy = p.sbuf("gy", [128, 64], F32, scope='pe_blk')
        mrow = p.sbuf("mrowp", [128, 8], F32, scope='pe_blk')
        pacc = p.sbuf("pacc", [128, NE], F32, scope='pe_blk')
        outer = p.sbuf("outer", [128, NE], F32, scope='pe_blk')
        wsil = p.sbuf("wsil", [128, NE], BF16, scope='pe_blk')
        gds = p.sbuf("gds", [128, NE], BF16, scope='pe_blk')
        ws = [p.sbuf(f"dws{b}", [128, 512], BF16, scope='pe_blk') for b in range(3)]
        for sj in range(ntb):
            ti = tb0 // 128 + sj
            for h in range(4):
                hcx = h // 2
                hrx = (h % 2) * 64
                for pi, (ksrc, s8, gdst) in enumerate([(kxs, sx8, gx), (kys, sy8, gy)]):
                    qch = qf[hcx] if pi == 0 else qf[2 + hcx]
                    p.op('pe', (lambda qch_, hrx_, ti_, ks_: lambda e: e.matmul(
                        sim_ps[:, 0:64], qch_[hrx_:hrx_ + 64, ti_ * 128:(ti_ + 1) * 128],
                        ks_[hcx][hrx_:hrx_ + 64, 0:64], start=True, stop=True))(
                            qch, hrx, ti, ksrc),
                        reads=[qch, ksrc[hcx]], writes=[sim_ps])
                    p.op('dve', lambda e: e.tensor_copy(srow[:, 0:64], sim_ps[:, 0:64]),
                         reads=[sim_ps], writes=[srow])
                    p.op('dve', (lambda s8_: lambda e: e.max(s8_[:], srow[:, 0:64]))(s8),
                         reads=[srow], writes=[s8])
                    p.op('dve', (lambda s8_: lambda e: e.tensor_scalar(
                        mrow[:, 0:1], s8_[:, 0:1], -1.0, None, OP.mult, OP.bypass))(s8),
                        reads=[s8], writes=[(mrow, 'nm')])
                    p.op('act', (lambda gd_: lambda e: e.activation(
                        gd_[:], srow[:, 0:64], AF.Exp, bias=mrow[:, 0:1]))(gdst),
                        reads=[srow, (mrow, 'nm')], writes=[(gdst, 'e')])
                    p.op('dve', (lambda s8_: lambda e: e.tensor_scalar(
                        g64[:], srow[:, 0:64], s8_[:, 7:8], None, OP.is_ge, OP.bypass))(s8),
                        reads=[srow, s8], writes=[g64])
                    p.op('dve', (lambda gd_: lambda e: e.tensor_tensor(
                        gd_[:], gd_[:], g64[:], OP.mult))(gdst),
                        reads=[(gdst, 'e'), g64], writes=[gdst])
                p.op('dve', lambda e: e.tensor_tensor(
                    g64[:].rearrange("p (a b) -> p a b", b=8),
                    sx8[:].unsqueeze(2).broadcast_to((128, 8, 8)),
                    sy8[:].unsqueeze(1).broadcast_to((128, 8, 8)), OP.add),
                    reads=[sx8, sy8], writes=[g64])
                p.op('dve', lambda e: e.max(gg8[:], g64[:]), reads=[g64], writes=[gg8])
                p.op('dve', lambda e: e.tensor_scalar(mrow[:, 1:2], gg8[:, 0:1], -1.0,
                                                      None, OP.mult, OP.bypass),
                     reads=[gg8], writes=[(mrow, 'nm2')])
                p.op('act', lambda e: e.activation(srow[:, 0:8], gg8[:], AF.Exp,
                                                   bias=mrow[:, 1:2],
                                                   accum_out=mrow[:, 2:3]),
                     reads=[gg8, (mrow, 'nm2')], writes=[(srow, 'z8'), (mrow, 'Z')])
                p.op('dve', lambda e: e.reciprocal(mrow[:, 3:4], mrow[:, 2:3]),
                     reads=[(mrow, 'Z')], writes=[(mrow, 'rZ')])
                p.op('dve', lambda e: e.tensor_scalar(gx[:], gx[:], mrow[:, 3:4], None,
                                                      OP.mult, OP.bypass),
                     reads=[gx, (mrow, 'rZ')], writes=[gx])
                p.op('dve', lambda e: e.tensor_tensor(mrow[:, 4:5], srow[:, 7:8],
                                                      mrow[:, 3:4], OP.mult),
                     reads=[(srow, 'z8'), (mrow, 'rZ')], writes=[(mrow, 'tz')])
                p.op('dve', lambda e: e.tensor_tensor(
                    outer[:].rearrange("p (a b) -> p a b", b=64),
                    gx[:].unsqueeze(2).broadcast_to((128, 64, 64)),
                    gy[:].unsqueeze(1).broadcast_to((128, 64, 64)), OP.mult),
                    reads=[gx, gy], writes=[outer])
                if h == 0:
                    p.op('dve', lambda e: e.scalar_tensor_tensor(
                        pacc[:], outer[:], mrow[:, 4:5], outer[:], OP.is_ge, OP.mult),
                        reads=[outer, (mrow, 'tz')], writes=[pacc])
                else:
                    p.op('dve', lambda e: e.scalar_tensor_tensor(
                        outer[:], outer[:], mrow[:, 4:5], outer[:], OP.is_ge, OP.mult),
                        reads=[outer, (mrow, 'tz')], writes=[outer])
                    p.op('dve', lambda e: e.tensor_tensor(pacc[:], pacc[:], outer[:], OP.add),
                         reads=[pacc, outer], writes=[pacc])
            for es in range(NE // 512):
                for kc in range(8):
                    wb = ws[kc % 3]
                    p.dma('pool', (lambda wb_, kc_, es_: lambda e: e.dma_start(
                        out=wb_[:], in_=det[kc_ * 128:(kc_ + 1) * 128,
                                            es_ * 512:(es_ + 1) * 512]))(wb, kc, es),
                        writes=[wb])
                    p.op('pe', (lambda wb_, kc_, ti_: lambda e: e.matmul(
                        w_ps[:, 0:512], ln[kc_][:, ti_ * 128:(ti_ + 1) * 128], wb_[:],
                        start=(kc_ == 0), stop=(kc_ == 7)))(wb, kc, ti),
                        reads=[wb, ln[kc]], writes=[w_ps])
                p.op('act', (lambda es_: lambda e: e.activation(
                    wsil[:, es_ * 512:(es_ + 1) * 512], w_ps[:, 0:512], AF.Silu))(es),
                    reads=[w_ps], writes=[(wsil, es)])
            p.op('dve', lambda e: e.tensor_tensor(gds[:], wsil[:], pacc[:], OP.mult),
                 reads=[pacc] + [(wsil, es) for es in range(8)], writes=[gds])
            for ec in range(32):
                p.op('pe', (lambda ec_: lambda e: e.transpose(
                    gt_ps[:], gds[:, ec_ * 128:(ec_ + 1) * 128], identb[:]))(ec),
                    reads=[gds, identb], writes=[gt_ps])
                p.op('dve', (lambda ec_, sj_: lambda e: e.tensor_copy(
                    gT[ec_][:, sj_ * 128:(sj_ + 1) * 128], gt_ps[:]))(ec, sj),
                    reads=[gt_ps], writes=[(gT[ec], sj)])
        p.close_scope('pe_ps')
        p.open_scope('ue2')
        ue_ps = [p.psum(f"u2_ps{m}", [128, TB], F32, scope='ue2') for m in range(8)]
        uws = [p.sbuf(f"u2ws{b}", [128, 1024], BF16, scope='ue2') for b in range(3)]
        for ec in range(32):
            wb = uws[ec % 3]
            p.dma('pool', (lambda wb_, ec_: lambda e: e.dma_start(
                out=wb_[:], in_=ue[ec_ * 128:(ec_ + 1) * 128, :]))(wb, ec),
                writes=[wb])
            for mc in range(8):
                p.op('pe', (lambda wb_, ec_, mc_: lambda e: e.matmul(
                    ue_ps[mc_][:, 0:TB], wb_[:, mc_ * 128:(mc_ + 1) * 128],
                    gT[ec_][:, 0:TB], start=(ec_ == 0), stop=(ec_ == 31)))(wb, ec, mc),
                    reads=[wb] + [(gT[ec], sj) for sj in range(ntb)],
                    writes=[ue_ps[mc]])
        for mc in range(8):
            p.op('dve', (lambda mc_: lambda e: e.tensor_tensor(
                out_fm[mc_][:, tb0:tb0 + TB], out_fm[mc_][:, tb0:tb0 + TB],
                ue_ps[mc_][:, 0:TB], OP.add))(mc),
                reads=[(out_fm[mc], tb0), ue_ps[mc]], writes=[(out_fm[mc], tb0)])
        p.close_scope('ue2')
        p.close_scope('pe_blk')

    store_residual_out(p, ctx, out_fm, x, y, T=T, D=D, f_dt=BF16)
    return p


def ref_l0_ffn(x, Wg, bg, Wu, bu, Wd, bd, Wq, keys, down_embed, up_embed):
    xs = x.astype(np.float64)
    mu = xs.mean(-1, keepdims=True)
    z = (xs - mu) / np.sqrt(xs.var(-1, keepdims=True) + 1e-5)
    T = x.shape[0]
    H, KPH, NK, RET = 4, 8, 64, 128
    q = (z @ Wq.astype(np.float64)).reshape(T, 2, H, RET // 2)
    sim = np.einsum('tphn,hkpn->pthk', q, keys.astype(np.float64))
    sx = -np.sort(-sim[0], -1)[..., :8]
    ixx = np.argsort(-sim[0], -1, kind='stable')[..., :8]
    sy = -np.sort(-sim[1], -1)[..., :8]
    ixy = np.argsort(-sim[1], -1, kind='stable')[..., :8]
    all_sc = (sx[..., :, None] + sy[..., None, :]).reshape(T, H, 64)
    all_ix = (ixx[..., :, None] * NK + ixy[..., None, :]).reshape(T, H, 64)
    order = np.argsort(-all_sc, -1, kind='stable')[..., :8]
    scores = np.take_along_axis(all_sc, order, -1)
    idx = np.take_along_axis(all_ix, order, -1)
    de = down_embed.astype(np.float64)[idx]
    ue_ = up_embed.astype(np.float64)[idx]
    w = np.einsum('td,thkd->thk', z, de)
    e = np.exp(scores - scores.max(-1, keepdims=True))
    sm = e / e.sum(-1, keepdims=True)
    w = w / (1 + np.exp(-w)) * sm
    experts = np.einsum('thk,thkd->td', w, ue_)
    gate = z @ Wg.astype(np.float64) + bg
    mlp = (gate / (1 + np.exp(-gate)) * (z @ Wu.astype(np.float64) + bu)) @ Wd.astype(np.float64) + bd
    return x + (mlp + experts).astype(np.float32)



"""L1_attn_a: differential attention.
Host inputs: Wq_ext, Wk_ext (1024, 2048): ext-halfhead j at cols [j*64, j*64+32),
upper 32 cols zero;  Wv (1024, 1024) [v part of W_qkv];  Wp (1024, 1024) [W_proj];
bp col (128, 8); lamq1/lamk1/lamq2/lamk2 (1, 32); swc (128, 1) subln*(1-LI) col.
"""



LAMBDA_INIT = 0.8 - 0.6 * float(np.exp(-0.3 * 12))


def build_l1_attn_a(T=2048):
    p = Prog("l1a")
    ctx = Ctx(p)
    D = 1024
    x = p.dram_in('x', (T, D), F32)
    y = p.dram_out('y', (T, D), F32)
    wqn = p.dram_in('wqn', (D, D), F32)
    wkn = p.dram_in('wkn', (D, D), F32)
    wv = p.dram_in('wv', (D, D), F32)
    wp = p.dram_in('wp', (D, D), F32)
    bp = p.dram_in('bp', (128, 8), F32)
    lq1 = p.dram_in('lq1', (1, 32), F32)
    lk1 = p.dram_in('lk1', (1, 32), F32)
    lq2 = p.dram_in('lq2', (1, 32), F32)
    lk2 = p.dram_in('lk2', (1, 32), F32)
    swc = p.dram_in('swc', (128, 1), F32)

    stg = stage_input_ln(p, ctx, x, T=T)
    ln = stg['ln_bf']
    bpc = load_col_vec(p, 'bpc', bp, D)
    swt = p.sbuf("swt", [128, 1], F32)
    p.dma('sp', lambda e: e.dma_start(out=swt[:], in_=swc[:, :]), writes=[swt])

    lrow = p.sbuf("lrow", [1, 128], F32)
    for nm, ap_ in [('lq1', lq1), ('lk1', lk1), ('lq2', lq2), ('lk2', lk2)]:
        t = p.sbuf(nm + "_t", [1, 32], F32)
        p.dma('sp', (lambda t_, a_: lambda e: e.dma_start(out=t_[:], in_=a_[:, :]))(t, ap_),
              writes=[t])
        locals()[nm + '_t'] = t
        if nm == 'lq1':
            lq1t = t
        elif nm == 'lk1':
            lk1t = t
        elif nm == 'lq2':
            lq2t = t
        else:
            lk2t = t
    lw = p.sbuf("lw", [1, 40], F32)
    p.op('dve', lambda e: e.tensor_tensor(lw[0:1, 0:32], lq1t[:], lk1t[:], OP.mult),
         reads=[lq1t, lk1t], writes=[(lw, 'a')])
    p.op('dve', lambda e: e.tensor_reduce(lw[0:1, 32:33], lw[0:1, 0:32], AX.X, OP.add),
         reads=[(lw, 'a')], writes=[(lw, 's1')])
    p.op('dve', lambda e: e.tensor_tensor(lw[0:1, 0:32], lq2t[:], lk2t[:], OP.mult),
         reads=[lq2t, lk2t], writes=[(lw, 'a')])
    p.op('dve', lambda e: e.tensor_reduce(lw[0:1, 33:34], lw[0:1, 0:32], AX.X, OP.add),
         reads=[(lw, 'a')], writes=[(lw, 's2')])
    p.op('act', lambda e: e.activation(lw[0:1, 34:35], lw[0:1, 32:33], AF.Exp),
         reads=[(lw, 's1')], writes=[(lw, 'e1')])
    p.op('act', lambda e: e.activation(lw[0:1, 35:36], lw[0:1, 33:34], AF.Exp),
         reads=[(lw, 's2')], writes=[(lw, 'e2')])
    p.op('dve', lambda e: e.tensor_tensor(lw[0:1, 36:37], lw[0:1, 35:36], lw[0:1, 34:35],
                                          OP.subtract),
         reads=[(lw, 'e1'), (lw, 'e2')], writes=[(lw, 'd')])
    p.op('dve', lambda e: e.tensor_scalar(lw[0:1, 37:38], lw[0:1, 36:37], -LAMBDA_INIT,
                                          None, OP.add, OP.bypass),
         reads=[(lw, 'd')], writes=[(lw, 'nl')])
    ones = ctx.get_ones_col()
    nl_col = p.sbuf("nl_col", [128, 1], F32)
    p.open_scope('nl')
    nl_ps = p.psum("nl_ps", [128, 128], F32, scope='nl')
    p.op('pe', lambda e: e.matmul(nl_ps[:, 0:1], ones[0:1, :].broadcast_to((1, 128)),
                                  lw[0:1, 37:38], start=True, stop=True),
         reads=[(lw, 'nl'), ones], writes=[nl_ps])
    p.op('dve', lambda e: e.tensor_copy(nl_col[:], nl_ps[:, 0:1]),
         reads=[nl_ps], writes=[nl_col])
    p.close_scope('nl')

    vf = [p.sbuf(f"vf{c}", [128, T], BF16) for c in range(8)]
    proj_fm(p, 'v', wv, ln, T, D, D, out_chunks=vf)
    ao1 = [p.sbuf(f"ao1{c}", [128, T], BF16) for c in range(8)]
    NSq = min(512, T)
    for gi in range(2):
        p.open_scope('g')
        if gi == 1:
            ao2 = [p.sbuf(f"ao2{c}", [128, T], BF16, scope='g') for c in range(8)]
        ao = ao1 if gi == 0 else ao2
        wq_g = [p.sbuf(f"wq_g{kc}", [128, 512], BF16, scope='g') for kc in range(8)]
        wk_g = [p.sbuf(f"wk_g{kc}", [128, 512], BF16, scope='g') for kc in range(8)]
        for kc in range(8):
            p.dma('pool', (lambda kc_: lambda e: e.dma_start(
                out=wq_g[kc_][:], in_=wqn[kc_ * 128:(kc_ + 1) * 128,
                                          gi * 512:(gi + 1) * 512]))(kc),
                writes=[wq_g[kc]])
            p.dma('pool', (lambda kc_: lambda e: e.dma_start(
                out=wk_g[kc_][:], in_=wkn[kc_ * 128:(kc_ + 1) * 128,
                                          gi * 512:(gi + 1) * 512]))(kc),
                writes=[wk_g[kc]])
        qb = p.sbuf("qb", [32, T], BF16, scope='g')
        kb = p.sbuf("kb", [32, T], BF16, scope='g')
        sps_holder = {}

        def prepare(h):
            sps = sps_holder['sps']
            for wsrc, dst in ((wq_g, qb), (wk_g, kb)):
                for kc in range(8):
                    for n0 in range(0, T, NSq):
                        nn = min(NSq, T - n0)
                        p.op('pe', (lambda w_, kc_, n0_, nn_, h_: lambda e: e.matmul(
                            sps[0:32, n0_:n0_ + nn_],
                            w_[kc_][:, h_ * 32:(h_ + 1) * 32],
                            ln[kc_][:, n0_:n0_ + nn_],
                            start=(kc_ == 0), stop=(kc_ == 7)))(wsrc, kc, n0, nn, h),
                            reads=[wsrc[kc], ln[kc]], writes=[(sps, n0)])
                p.op('dve', (lambda dst_: lambda e: e.tensor_copy(
                    dst_[0:32, :], sps[0:32, 0:T]))(dst),
                    reads=[(sps, n0) for n0 in range(0, T, NSq)], writes=[dst])

        def put_out(h, ti, ps_ap, reads):
            ch, r0 = h // 2, (h % 2) * 64
            p.op('dve', lambda e: e.tensor_copy(
                ao[ch][r0:r0 + 64, ti * 128:(ti + 1) * 128], ps_ap),
                reads=reads, writes=[(ao[ch], (h, ti))])

        def get_v(h):
            return vf[h // 2], (h % 2) * 64, [vf[h // 2]]

        attention_heads(p, ctx, f'at{gi}', 16,
                        lambda h: (qb, 0, [qb]), lambda h: (kb, 0, [kb]),
                        get_v, put_out, T=T, hd_qk=32, hd_v=64,
                        scale=float(32 ** -0.5), prepare=prepare,
                        sps_holder=sps_holder)
        if gi == 1:
            for c in range(8):
                p.op('dve', (lambda c_: lambda e: e.scalar_tensor_tensor(
                    ao1[c_][:], ao2[c_][:], nl_col[:], ao1[c_][:], OP.mult, OP.add))(c),
                    reads=[ao2[c], nl_col, ao1[c]], writes=[ao1[c]])
        p.close_scope('g')

    blk1 = np.zeros((128, 2), np.float32)
    blk1[0:64, 0] = 1
    blk1[64:128, 1] = 1
    bsel = np.zeros((2, 128), np.float32)
    bsel[0, 0:64] = 1
    bsel[1, 64:128] = 1
    b1t = p.sbuf("b1t", [128, 2], F32)
    bst = p.sbuf("bst", [2, 128], F32)
    p.dma('sp', lambda e: e.dma_start(out=b1t[:], in_=p.const('b1c', blk1, F32)[:, :]),
          writes=[b1t])
    p.dma('sp', lambda e: e.dma_start(out=bst[:], in_=p.const('bsc', bsel, F32)[:, :]),
          writes=[bst])
    out_in = ao1
    p.open_scope('rms')
    of = p.sbuf("of", [128, T], F32, scope='rms')
    sqf = p.sbuf("sqf", [128, T], F32, scope='rms')
    ssum = p.sbuf("ssum", [2, T], F32, scope='rms')
    s_ps = p.psum("s_ps", [128, 2048], F32, scope='rms')
    r_ps = p.psum("r_ps", [128, 2048], F32, scope='rms')
    NS = min(512, T)
    epsb = ctx.cval(1e-5)
    for c in range(8):
        p.op('dve', lambda e: e.tensor_copy(of[:], ao1[c][:]),
             reads=[ao1[c]], writes=[of])
        p.op('act', lambda e: e.activation(sqf[:], of[:], AF.Square),
             reads=[of], writes=[sqf])
        for n0 in range(0, T, NS):
            p.op('pe', lambda e: e.matmul(s_ps[0:2, n0:n0 + NS], b1t[:],
                                          sqf[:, n0:n0 + NS], start=True, stop=True),
                 reads=[sqf, b1t], writes=[(s_ps, n0)])
        p.op('dve', lambda e: e.tensor_scalar(ssum[0:2, :], s_ps[0:2, 0:T], 1.0 / 64.0,
                                              None, OP.mult, OP.bypass),
             reads=[(s_ps, n0) for n0 in range(0, T, NS)], writes=[ssum])
        p.op('act', lambda e: e.activation(ssum[0:2, :], ssum[0:2, :], AF.Ln,
                                           bias=epsb[0:2, :]),
             reads=[ssum, epsb], writes=[ssum])
        p.op('act', lambda e: e.activation(ssum[0:2, :], ssum[0:2, :], AF.Exp, scale=-0.5),
             reads=[ssum], writes=[ssum])
        for n0 in range(0, T, NS):
            p.op('pe', lambda e: e.matmul(r_ps[:, n0:n0 + NS], bst[0:2, :],
                                          ssum[0:2, n0:n0 + NS], start=True, stop=True),
                 reads=[ssum, bst], writes=[(r_ps, n0)])
        p.op('dve', lambda e: e.tensor_tensor(of[:], of[:], r_ps[:, 0:T], OP.mult),
             reads=[of] + [(r_ps, n0) for n0 in range(0, T, NS)], writes=[of])
        p.op('dve', lambda e: e.tensor_scalar(out_in[c][:], of[:], swt[:],
                                              1.0 - LAMBDA_INIT, OP.mult, OP.mult),
             reads=[of, swt], writes=[out_in[c]])
    p.close_scope('rms')
    out_fm = proj_fm(p, 'o', wp, out_in, T, D, D, bias_col=bpc, out_dt=BF16)
    store_residual_out(p, ctx, out_fm, x, y, T=T, D=D, f_dt=BF16)
    return p


def ref_l1_attn_a(x, W_qkv, W_proj, b_proj, lq1, lk1, lq2, lk2, subln_w):
    xs = x.astype(np.float64)
    mu = xs.mean(-1, keepdims=True)
    z = (xs - mu) / np.sqrt(xs.var(-1, keepdims=True) + 1e-5)
    T, C = x.shape
    H, hd = 16, 32
    qkv = (z @ W_qkv.astype(np.float64)).reshape(T, 3, H, 64)
    q, k, v = qkv[:, 0], qkv[:, 1], qkv[:, 2]
    q = q.reshape(T, 2 * H, hd).transpose(1, 0, 2)
    k = k.reshape(T, 2 * H, hd).transpose(1, 0, 2)
    v = v.transpose(1, 0, 2)
    scale = hd ** -0.5

    def sdpa(qh, kh):
        s = np.einsum('hqd,hkd->hqk', qh, kh) * scale
        e = np.exp(s - s.max(-1, keepdims=True))
        a = e / e.sum(-1, keepdims=True)
        return np.einsum('hqk,hkd->hqd', a, v)
    a1 = sdpa(q[:H], k[:H])
    a2 = sdpa(q[H:], k[H:])
    lam = np.exp((lq1 * lk1).sum()) - np.exp((lq2 * lk2).sum()) + LAMBDA_INIT
    o = a1 - lam * a2
    o = o / np.sqrt((o * o).mean(-1, keepdims=True) + 1e-5)
    o = o * subln_w * (1 - LAMBDA_INIT)
    o = o.transpose(1, 0, 2).reshape(T, C)
    return x + (o @ W_proj.astype(np.float64) + b_proj).astype(np.float32)


def make_ext(Wpart):
    W = np.zeros((1024, 2048), np.float32)
    for j in range(32):
        W[:, j * 64:j * 64 + 32] = Wpart[:, j * 32:(j + 1) * 32]
    return W



"""L1_attn_b: windowed attention, 8 full heads (hd 64, win 1024) + 8 reduced
heads (qk 32 zero-padded to 64, v 64, win 256), concat, proj.
Host: wqkvf (1024, 1536) [w_qkv_full.T]; wqkre (1024, 1024) [qk_red ext-padded];
wvr (1024, 512) [w_v_red.T]; wp (1024, 1024) [w_proj.T].
"""




def build_l1_attn_b(T=2048):
    p = Prog("l1b")
    ctx = Ctx(p)
    D = 1024
    x = p.dram_in('x', (T, D), F32)
    y = p.dram_out('y', (T, D), F32)
    wqkvf = p.dram_in('wqkvf', (D, 1536), F32)
    wqkre = p.dram_in('wqkre', (D, 1024), F32)
    wvr = p.dram_in('wvr', (D, 512), F32)
    wp = p.dram_in('wp', (D, D), F32)

    stg = stage_input_ln(p, ctx, x, T=T)
    ln = stg['ln_bf']

    mt = np.where(np.arange(128)[None, :] > np.arange(128)[:, None], -1e30, 0.0)
    mtile = p.sbuf("mtile", [128, 128], F32)
    p.dma('sp', lambda e: e.dma_start(out=mtile[:], in_=p.const('mtc', mt, F32)[:, :]),
          writes=[mtile])
    mt2 = np.where(np.arange(128)[:, None] >= np.arange(128)[None, :], -1e30, 0.0)
    mtile2 = p.sbuf("mtile2", [128, 128], F32)
    p.dma('sp', lambda e: e.dma_start(out=mtile2[:], in_=p.const('mtc2', mt2, F32)[:, :]),
          writes=[mtile2])

    ao = [p.sbuf(f"ao{c}", [128, T], BF16) for c in range(8)]

    def mk_mask(wb):
        def mask_bias(ti, kc):
            if kc == ti:
                return (mtile[:], mtile)
            if kc == ti - wb:
                return (mtile2[:], mtile2)
            return None
        return mask_bias

    p.open_scope('F')
    qkvf = [p.sbuf(f"qkvf{c}", [128, T], BF16, scope='F') for c in range(12)]
    proj_fm(p, 'f', wqkvf, ln, T, D, 1536, out_chunks=qkvf)
    attention_full(p, ctx, 'atf', qkvf[0:4], qkvf[4:8], qkvf[8:12], ao[0:4],
                   T=T, heads=8, hd=64, scale=0.125,
                   kblock_range=lambda ti: (max(0, ti - 8), ti + 1),
                   mask_bias=mk_mask(8))
    p.close_scope('F')

    p.open_scope('R')
    qkre = [p.sbuf(f"qkre{c}", [128, T], BF16, scope='R') for c in range(8)]
    vr = [p.sbuf(f"vr{c}", [128, T], BF16, scope='R') for c in range(4)]
    proj_fm(p, 'r', wqkre, ln, T, D, 1024, out_chunks=qkre)
    proj_fm(p, 'vr', wvr, ln, T, D, 512, out_chunks=vr)
    
    def get_q(h):
        return qkre[h // 2], (h % 2) * 64, [qkre[h // 2]]

    def get_k(h):
        return qkre[4 + h // 2], (h % 2) * 64, [qkre[4 + h // 2]]

    def get_v(h):
        return vr[h // 2], (h % 2) * 64, [vr[h // 2]]

    def put_out(h, ti, ps_ap, reads):
        ch, r0 = 4 + h // 2, (h % 2) * 64
        p.op('dve', lambda e: e.tensor_copy(
            ao[ch][r0:r0 + 64, ti * 128:(ti + 1) * 128], ps_ap),
            reads=reads, writes=[(ao[ch], (h, ti))])

    attention_heads(p, ctx, 'atr', 8, get_q, get_k, get_v, put_out,
                    T=T, hd_qk=64, hd_v=64, scale=float(32 ** -0.5),
                    kblock_range=lambda ti: (max(0, ti - 2), ti + 1),
                    mask_bias=mk_mask(2))
    p.close_scope('R')

    out_fm = proj_fm(p, 'o', wp, ao, T, D, D, out_dt=BF16)
    store_residual_out(p, ctx, out_fm, x, y, T=T, D=D, f_dt=BF16)
    return p


def ref_l1_attn_b(x, w_qkv_full, w_qk_red, w_v_red, w_proj):
    xs = x.astype(np.float64)
    mu = xs.mean(-1, keepdims=True)
    z = (xs - mu) / np.sqrt(xs.var(-1, keepdims=True) + 1e-5)
    T, C = x.shape

    def windowed(q, k, v, win):
        d = q.shape[-1]
        s = np.einsum('hqd,hkd->hqk', q, k) / np.sqrt(d)
        i = np.arange(q.shape[1])[:, None]
        j = np.arange(k.shape[1])[None, :]
        s = np.where((i - j < 0) | (i - j >= win), -np.inf, s)
        e = np.exp(s - s.max(-1, keepdims=True))
        a = e / e.sum(-1, keepdims=True)
        return np.einsum('hqk,hkd->hqd', a, v)

    def split(t, d):
        return t.reshape(T, -1, d).transpose(1, 0, 2)
    q, k, v = np.split(z @ w_qkv_full.astype(np.float64), 3, axis=-1)
    yf = windowed(split(q, 64), split(k, 64), split(v, 64), 1024)
    qr, kr = np.split(z @ w_qk_red.astype(np.float64), 2, axis=-1)
    vr_ = z @ w_v_red.astype(np.float64)
    yr = windowed(split(qr, 32), split(kr, 32), split(vr_, 64), 256)
    yy = np.concatenate([yf, yr], axis=0).transpose(1, 0, 2).reshape(T, C)
    return x + (yy @ w_proj.astype(np.float64)).astype(np.float32)



"""L1_ffn: int8-dequant SwiGLU.  y = x + FFN(LN(x)).
FFN(z) = (silu(z @ wgT*s) * (z @ wuT*s)) @ wdT*s,  s = 1/127.
Weight inputs here are HOST-TRANSPOSED: wg (1024, 3072) int32, wu same,
wd (3072, 1024) int32.
"""



S127 = 1.0 / 127.0


def build_l1_ffn(T=2048):
    p = Prog("l1ffn")
    ctx = Ctx(p)
    D, H = 1024, 3072
    x = p.dram_in('x', (T, D), F32)
    y = p.dram_out('y', (T, D), F32)
    wg = p.dram_in('wg', (D, H), I32)
    wu = p.dram_in('wu', (D, H), I32)
    wd = p.dram_in('wd', (H, D), I32)

    stg = stage_input_ln(p, ctx, x, T=T)
    ln = stg['ln_bf']

    nh = H // 128
    h_chunks = [p.sbuf(f"h{c}", [128, T], BF16) for c in range(nh)]

    p.open_scope('up')
    wsg = WStream(p, "wg", wg, nbuf=3)
    wsu = WStream(p, "wu", wu, nbuf=3)
    ps_g = [p.psum(f"psg{i}", [128, min(T, 2048)], F32, scope='up') for i in range(1)]
    ps_u = [p.psum(f"psu{i}", [128, min(T, 2048)], F32, scope='up') for i in range(1)]
    silu_t = p.sbuf("silu_t", [128, T], F32, scope='up')
    u_t = p.sbuf("u_t", [128, T], F32, scope='up')
    nk = D // 128
    NS = 512
    for hc in range(nh):
        pg = ps_g[0]
        pu = ps_u[0]
        for kc in range(nk):
            wbg = wsg.fetch(kc * 128, hc * 128)
            wbu = wsu.fetch(kc * 128, hc * 128)
            for n0 in range(0, T, NS):
                nn = min(NS, T - n0)
                p.op('pe', lambda e: e.matmul(pg[:, n0:n0 + nn], wbg[:],
                                              ln[kc][:, n0:n0 + nn],
                                              start=(kc == 0), stop=(kc == nk - 1)),
                     reads=[wbg, ln[kc]], writes=[(pg, n0)])
                p.op('pe', lambda e: e.matmul(pu[:, n0:n0 + nn], wbu[:],
                                              ln[kc][:, n0:n0 + nn],
                                              start=(kc == 0), stop=(kc == nk - 1)),
                     reads=[wbu, ln[kc]], writes=[(pu, n0)])
        ptags = [(pg, n0) for n0 in range(0, T, NS)]
        utags = [(pu, n0) for n0 in range(0, T, NS)]
        p.op('act', lambda e: e.activation(silu_t[:], pg[:], AF.Silu, scale=S127),
             reads=ptags, writes=[silu_t])
        p.op('dve', lambda e: e.tensor_scalar(u_t[:], pu[:], S127, None, OP.mult, OP.bypass),
             reads=utags, writes=[u_t])
        p.op('dve', lambda e: e.tensor_tensor(h_chunks[hc][:], silu_t[:], u_t[:], OP.mult),
             reads=[silu_t, u_t], writes=[h_chunks[hc]])
    p.close_scope('up')

    out_fm = [p.sbuf(f"of{c}", [128, T], BF16) for c in range(D // 128)]
    p.open_scope('down')
    wsd = WStream(p, "wd", wd, nbuf=3)
    ps_d = [p.psum(f"psd{i}", [128, min(T, 2048)], F32, scope='down') for i in range(2)]
    for mc in range(D // 128):
        ps = ps_d[mc % 2]
        for kc in range(nh):
            wb = wsd.fetch(kc * 128, mc * 128)
            for n0 in range(0, T, NS):
                nn = min(NS, T - n0)
                p.op('pe', lambda e: e.matmul(ps[:, n0:n0 + nn], wb[:],
                                              h_chunks[kc][:, n0:n0 + nn],
                                              start=(kc == 0), stop=(kc == nh - 1)),
                     reads=[wb, h_chunks[kc]], writes=[(ps, n0)])
        ptags = [(ps, n0) for n0 in range(0, T, NS)]
        p.op('dve', lambda e: e.tensor_scalar(out_fm[mc][:], ps[:], S127, None,
                                              OP.mult, OP.bypass),
             reads=ptags, writes=[out_fm[mc]])
    p.close_scope('down')

    store_residual_out(p, ctx, out_fm, x, y, T=T, D=D, f_dt=BF16)
    return p


def ref_l1_ffn(x, wg_nat, wu_nat, wd_nat):
    xs = x.astype(np.float64)
    mu = xs.mean(-1, keepdims=True)
    var = xs.var(-1, keepdims=True)
    z = (xs - mu) / np.sqrt(var + 1e-5)
    s = 1.0 / 127.0
    wgf = wg_nat.astype(np.float64) * s
    wuf = wu_nat.astype(np.float64) * s
    wdf = wd_nat.astype(np.float64) * s
    gate = z @ wgf.T
    up = z @ wuf.T
    sig = 1.0 / (1.0 + np.exp(-gate))
    h = gate * sig * up
    return x + (h @ wdf.T).astype(np.float32)



"""L2_attn_a: gated delta rule (chunked WY, C=128) + static path + final LN.
Host: W (1024, 1024) [W.T so x@W.T works as fm matmul]; lgam/lbet cols (128, 8);
eta (1,1); lam_logit (1,1).
"""



C = 128


def build_l2_attn_a(T=2048):
    p = Prog("l2a")
    ctx = Ctx(p)
    D = 1024
    nq = T // C
    x = p.dram_in('x', (T, D), F32)
    y = p.dram_out('y', (T, D), F32)
    W = p.dram_in('W', (D, D), F32)
    lgam = p.dram_in('lgam', (128, 8), F32)
    lbet = p.dram_in('lbet', (128, 8), F32)
    eta_in = p.dram_in('eta', (1, 1), F32)
    llog = p.dram_in('llog', (1, 1), F32)

    stg = stage_input_ln(p, ctx, x, T=T)
    ln = stg['ln_bf']
    identf = ctx.get_ident(F32)
    identb = ctx.get_ident(BF16)
    identblk = ctx.get_ident_block(64, BF16)
    ones = ctx.get_ones_col()
    gcol = load_col_vec(p, 'gc', lgam, D)
    bcol = load_col_vec(p, 'bc', lbet, D)

    sc = p.sbuf("sc", [1, 16], F32)
    p.dma('sp', lambda e: e.dma_start(out=sc[0:1, 0:1], in_=llog[:, :]), writes=[(sc, 'l')])
    p.dma('sp', lambda e: e.dma_start(out=sc[0:1, 1:2], in_=eta_in[:, :]), writes=[(sc, 'e')])
    p.op('act', lambda e: e.activation(sc[0:1, 2:3], sc[0:1, 0:1], AF.Sigmoid),
         reads=[(sc, 'l')], writes=[(sc, 'lam')])
    p.op('act', lambda e: e.activation(sc[0:1, 3:4], sc[0:1, 2:3], AF.Ln),
         reads=[(sc, 'lam')], writes=[(sc, 'lnl')])
    p.op('dve', lambda e: e.tensor_scalar(sc[0:1, 4:5], sc[0:1, 2:3], -1.0, 1.0,
                                          OP.mult, OP.add),
         reads=[(sc, 'lam')], writes=[(sc, '1ml')])
    p.op('dve', lambda e: e.tensor_tensor(sc[0:1, 5:6], sc[0:1, 4:5], sc[0:1, 1:2], OP.mult),
         reads=[(sc, '1ml'), (sc, 'e')], writes=[(sc, 'be0')])
    p.op('dve', lambda e: e.tensor_scalar(sc[0:1, 6:7], sc[0:1, 5:6], 1.0 / 64.0, None,
                                          OP.mult, OP.bypass),
         reads=[(sc, 'be0')], writes=[(sc, 'beta')])
    bccol = p.sbuf("bccol", [128, 4], F32)
    p.open_scope('bc0')
    b_ps = p.psum("b_ps", [128, 128], F32, scope='bc0')
    p.op("pe", lambda e: e.matmul(b_ps[:, 0:1], ones[0:1, :].broadcast_to((1, 128)),
                                  sc[0:1, 3:5] if False else sc[0:1, 3:4], start=True, stop=True),
         reads=[(sc, 'lnl'), ones], writes=[(b_ps, 0)])
    p.op('pe', lambda e: e.matmul(b_ps[:, 2:3], ones[0:1, :].broadcast_to((1, 128)),
                                  sc[0:1, 6:7], start=True, stop=True),
         reads=[(sc, 'beta'), ones], writes=[(b_ps, 1)])
    p.op('dve', lambda e: e.tensor_copy(bccol[:, 0:1], b_ps[:, 0:1]),
         reads=[(b_ps, 0)], writes=[(bccol, 'lnl')])
    p.op('dve', lambda e: e.tensor_copy(bccol[:, 1:2], b_ps[:, 2:3]),
         reads=[(b_ps, 1)], writes=[(bccol, 'beta')])
    p.close_scope('bc0')
    lnl_col = bccol[:, 0:1]
    beta_col = bccol[:, 1:2]

    sr = np.arange(C)
    diff = sr[:, None] - sr[None, :]
    stril = (diff > 0).astype(np.float32)
    stril_i = (diff >= 0).astype(np.float32)
    dgrid = p.const('dgrid', diff.astype(np.float32), F32)
    mask_l = p.const('mask_l', stril, F32)
    mask_li = p.const('mask_li', stril_i, F32)
    scol_c = p.const('scol', np.arange(1, C + 1, dtype=np.float32).reshape(128, 1), F32)
    srev_c = p.const('srev', np.arange(C - 1, -1, -1, dtype=np.float32).reshape(128, 1), F32)
    dg = p.sbuf("dg", [128, 128], F32)
    ml = p.sbuf("ml", [128, 128], F32)
    mli = p.sbuf("mli", [128, 128], F32)
    scolv = p.sbuf("scolv", [128, 2], F32)
    p.dma('sp', lambda e: e.dma_start(out=dg[:], in_=dgrid[:, :]), writes=[dg])
    p.dma('sp', lambda e: e.dma_start(out=ml[:], in_=mask_l[:, :]), writes=[ml])
    p.dma('sp', lambda e: e.dma_start(out=mli[:], in_=mask_li[:, :]), writes=[mli])
    p.dma('sp', lambda e: e.dma_start(out=scolv[:, 0:1], in_=scol_c[:, :]), writes=[(scolv, 0)])
    p.dma('sp', lambda e: e.dma_start(out=scolv[:, 1:2], in_=srev_c[:, :]), writes=[(scolv, 1)])
    Td = p.sbuf("Td", [128, 128], F32)
    TdT = p.sbuf("TdT", [128, 128], F32)
    Tdi = p.sbuf("Tdi", [128, 128], F32)
    TdiT = p.sbuf("TdiT", [128, 128], F32)
    tmpd = p.sbuf("tmpd", [128, 128], F32)
    p.op('dve', lambda e: e.tensor_scalar(tmpd[:], dg[:], lnl_col, None, OP.mult, OP.bypass),
         reads=[dg, (bccol, 'lnl')], writes=[tmpd])
    p.op('act', lambda e: e.activation(tmpd[:], tmpd[:], AF.Exp),
         reads=[tmpd], writes=[tmpd])
    p.op('dve', lambda e: e.tensor_tensor(Td[:], tmpd[:], ml[:], OP.mult),
         reads=[tmpd, ml], writes=[Td])
    p.op('dve', lambda e: e.tensor_tensor(Tdi[:], tmpd[:], mli[:], OP.mult),
         reads=[tmpd, mli], writes=[Tdi])
    p.op('dve', lambda e: e.tensor_scalar(tmpd[:], dg[:], lnl_col, None, OP.mult, OP.bypass),
         reads=[dg, (bccol, 'lnl')], writes=[tmpd])
    p.op('act', lambda e: e.activation(tmpd[:], tmpd[:], AF.Exp, scale=-1.0),
         reads=[tmpd], writes=[tmpd])
    mlT = p.sbuf("mlT", [128, 128], F32)
    mliT = p.sbuf("mliT", [128, 128], F32)
    p.dma('sp', lambda e: e.dma_start(out=mlT[:], in_=p.const(
        'mask_u', stril.T.copy(), F32)[:, :]), writes=[mlT])
    p.dma('sp', lambda e: e.dma_start(out=mliT[:], in_=p.const(
        'mask_ui', stril_i.T.copy(), F32)[:, :]), writes=[mliT])
    p.op('dve', lambda e: e.tensor_tensor(TdT[:], tmpd[:], mlT[:], OP.mult),
         reads=[tmpd, mlT], writes=[TdT])
    p.op('dve', lambda e: e.tensor_tensor(TdiT[:], tmpd[:], mliT[:], OP.mult),
         reads=[tmpd, mliT], writes=[TdiT])
    lcols = p.sbuf("lcols", [128, 4], F32)
    p.op('dve', lambda e: e.tensor_tensor(lcols[:, 0:1], scolv[:, 0:1], lnl_col, OP.mult),
         reads=[(scolv, 0), (bccol, 'lnl')], writes=[(lcols, 0)])
    p.op('act', lambda e: e.activation(lcols[:, 0:1], lcols[:, 0:1], AF.Exp),
         reads=[(lcols, 0)], writes=[(lcols, 0)])
    p.op('dve', lambda e: e.tensor_tensor(lcols[:, 1:2], scolv[:, 1:2], lnl_col, OP.mult),
         reads=[(scolv, 1), (bccol, 'lnl')], writes=[(lcols, 1)])
    p.op('act', lambda e: e.activation(lcols[:, 1:2], lcols[:, 1:2], AF.Exp),
         reads=[(lcols, 1)], writes=[(lcols, 1)])
    p.op('dve', lambda e: e.tensor_scalar(lcols[:, 2:3], lnl_col, float(C), None,
                                          OP.mult, OP.bypass),
         reads=[(bccol, 'lnl')], writes=[(lcols, 2)])
    p.op('act', lambda e: e.activation(lcols[:, 2:3], lcols[:, 2:3], AF.Exp),
         reads=[(lcols, 2)], writes=[(lcols, 2)])
    lam_s = lcols[:, 0:1]
    lam_rev = lcols[:, 1:2]
    lam_C = lcols[:, 2:3]

    xact = [p.sbuf(f"xact{c}", [128, T], BF16) for c in range(8)]
    for c in range(8):
        p.op('act', (lambda c_: lambda e: e.activation(xact[c_][:], ln[c_][:], AF.Tanh))(c),
             reads=[ln[c]], writes=[xact[c]])
    so = [p.sbuf(f"so{c}", [128, T], BF16) for c in range(8)]
    proj_fm(p, 'so', W, xact, T, D, D, out_chunks=so)

    p.open_scope('dr')
    G_ps = p.psum("G_ps", [128, 128], F32, scope='dr')
    n_ps = [p.psum(f"n_ps{i}", [128, 128], F32, scope='dr') for i in range(3)]
    u_ps = p.psum("u_ps", [128, 128], F32, scope='dr')
    tb_ps = p.psum("tb_ps", [128, 128], BF16, scope='dr')
    A = p.sbuf("A", [128, 128], BF16, scope='dr')
    At = p.sbuf("At", [128, 128], BF16, scope='dr')
    Xm = p.sbuf("Xm", [128, 128], BF16, scope='dr')
    Xt = p.sbuf("Xt", [128, 128], BF16, scope='dr')
    Mt = p.sbuf("Mt", [128, 128], BF16, scope='dr')
    Pn = p.sbuf("Pn", [128, 128], BF16, scope='dr')
    MoT = p.sbuf("MoT", [128, 128], BF16, scope='dr')
    Ktm = p.sbuf("Ktm", [128, 64], BF16, scope='dr')
    Vtm = p.sbuf("Vtm", [128, 64], BF16, scope='dr')
    Wm = p.sbuf("Wm", [128, 64], BF16, scope='dr')
    Uu = p.sbuf("Uu", [128, 64], BF16, scope='dr')
    Urev = p.sbuf("Urev", [128, 64], BF16, scope='dr')
    P0sb = p.sbuf("P0sb", [128, 64], F32, scope='dr')
    Osb = p.sbuf("Osb", [128, 64], BF16, scope='dr')
    S_sb = p.sbuf("S_sb", [64, 64], F32, scope='dr')
    S_bf = p.sbuf("S_bf", [128, 64], BF16, scope='dr')
    idf_sb = p.sbuf("idf_sb", [128, 128], F32, scope='dr')
    p.dma('sp', lambda e: e.dma_start(out=idf_sb[:], in_=p.const(
        'ident128', np.eye(128, dtype=np.float32), F32)[:, :]), writes=[idf_sb])

    for h in range(16):
        ch, hr = h // 2, (h % 2) * 64
        p.op('pool', lambda e: e.memset(S_sb[:], 0.0), writes=[S_sb])
        for cidx in range(nq):
            c0 = cidx * C
            kf_sl = xact[ch][hr:hr + 64, c0:c0 + C]
            vf_sl = so[ch][hr:hr + 64, c0:c0 + C]
            p.op('pe', lambda e: e.matmul(G_ps[:, 0:C], kf_sl, kf_sl, start=True, stop=True),
                 reads=[xact[ch]], writes=[G_ps])
            p.op('dve', lambda e: e.tensor_tensor(tmpd[:], G_ps[:, 0:C], Td[:], OP.mult),
                 reads=[G_ps, Td], writes=[tmpd])
            p.op('dve', lambda e: e.tensor_scalar(A[:], tmpd[:], beta_col, None,
                                                  OP.mult, OP.bypass),
                 reads=[tmpd, (bccol, 'beta')], writes=[A])
            p.op('dve', lambda e: e.tensor_tensor(tmpd[:], G_ps[:, 0:C], TdT[:], OP.mult),
                 reads=[G_ps, TdT], writes=[tmpd])
            p.op('dve', lambda e: e.tensor_scalar(At[:], tmpd[:], beta_col, None,
                                                  OP.mult, OP.bypass),
                 reads=[tmpd, (bccol, 'beta')], writes=[At])
            p.op('dve', lambda e: e.tensor_tensor(MoT[:], G_ps[:, 0:C], TdiT[:], OP.mult),
                 reads=[G_ps, TdiT], writes=[MoT])
            p.op('dve', lambda e: e.tensor_tensor(Mt[:], idf_sb[:], At[:], OP.add),
                 reads=[idf_sb, At], writes=[Mt])
            p.op('dve', lambda e: e.tensor_tensor(Xm[:], idf_sb[:], A[:], OP.subtract),
                 reads=[idf_sb, A], writes=[Xm])
            p.op('dve', lambda e: e.tensor_tensor(Xt[:], idf_sb[:], At[:], OP.subtract),
                 reads=[idf_sb, At], writes=[Xt])
            for it in range(6):
                p.op('pe', lambda e: e.matmul(n_ps[0][:, 0:C], Mt[:], Xm[:],
                                              start=True, stop=True),
                     reads=[Mt, Xm], writes=[n_ps[0]])
                p.op('dve', lambda e: e.tensor_copy(Pn[:], n_ps[0][:, 0:C]),
                     reads=[n_ps[0]], writes=[Pn])
                p.op('pe', lambda e: e.matmul(n_ps[1][:, 0:C], Xt[:], Pn[:],
                                              start=True, stop=True),
                     reads=[Xt, Pn], writes=[n_ps[1]])
                p.op('pe', lambda e: e.matmul(n_ps[2][:, 0:C], Pn[:], Xt[:],
                                              start=True, stop=True),
                     reads=[Pn, Xt], writes=[n_ps[2]])
                p.op('dve', lambda e: e.scalar_tensor_tensor(
                    Xm[:], Xm[:], 2.0, n_ps[1][:, 0:C], OP.mult, OP.subtract),
                    reads=[Xm, n_ps[1]], writes=[Xm])
                p.op('dve', lambda e: e.scalar_tensor_tensor(
                    Xt[:], Xt[:], 2.0, n_ps[2][:, 0:C], OP.mult, OP.subtract),
                    reads=[Xt, n_ps[2]], writes=[Xt])
            p.op('pe', lambda e: e.transpose(tb_ps[0:128, 0:64], kf_sl,
                                             identblk[hr:hr + 64, 0:64]),
                 reads=[xact[ch], identblk], writes=[tb_ps])
            p.op('dve', lambda e: e.tensor_copy(Ktm[:, 0:64], tb_ps[0:128, 0:64]),
                 reads=[tb_ps], writes=[Ktm])
            p.op('pe', lambda e: e.transpose(tb_ps[0:128, 0:64], vf_sl,
                                             identblk[hr:hr + 64, 0:64]),
                 reads=[so[ch], identblk], writes=[tb_ps])
            p.op('dve', lambda e: e.tensor_copy(Vtm[:, 0:64], tb_ps[0:128, 0:64]),
                 reads=[tb_ps], writes=[Vtm])
            p.op('act', (lambda hr_: lambda e: e.copy(S_bf[hr_:hr_ + 64, :], S_sb[:]))(hr),
                 reads=[S_sb], writes=[S_bf])
            p.op('pe', (lambda hr_: lambda e: e.matmul(
                u_ps[0:C, 0:64], kf_sl, S_bf[hr_:hr_ + 64, :],
                start=True, stop=True))(hr),
                 reads=[xact[ch], S_bf], writes=[u_ps])
            p.op('dve', lambda e: e.tensor_copy(P0sb[:, 0:64], u_ps[0:C, 0:64]),
                 reads=[u_ps], writes=[P0sb])
            p.op('dve', lambda e: e.scalar_tensor_tensor(
                Wm[:, 0:64], P0sb[:, 0:64], lam_s, Vtm[:, 0:64], OP.mult, OP.subtract),
                reads=[P0sb, (lcols, 0), Vtm], writes=[(Wm, 'pre')])
            p.op('dve', lambda e: e.tensor_scalar(Wm[:, 0:64], Wm[:, 0:64], beta_col,
                                                  -1.0, OP.mult, OP.mult),
                 reads=[(Wm, 'pre'), (bccol, 'beta')], writes=[Wm])
            p.op('pe', lambda e: e.matmul(u_ps[0:C, 0:64], Xt[:], Wm[:, 0:64],
                                          start=True, stop=True),
                 reads=[Xt, Wm], writes=[u_ps])
            p.op('dve', lambda e: e.tensor_copy(Uu[:, 0:64], u_ps[0:C, 0:64]),
                 reads=[u_ps], writes=[Uu])
            p.op('pe', lambda e: e.matmul(u_ps[0:C, 0:64], MoT[:], Uu[:, 0:64],
                                          start=True, stop=True),
                 reads=[MoT, Uu], writes=[u_ps])
            p.op('dve', lambda e: e.scalar_tensor_tensor(
                Osb[:, 0:64], P0sb[:, 0:64], lam_s, u_ps[0:C, 0:64], OP.mult, OP.add),
                reads=[P0sb, (lcols, 0), u_ps], writes=[Osb])
            p.op('pe', lambda e: e.transpose(tb_ps[0:64, 0:128], Osb[:, 0:64], identb[:]),
                 reads=[Osb, identb], writes=[tb_ps])
            p.op('dve', (lambda c0_: lambda e: e.tensor_tensor(
                so[ch][hr:hr + 64, c0_:c0_ + C], so[ch][hr:hr + 64, c0_:c0_ + C],
                tb_ps[0:64, 0:128], OP.add))(c0),
                reads=[so[ch], tb_ps], writes=[so[ch]])
            p.op('dve', lambda e: e.tensor_scalar(Urev[:, 0:64], Uu[:, 0:64], lam_rev,
                                                  None, OP.mult, OP.bypass),
                 reads=[Uu, (lcols, 1)], writes=[Urev])
            p.op('pe', lambda e: e.matmul(u_ps[0:64, 0:64], Ktm[:, 0:64], Urev[:, 0:64],
                                          start=True, stop=True),
                 reads=[Ktm, Urev], writes=[u_ps])
            p.op('dve', lambda e: e.scalar_tensor_tensor(
                S_sb[:], S_sb[:], lam_C[0:64, :], u_ps[0:64, 0:64], OP.mult, OP.add),
                reads=[S_sb, (lcols, 2), u_ps], writes=[S_sb])
    p.close_scope('dr')

    out_fm = [p.sbuf(f"of{c}", [128, T], BF16) for c in range(8)]
    ln_fm_sbuf(p, ctx, so, out_fm, T, D, gcol=gcol, bcol=bcol, eps=1e-5)
    store_residual_out(p, ctx, out_fm, x, y, T=T, D=D, f_dt=BF16)
    return p


def ref_l2_attn_a(x, W, ln_gamma, ln_beta, eta, lam_logit):
    xs = x.astype(np.float64)
    mu = xs.mean(-1, keepdims=True)
    z = (xs - mu) / np.sqrt(xs.var(-1, keepdims=True) + 1e-5)
    T, D = x.shape
    H, d = 16, 64
    lam = 1 / (1 + np.exp(-lam_logit))
    beta = eta * (1 - lam) / d
    xa = np.tanh(z)
    so = xa @ W.T.astype(np.float64)

    def heads(t):
        return t.reshape(T, H, d).transpose(1, 0, 2)
    q = heads(xa)
    v = heads(so)
    o = np.zeros_like(v)
    for h in range(H):
        S = np.zeros((d, d))
        for t in range(T):
            kt, vt, qt = q[h, t], v[h, t], q[h, t]
            S = lam * S
            u = beta * (vt - S.T @ kt)
            S = S + np.outer(kt, u)
            o[h, t] = S.T @ qt
    yv = so + o.transpose(1, 0, 2).reshape(T, D)
    mu2 = yv.mean(-1, keepdims=True)
    v2 = ((yv - mu2) ** 2).mean(-1, keepdims=True)
    return x + ((yv - mu2) / np.sqrt(v2 + 1e-5) * ln_gamma + ln_beta).astype(np.float32)



"""L2_attn_b: 4 stacked Mamba mixers + final LN, selective scan via
tensor_tensor_scan.  See host layout builder make_l2b_inputs below.
"""



DI = 2048
NST = 16
DTR = 64
NM = 4


def build_l2_attn_b(T=2048):
    p = Prog("l2b")
    ctx = Ctx(p)
    D = 1024
    ndc = DI // 128
    x = p.dram_in('x', (T, D), F32)
    y = p.dram_out('y', (T, D), F32)
    nwc_d = p.dram_in('nwc', (NM * 128, 8), F32)
    nbc_d = p.dram_in('nbc', (NM * 128, 8), F32)
    inpw = p.dram_in('inpw', (NM * D, 2 * DI), F32)
    convc_d = p.dram_in('convc', (NM * 128, 4 * ndc), F32)
    convbc_d = p.dram_in('convbc', (NM * 128, ndc), F32)
    xpw = p.dram_in('xpw', (NM * DI, 96), F32)
    dtw = p.dram_in('dtw', (NM * DTR, DI), F32)
    dtbc_d = p.dram_in('dtbc', (NM * 128, ndc), F32)
    alogc_d = p.dram_in('alogc', (NM * 128, NST * ndc), F32)
    dskc_d = p.dram_in('dskc', (NM * 128, ndc), F32)
    outw = p.dram_in('outw', (NM * DI, D), F32)
    nfwc_d = p.dram_in('nfwc', (128, 8), F32)
    nfbc_d = p.dram_in('nfbc', (128, 8), F32)

    stg = stage_input_ln(p, ctx, x, T=T)
    hf = stg['ln_bf']
    identb = ctx.get_ident(BF16)
    nfw = load_col_vec(p, 'nfw', nfwc_d, D)
    nfb = load_col_vec(p, 'nfb', nfbc_d, D)
    selB_np = np.zeros((32, 16 * 128), np.float32)
    selC_np = np.zeros((32, 16 * 128), np.float32)
    for n in range(16):
        selB_np[n, n * 128:(n + 1) * 128] = 1
        selC_np[16 + n, n * 128:(n + 1) * 128] = 1
    selB = p.sbuf("selB", [32, 16 * 128], BF16)
    selC = p.sbuf("selC", [32, 16 * 128], BF16)
    p.dma('pool', lambda e: e.dma_start(out=selB[:], in_=p.const('selBc', selB_np, F32)[:, :]),
          writes=[selB])
    p.dma('pool', lambda e: e.dma_start(out=selC[:], in_=p.const('selCc', selC_np, F32)[:, :]),
          writes=[selC])

    rf = [p.sbuf(f"rf{c}", [128, T], BF16) for c in range(8)]
    for c in range(8):
        p.op('pool', (lambda c_: lambda e: e.memset(rf[c_][:], 0.0))(c), writes=[rf[c]])

    TB = min(512, T)
    ntb = T // TB
    nsub = TB // 128

    for m in range(NM):
        for c in range(8):
            p.op('dve', (lambda c_: lambda e: e.tensor_tensor(
                rf[c_][:], rf[c_][:], hf[c_][:], OP.add))(c),
                reads=[rf[c], hf[c]], writes=[rf[c]])
        p.open_scope('mx')
        nwcol = p.sbuf("nwcol", [128, 8], F32, scope='mx')
        nbcol = p.sbuf("nbcol", [128, 8], F32, scope='mx')
        convw = p.sbuf("convw", [128, 4 * ndc], F32, scope='mx')
        convb = p.sbuf("convb", [128, ndc], F32, scope='mx')
        dtb = p.sbuf("dtb", [128, ndc], F32, scope='mx')
        dsk = p.sbuf("dsk", [128, ndc], F32, scope='mx')
        alog = p.sbuf("alog", [128, NST * ndc], F32, scope='mx')
        expa = p.sbuf("expa", [128, NST * ndc], F32, scope='mx')
        for t_, src_ in [(nwcol, nwc_d), (nbcol, nbc_d), (convw, convc_d),
                         (convb, convbc_d), (dtb, dtbc_d), (dsk, dskc_d),
                         (alog, alogc_d)]:
            p.dma('sp', (lambda t2, s2: lambda e: e.dma_start(
                out=t2[:], in_=s2[m * 128:(m + 1) * 128, :]))(t_, src_), writes=[t_])
        p.op('act', lambda e: e.activation(expa[:], alog[:], AF.Exp),
             reads=[alog], writes=[expa])
        xpw_sb = [p.sbuf(f"xpw{kc}", [128, 96], BF16, scope='mx') for kc in range(ndc)]
        for kc in range(ndc):
            p.dma('pool', (lambda kc_: lambda e: e.dma_start(
                out=xpw_sb[kc_][:], in_=xpw[m * DI + kc_ * 128:m * DI + (kc_ + 1) * 128, :]))(kc),
                writes=[xpw_sb[kc]])
        dtw_sb = p.sbuf("dtw_sb", [DTR, DI], F32, scope='mx')
        p.dma('sp', lambda e: e.dma_start(out=dtw_sb[:], in_=dtw[m * DTR:(m + 1) * DTR, :]),
              writes=[dtw_sb])
        state = p.sbuf("state", [128, ndc * NST], F32, scope='mx')
        p.op('pool', lambda e: e.memset(state[:], 0.0), writes=[state])
        ctail = p.sbuf("ctail", [128, 4 * ndc], BF16, scope='mx')
        p.op('pool', lambda e: e.memset(ctail[:], 0.0), writes=[ctail])

        for tbi in range(ntb):
            tb0 = tbi * TB
            p.open_scope('tb')
            xm = [p.sbuf(f"xm{c}", [128, TB], BF16, scope='tb') for c in range(8)]
            ln_fm_sbuf(p, ctx, rf, xm, T, D, gcol=nwcol, bcol=nbcol, eps=1e-5,
                       ti0=tb0 // 128, ti1=tb0 // 128 + nsub, out_off0=True)
            xcr = [p.sbuf(f"xcr{dc}", [128, TB], BF16, scope='tb') for dc in range(ndc)]
            bc32 = p.sbuf("bc32", [32, TB], BF16, scope='tb')
            dt_sb = [p.sbuf(f"dt{dc}", [128, TB], BF16, scope='tb') for dc in range(ndc)]
            zsil = [p.sbuf(f"zs{dc}", [128, TB], BF16, scope='tb') for dc in range(ndc)]

            def epi_in(mcj, ps, ptags):
                if mcj < ndc:
                    p.op('dve', (lambda mm: lambda e: e.tensor_copy(
                        xcr[mm][:], ps[:, 0:TB]))(mcj), reads=ptags, writes=[xcr[mcj]])
                else:
                    p.op('act', (lambda mm: lambda e: e.activation(
                        zsil[mm][:], ps[:, 0:TB], AF.Silu))(mcj - ndc),
                        reads=ptags, writes=[zsil[mcj - ndc]])

            proj_fm(p, 'ip', inpw[m * D:(m + 1) * D, :], xm, TB, D, 2 * DI,
                    epilogue=epi_in, fetch_w=512)
            xcs = [p.sbuf(f"xcs{dc}", [128, TB], BF16, scope='tb') for dc in range(ndc)]
            p.open_scope('cv')
            cpad = p.sbuf("cpad", [128, TB + 4], BF16, scope='cv')
            cacc = p.sbuf("cacc", [128, TB], F32, scope='cv')
            for dc in range(ndc):
                p.op('dve', (lambda dc_: lambda e: e.tensor_copy(
                    cpad[:, 0:3], ctail[:, dc_ * 4:dc_ * 4 + 3]))(dc),
                    reads=[(ctail, dc)], writes=[(cpad, 'tail')])
                p.op('dve', (lambda dc_: lambda e: e.tensor_copy(
                    cpad[:, 3:TB + 3], xcr[dc_][:]))(dc),
                    reads=[xcr[dc]], writes=[(cpad, 'm')])
                p.op('dve', (lambda dc_: lambda e: e.tensor_scalar(
                    cacc[:], cpad[:, 0:TB], convw[:, dc_ * 4:dc_ * 4 + 1],
                    convb[:, dc_:dc_ + 1], OP.mult, OP.add))(dc),
                    reads=[(cpad, 'tail'), (cpad, 'm'), convw, convb], writes=[cacc])
                for k in range(1, 4):
                    p.op('dve', (lambda dc_, k_: lambda e: e.scalar_tensor_tensor(
                        cacc[:], cpad[:, k_:k_ + TB], convw[:, dc_ * 4 + k_:dc_ * 4 + k_ + 1],
                        cacc[:], OP.mult, OP.add))(dc, k),
                        reads=[(cpad, 'tail'), (cpad, 'm'), convw, cacc], writes=[cacc])
                p.op('act', (lambda dc_: lambda e: e.activation(
                    xcs[dc_][:], cacc[:], AF.Silu))(dc),
                    reads=[cacc], writes=[xcs[dc]])
                p.op('dve', (lambda dc_: lambda e: e.tensor_copy(
                    ctail[:, dc_ * 4:dc_ * 4 + 3], xcr[dc_][:, TB - 3:TB]))(dc),
                    reads=[xcr[dc]], writes=[(ctail, dc)])
            p.close_scope('cv')
            p.open_scope('xp')
            xp_ps = p.psum("xp_ps", [128, TB], F32, scope='xp')
            xdb = p.sbuf("xdb", [96, TB], F32, scope='xp')
            for kc in range(ndc):
                p.op('pe', (lambda kc_: lambda e: e.matmul(
                    xp_ps[0:96, 0:TB], xpw_sb[kc_][:], xcs[kc_][:],
                    start=(kc_ == 0), stop=(kc_ == ndc - 1)))(kc),
                    reads=[xpw_sb[kc], xcs[kc]], writes=[xp_ps])
            p.op('dve', lambda e: e.tensor_copy(xdb[:], xp_ps[0:96, 0:TB]),
                 reads=[xp_ps], writes=[xdb])
            p.op('dve', lambda e: e.tensor_copy(bc32[:], xdb[64:96, 0:TB]),
                 reads=[xdb], writes=[bc32])
            sp1 = p.sbuf("sp1", [128, TB], F32, scope='xp')
            sp2 = p.sbuf("sp2", [128, TB], F32, scope='xp')
            one_b = ctx.cval(1.0)
            for dc in range(ndc):
                p.op('pe', (lambda dc_: lambda e: e.matmul(
                    xp_ps[:, 0:TB], dtw_sb[0:DTR, dc_ * 128:(dc_ + 1) * 128],
                    xdb[0:DTR, 0:TB], start=True, stop=True))(dc),
                    reads=[dtw_sb, xdb], writes=[xp_ps])
                p.op('dve', (lambda dc_: lambda e: e.tensor_scalar(
                    sp1[:], xp_ps[:, 0:TB], dtb[:, dc_:dc_ + 1], None, OP.add, OP.bypass))(dc),
                    reads=[xp_ps, dtb], writes=[sp1])
                p.op('act', lambda e: e.activation(sp2[:], sp1[:], AF.Abs),
                     reads=[sp1], writes=[(sp2, 'a')])
                p.op('act', lambda e: e.activation(sp2[:], sp2[:], AF.Exp, scale=-1.0),
                     reads=[(sp2, 'a')], writes=[(sp2, 'e')])
                p.op('act', lambda e: e.activation(sp2[:], sp2[:], AF.Ln, bias=one_b[:, 0:1]),
                     reads=[(sp2, 'e'), one_b], writes=[(sp2, 'l')])
                p.op('dve', lambda e: e.tensor_scalar(sp1[:], sp1[:], 0.0, None,
                                                      OP.max, OP.bypass),
                     reads=[sp1], writes=[(sp1, 'mx')])
                p.op('dve', (lambda dc_: lambda e: e.tensor_tensor(
                    dt_sb[dc_][:], sp1[:], sp2[:], OP.add))(dc),
                    reads=[(sp1, 'mx'), (sp2, 'l')], writes=[dt_sb[dc]])
                p.op('dve', (lambda dc_: lambda e: e.tensor_tensor(
                    xcr[dc_][:], dt_sb[dc_][:], xcs[dc_][:], OP.mult))(dc),
                    reads=[dt_sb[dc], xcs[dc]], writes=[xcr[dc]])
            p.close_scope('xp')
            p.open_scope('sc')
            yacc = [p.sbuf(f"ya{dc}", [128, TB], BF16, scope='sc') for dc in range(ndc)]
            b_ps = p.psum("b_ps", [128, TB], F32, scope='sc')
            c_ps = p.psum("c_ps", [128, TB], F32, scope='sc')
            a_sb = p.sbuf("a_sb", [128, TB], F32, scope='sc')
            b_sb = p.sbuf("b_sb", [128, TB], BF16, scope='sc')
            h_sb = p.sbuf("h_sb", [128, TB], F32, scope='sc')
            z_sb = p.sbuf("z_sb", [128, TB], BF16, scope='sc')
            for n in range(NST):
                p.op('pe', (lambda n_: lambda e: e.matmul(
                    b_ps[:, 0:TB], selB[:, n_ * 128:(n_ + 1) * 128], bc32[:],
                    start=True, stop=True))(n),
                    reads=[selB, bc32], writes=[b_ps])
                p.op('pe', (lambda n_: lambda e: e.matmul(
                    c_ps[:, 0:TB], selC[:, n_ * 128:(n_ + 1) * 128], bc32[:],
                    start=True, stop=True))(n),
                    reads=[selC, bc32], writes=[c_ps])
                for dc in range(ndc):
                    col = dc * NST + n
                    p.op('dve', (lambda dc_, n_: lambda e: e.tensor_scalar(
                        a_sb[:], dt_sb[dc_][:], expa[:, n_ * ndc + dc_:n_ * ndc + dc_ + 1],
                        None, OP.mult, OP.bypass))(dc, n),
                        reads=[dt_sb[dc], expa], writes=[(a_sb, 'm')])
                    p.op('act', lambda e: e.activation(a_sb[:], a_sb[:], AF.Exp, scale=-1.0),
                         reads=[(a_sb, 'm')], writes=[(a_sb, 'e')])
                    p.op('dve', (lambda dc_: lambda e: e.tensor_tensor(
                        b_sb[:], xcr[dc_][:], b_ps[:, 0:TB], OP.mult))(dc),
                        reads=[xcr[dc], b_ps], writes=[b_sb])
                    p.op('dve', (lambda col_: lambda e: e.tensor_tensor_scan(
                        h_sb[:], a_sb[:], b_sb[:], state[:, col_:col_ + 1],
                        OP.mult, OP.add))(col),
                        reads=[(a_sb, 'e'), b_sb, (state, col)], writes=[h_sb])
                    p.op('dve', (lambda col_: lambda e: e.tensor_copy(
                        state[:, col_:col_ + 1], h_sb[:, TB - 1:TB]))(col),
                        reads=[h_sb], writes=[(state, col)])
                    p.op('dve', lambda e: e.tensor_tensor(z_sb[:], h_sb[:],
                                                          c_ps[:, 0:TB], OP.mult),
                         reads=[h_sb, c_ps], writes=[z_sb])
                    if n == 0:
                        p.op('dve', (lambda dc_: lambda e: e.tensor_copy(
                            yacc[dc_][:], z_sb[:]))(dc),
                            reads=[z_sb], writes=[yacc[dc]])
                    else:
                        p.op('dve', (lambda dc_: lambda e: e.tensor_tensor(
                            yacc[dc_][:], yacc[dc_][:], z_sb[:], OP.add))(dc),
                            reads=[yacc[dc], z_sb], writes=[yacc[dc]])
            for dc in range(ndc):
                p.op('dve', (lambda dc_: lambda e: e.scalar_tensor_tensor(
                    h_sb[:], xcs[dc_][:], dsk[:, dc_:dc_ + 1], yacc[dc_][:],
                    OP.mult, OP.add))(dc),
                    reads=[xcs[dc], dsk, yacc[dc]], writes=[h_sb])
                p.op('dve', (lambda dc_: lambda e: e.tensor_tensor(
                    xcs[dc_][:], h_sb[:], zsil[dc_][:], OP.mult))(dc),
                    reads=[h_sb, zsil[dc]], writes=[xcs[dc]])
            p.close_scope('sc')
            p.open_scope('op')

            def epi_out(mcj, ps, ptags):
                p.op('dve', (lambda mm: lambda e: e.tensor_copy(
                    hf[mm][:, tb0:tb0 + TB], ps[:, 0:TB]))(mcj),
                    reads=ptags, writes=[(hf[mcj], tb0)])

            proj_fm(p, 'op', outw[m * DI:(m + 1) * DI, :], xcs, TB, DI, D,
                    epilogue=epi_out, fetch_w=256)
            p.close_scope('op')
            p.close_scope('tb')
        p.close_scope('mx')

    for c in range(8):
        p.op('dve', (lambda c_: lambda e: e.tensor_tensor(
            rf[c_][:], rf[c_][:], hf[c_][:], OP.add))(c),
            reads=[rf[c], hf[c]], writes=[rf[c]])
    out_fm = [p.sbuf(f"of{c}", [128, T], BF16) for c in range(8)]
    ln_fm_sbuf(p, ctx, rf, out_fm, T, D, gcol=nfw, bcol=nfb, eps=1e-5)
    store_residual_out(p, ctx, out_fm, x, y, T=T, D=D, f_dt=BF16)
    return p


def make_l2b_inputs(x, norm_w, norm_b, in_proj_w, conv_w, conv_b, x_proj_w,
                    dt_proj_w, dt_proj_b, A_log, D_skip, out_proj_w, normf_w, normf_b):
    NMl = 4
    ins = {'x': x}
    ins['nwc'] = np.concatenate([col_of(norm_w[m]) for m in range(NMl)], 0)
    ins['nbc'] = np.concatenate([col_of(norm_b[m]) for m in range(NMl)], 0)
    ins['inpw'] = np.concatenate([np.ascontiguousarray(in_proj_w[m].T) for m in range(NMl)], 0)
    cc = np.zeros((NMl * 128, 4 * 16), np.float32)
    for m in range(NMl):
        for k in range(4):
            cw = col_of(conv_w[m, :, k])
            for dc in range(16):
                cc[m * 128:(m + 1) * 128, dc * 4 + k] = cw[:, dc]
    ins['convc'] = cc
    ins['convbc'] = np.concatenate([col_of(conv_b[m]) for m in range(NMl)], 0)
    ins['xpw'] = np.concatenate([np.ascontiguousarray(x_proj_w[m].T) for m in range(NMl)], 0)
    ins['dtw'] = np.concatenate([np.ascontiguousarray(dt_proj_w[m].T) for m in range(NMl)], 0)
    ins['dtbc'] = np.concatenate([col_of(dt_proj_b[m]) for m in range(NMl)], 0)
    al = np.zeros((NMl * 128, 16 * 16), np.float32)
    for m in range(NMl):
        for n in range(16):
            ac = col_of(A_log[m, :, n])
            al[m * 128:(m + 1) * 128, n * 16:(n + 1) * 16] = ac
    ins['alogc'] = al
    ins['dskc'] = np.concatenate([col_of(D_skip[m]) for m in range(NMl)], 0)
    ins['outw'] = np.concatenate([np.ascontiguousarray(out_proj_w[m].T) for m in range(NMl)], 0)
    ins['nfwc'] = col_of(normf_w)
    ins['nfbc'] = col_of(normf_b)
    return ins


def ref_l2_attn_b(x, norm_w, norm_b, in_proj_w, conv_w, conv_b, x_proj_w,
                  dt_proj_w, dt_proj_b, A_log, D_skip, out_proj_w, normf_w, normf_b):
    xs = x.astype(np.float64)
    mu = xs.mean(-1, keepdims=True)
    z0 = (xs - mu) / np.sqrt(xs.var(-1, keepdims=True) + 1e-5)
    EPS = 1e-5

    def lnorm(v, w, b):
        mu2 = v.mean(-1, keepdims=True)
        var = ((v - mu2) ** 2).mean(-1, keepdims=True)
        return (v - mu2) / np.sqrt(var + EPS) * w + b

    def mixer(xm, m):
        xz = xm @ in_proj_w[m].T.astype(np.float64)
        xc, zz = np.split(xz, 2, axis=-1)
        K = 4
        xp = np.pad(xc, ((K - 1, 0), (0, 0)))
        yv = conv_b[m].astype(np.float64).copy()
        acc = np.zeros_like(xc) + conv_b[m]
        for k in range(K):
            acc = acc + xp[k:k + xc.shape[0], :] * conv_w[m, :, k]
        xcs = acc / (1 + np.exp(-acc))
        xdb = xcs @ x_proj_w[m].T.astype(np.float64)
        pre = xdb[:, :64] @ dt_proj_w[m].T.astype(np.float64) + dt_proj_b[m]
        dt = np.log1p(np.exp(-np.abs(pre))) + np.maximum(pre, 0)
        Bm = xdb[:, 64:80]
        Cm = xdb[:, 80:96]
        A = -np.exp(A_log[m].astype(np.float64))
        Tn, Di = xc.shape
        h = np.zeros((Di, 16))
        ys = np.zeros((Tn, Di))
        for t in range(Tn):
            dA = np.exp(dt[t][:, None] * A)
            h = dA * h + (dt[t] * xcs[t])[:, None] * Bm[t][None, :]
            ys[t] = h @ Cm[t]
        yv = (ys + xcs * D_skip[m]) * (zz / (1 + np.exp(-zz)))
        return yv @ out_proj_w[m].T.astype(np.float64)

    h = z0
    r = np.zeros_like(z0)
    for m in range(4):
        r = h + r
        xm = lnorm(r, norm_w[m], norm_b[m])
        h = mixer(xm, m)
    out = lnorm(h + r, normf_w, normf_b)
    return x + out.astype(np.float32)



"""L2_ffn: dense MoE, 8 experts, exact top-2 gating via dense weighted sum.
Host layout: gw (1024, 8) [gate_w.T]; gb (1, 8); w1 (8*1024, 4096) natural;
b1c (8*128, 32) [col_of per expert]; w2 (8*4096, 1024) natural; b2c (8*128, 8).
"""




def build_l2_ffn(T=2048):
    p = Prog("l2f")
    ctx = Ctx(p)
    D, H, E = 1024, 4096, 8
    x = p.dram_in('x', (T, D), F32)
    y = p.dram_out('y', (T, D), F32)
    gw = p.dram_in('gw', (D, E), F32)
    gb = p.dram_in('gb', (1, E), F32)
    w1 = p.dram_in('w1', (E * D, H), F32)
    b1c = p.dram_in('b1c', (E * 128, H // 128), F32)
    w2 = p.dram_in('w2', (E * H, D), F32)
    b2c = p.dram_in('b2c', (E * 128, D // 128), F32)

    p.open_scope('f32')
    stg = stage_input_ln(p, ctx, x, T=T, keep_f32=True, f32_scope='f32')
    ln = stg['ln_bf']
    lnf = stg['ln_f32']
    ident = ctx.get_ident(BF16)
    ones = ctx.get_ones_col()
    nt = T // 128

    gdram = p.dram_tmp('gdram', (E, T), F32)
    p.open_scope('gate')
    gwt = p.sbuf("gwt", [128, 8 * E], F32, scope='gate')
    for kc in range(8):
        p.dma('pool', lambda e: e.dma_start(out=gwt[:, kc * E:(kc + 1) * E],
                                            in_=gw[kc * 128:(kc + 1) * 128, :]),
              writes=[(gwt, kc)])
    gbt = p.sbuf("gbt", [1, E], F32, scope='gate')
    p.dma('sp', lambda e: e.dma_start(out=gbt[:], in_=gb[:, :]), writes=[gbt])
    g_ps = p.psum("g_ps", [128, 512], F32, scope='gate')
    t_ps = p.psum("t_ps", [128, 128], F32, scope='gate')
    grow = p.sbuf("grow", [128, 4], F32, scope='gate')
    t8 = p.sbuf("t8", [128, 8], F32, scope='gate')
    msk = p.sbuf("msk", [128, 8], F32, scope='gate')
    gtile = p.sbuf("gtile", [128, 8], F32, scope='gate')
    g8 = p.sbuf("g8", [8, T], F32, scope='gate')
    identf = ctx.get_ident(F32)
    gbias_bc = p.sbuf("gbias_bc", [128, 8], F32, scope='gate')
    p.op('pe', lambda e: e.matmul(g_ps[:, 8:8 + E], ones[0:1, :].broadcast_to((1, 128)),
                                  gbt[:], start=True, stop=True),
         reads=[gbt, ones], writes=[(g_ps, 'b')])
    p.op('dve', lambda e: e.tensor_copy(gbias_bc[:], g_ps[:, 8:8 + E]),
         reads=[(g_ps, 'b')], writes=[gbias_bc])
    for ti in range(nt):
        for kc in range(8):
            p.op('pe', lambda e: e.matmul(
                g_ps[:, 0:E], lnf[kc][:, ti * 128:(ti + 1) * 128], gwt[:, kc * E:(kc + 1) * E],
                start=(kc == 0), stop=(kc == 7)),
                reads=[(lnf[kc], ti), (gwt, kc)], writes=[(g_ps, 'g')])
        p.op('dve', lambda e: e.tensor_tensor(gtile[:], g_ps[:, 0:E], gbias_bc[:], OP.add),
             reads=[(g_ps, 'g'), gbias_bc], writes=[gtile])
        p.op('dve', lambda e: e.tensor_reduce(grow[:, 0:1], gtile[:], AX.X, OP.max),
             reads=[gtile], writes=[(grow, 'm')])
        p.op('dve', lambda e: e.tensor_scalar(grow[:, 1:2], grow[:, 0:1], -1.0, None,
                                              OP.mult, OP.bypass),
             reads=[(grow, 'm')], writes=[(grow, 'nm')])
        p.op('act', lambda e: e.activation(gtile[:], gtile[:], AF.Exp, bias=grow[:, 1:2],
                                           accum_out=grow[:, 2:3]),
             reads=[gtile, (grow, 'nm')], writes=[gtile, (grow, 's')])
        p.op('dve', lambda e: e.reciprocal(grow[:, 3:4], grow[:, 2:3]),
             reads=[(grow, 's')], writes=[(grow, 'r')])
        p.op('dve', lambda e: e.tensor_scalar(gtile[:], gtile[:], grow[:, 3:4], None,
                                              OP.mult, OP.bypass),
             reads=[gtile, (grow, 'r')], writes=[gtile])
        p.op('dve', lambda e: e.max(t8[:], gtile[:]), reads=[gtile], writes=[t8])
        p.op('dve', lambda e: e.tensor_scalar(msk[:], gtile[:], t8[:, 1:2], None,
                                              OP.is_ge, OP.bypass),
             reads=[gtile, t8], writes=[msk])
        p.op('dve', lambda e: e.tensor_tensor(gtile[:], gtile[:], msk[:], OP.mult),
             reads=[gtile, msk], writes=[gtile])
        p.op('pe', lambda e: e.transpose(t_ps[0:8, :], gtile[:], identf[:]),
             reads=[gtile, identf], writes=[t_ps])
        p.op('dve', lambda e: e.tensor_copy(g8[0:8, ti * 128:(ti + 1) * 128], t_ps[0:8, :]),
             reads=[t_ps], writes=[(g8, ti)])
    p.dma('sp', lambda e: e.dma_start(out=gdram[:, :], in_=g8[:]),
          reads=[(g8, ti) for ti in range(nt)])
    p.close_scope('gate')
    p.close_scope('f32')

    b1t = p.sbuf("b1t", [128, E * (H // 128)], F32)
    b2t = p.sbuf("b2t", [128, E * (D // 128)], F32)
    for e in range(E):
        p.dma('sp', lambda en: en.dma_start(
            out=b1t[:, e * (H // 128):(e + 1) * (H // 128)],
            in_=b1c[e * 128:(e + 1) * 128, :]), writes=[(b1t, e)])
        p.dma('sp', lambda en: en.dma_start(
            out=b2t[:, e * (D // 128):(e + 1) * (D // 128)],
            in_=b2c[e * 128:(e + 1) * 128, :]), writes=[(b2t, e)])

    acc = [p.sbuf(f"acc{mc}", [128, T], F32) for mc in range(D // 128)]
    first = [True] * (D // 128)
    TB = min(512, T)
    nh = H // 128
    for e in range(E):
        p.open_scope('bce')
        bc_sb = p.sbuf("bc_sb", [128, T], F32, scope='bce')
        gerow_e = p.sbuf("gerow", [1, T], F32, scope='bce')
        p.dma('sp', lambda en: en.dma_start(out=gerow_e[:], in_=gdram[e:e + 1, :]),
              writes=[gerow_e])
        p.open_scope('bcps')
        bc_ps = p.psum("bc_ps", [128, 2048], F32, scope='bcps')
        for n0 in range(0, T, 512):
            nn = min(512, T - n0)
            p.op('pe', lambda en: en.matmul(bc_ps[:, n0:n0 + nn],
                                            ones[0:1, :].broadcast_to((1, 128)),
                                            gerow_e[0:1, n0:n0 + nn], start=True, stop=True),
                 reads=[gerow_e, ones], writes=[(bc_ps, n0)])
        p.op('dve', lambda en: en.tensor_copy(bc_sb[:], bc_ps[:, 0:T]),
             reads=[(bc_ps, n0) for n0 in range(0, T, 512)], writes=[bc_sb])
        p.close_scope('bcps')
        hch = [p.sbuf(f"h{hc}", [128, TB], BF16, scope='bce') for hc in range(nh)]
        for tb0 in range(0, T, TB):
            p.open_scope('ph1')
            ps1 = p.psum("ps1", [128, TB], F32, scope='ph1')
            ws1 = [[p.sbuf(f"w1s{b}_{kc}", [128, 512], BF16, scope='ph1')
                    for kc in range(8)] for b in range(2)]
            for hb in range(0, nh, 4):
                ws = ws1[(hb // 4) % 2]
                for kc in range(8):
                    p.dma('pool', lambda en: en.dma_start(
                        out=ws[kc][:],
                        in_=w1[e * D + kc * 128:e * D + (kc + 1) * 128,
                               hb * 128:(hb + 4) * 128]), writes=[ws[kc]])
                for j in range(4):
                    hc = hb + j
                    for kc in range(8):
                        p.op('pe', lambda en: en.matmul(
                            ps1[:, 0:TB], ws[kc][:, j * 128:(j + 1) * 128],
                            ln[kc][:, tb0:tb0 + TB], start=(kc == 0), stop=(kc == 7)),
                            reads=[ws[kc], ln[kc]], writes=[ps1])
                    p.op('act', lambda en: en.activation(
                        hch[hc][:], ps1[:, 0:TB], AF.Relu,
                        bias=b1t[:, e * nh + hc:e * nh + hc + 1]),
                        reads=[ps1, (b1t, e)], writes=[hch[hc]])
            p.close_scope('ph1')
            p.open_scope('ph2')
            ps2 = [p.psum(f"ps2_{mc}", [128, TB], F32, scope='ph2') for mc in range(8)]
            ws2 = [p.sbuf(f"w2s{b}", [128, 1024], BF16, scope='ph2') for b in range(3)]
            for hc in range(nh):
                wb = ws2[hc % 3]
                p.dma('pool', lambda en: en.dma_start(
                    out=wb[:], in_=w2[e * H + hc * 128:e * H + (hc + 1) * 128, :]),
                    writes=[wb])
                for mc in range(8):
                    p.op('pe', lambda en: en.matmul(
                        ps2[mc][:, 0:TB], wb[:, mc * 128:(mc + 1) * 128],
                        hch[hc][:, 0:TB], start=(hc == 0), stop=(hc == nh - 1)),
                        reads=[wb, hch[hc]], writes=[ps2[mc]])
            tmp = p.sbuf("tmp2", [128, TB], F32, scope='ph2')
            for mc in range(8):
                p.op('dve', lambda en: en.scalar_tensor_tensor(
                    tmp[:], ps2[mc][:, 0:TB], b2t[:, e * 8 + mc:e * 8 + mc + 1],
                    bc_sb[:, tb0:tb0 + TB], OP.add, OP.mult),
                    reads=[ps2[mc], (b2t, e), bc_sb], writes=[tmp])
                if first[mc]:
                    p.op('act', lambda en: en.copy(acc[mc][:, tb0:tb0 + TB], tmp[:]),
                         reads=[tmp], writes=[(acc[mc], tb0)])
                else:
                    p.op('dve', lambda en: en.tensor_tensor(
                        acc[mc][:, tb0:tb0 + TB], acc[mc][:, tb0:tb0 + TB], tmp[:], OP.add),
                        reads=[(acc[mc], tb0), tmp], writes=[(acc[mc], tb0)])
            p.close_scope('ph2')
        first = [False] * 8
        p.close_scope('bce')

    store_residual_out(p, ctx, acc, x, y, T=T, D=D, f_dt=F32)
    return p


def ref_l2_ffn(x, gw, gb, w1, b1, w2, b2):
    xs = x.astype(np.float64)
    mu = xs.mean(-1, keepdims=True)
    z = (xs - mu) / np.sqrt(xs.var(-1, keepdims=True) + 1e-5)
    logits = z @ gw.T.astype(np.float64) + gb
    eexp = np.exp(logits - logits.max(-1, keepdims=True))
    probs = eexp / eexp.sum(-1, keepdims=True)
    order = np.argsort(-probs, axis=-1, kind='stable')
    gate = np.zeros_like(probs)
    np.put_along_axis(gate, order[:, :2],
                      np.take_along_axis(probs, order[:, :2], -1), -1)
    out = np.zeros_like(xs[:, :1024])
    for e in range(8):
        h = np.maximum(z @ w1[e].astype(np.float64) + b1[e], 0)
        ye = h @ w2[e].astype(np.float64) + b2[e]
        out += gate[:, e:e + 1] * ye
    return x + out.astype(np.float32)



"""L3_attn_a: standard MHA (16 heads, d=64, non-causal, q scaled by 1/8).
Weights host-transposed to (1024, 1024); biases host-reshaped via col_of.
y = x + (softmax((q/8) k^T) v) @ owT + ob
"""




def build_l3_attn_a(T=2048):
    p = Prog("l3a")
    ctx = Ctx(p)
    D = 1024
    x = p.dram_in('x', (T, D), F32)
    y = p.dram_out('y', (T, D), F32)
    qw = p.dram_in('qw', (D, D), F32)
    kw = p.dram_in('kw', (D, D), F32)
    vw = p.dram_in('vw', (D, D), F32)
    ow = p.dram_in('ow', (D, D), F32)
    qb = p.dram_in('qb', (128, 8), F32)
    kb = p.dram_in('kb', (128, 8), F32)
    vb = p.dram_in('vb', (128, 8), F32)
    ob = p.dram_in('ob', (128, 8), F32)

    stg = stage_input_ln(p, ctx, x, T=T)
    ln = stg['ln_bf']
    qbc = load_col_vec(p, 'qbc', qb, D)
    kbc = load_col_vec(p, 'kbc', kb, D)
    vbc = load_col_vec(p, 'vbc', vb, D)
    obc = load_col_vec(p, 'obc', ob, D)

    ao = [p.sbuf(f"ao{c}", [128, T], BF16) for c in range(8)]
    p.open_scope('qkv')
    qf = [p.sbuf(f"qf{c}", [128, T], BF16, scope='qkv') for c in range(8)]
    kf = [p.sbuf(f"kf{c}", [128, T], BF16, scope='qkv') for c in range(8)]
    vf = [p.sbuf(f"vf{c}", [128, T], BF16, scope='qkv') for c in range(8)]
    proj_fm(p, 'q', qw, ln, T, D, D, bias_col=qbc, out_chunks=qf)
    proj_fm(p, 'k', kw, ln, T, D, D, bias_col=kbc, out_chunks=kf)
    proj_fm(p, 'v', vw, ln, T, D, D, bias_col=vbc, out_chunks=vf)
    attention_full(p, ctx, 'at', qf, kf, vf, ao, T=T, heads=16, hd=64, scale=0.125)
    p.close_scope('qkv')

    out_fm = proj_fm(p, 'o', ow, ao, T, D, D, bias_col=obc, out_dt=BF16)
    store_residual_out(p, ctx, out_fm, x, y, T=T, D=D, f_dt=BF16)
    return p


def ref_l3_attn_a(x, qw, qb, kw, kb, vw, vb, ow, ob):
    xs = x.astype(np.float64)
    mu = xs.mean(-1, keepdims=True)
    z = (xs - mu) / np.sqrt(xs.var(-1, keepdims=True) + 1e-5)
    T, D = x.shape
    H, hd = 16, 64
    q = (z @ qw.T.astype(np.float64) + qb) / np.sqrt(hd)
    k = z @ kw.T.astype(np.float64) + kb
    v = z @ vw.T.astype(np.float64) + vb
    q = q.reshape(T, H, hd).transpose(1, 0, 2)
    k = k.reshape(T, H, hd).transpose(1, 0, 2)
    v = v.reshape(T, H, hd).transpose(1, 0, 2)
    s = q @ k.transpose(0, 2, 1)
    s = s - s.max(-1, keepdims=True)
    e = np.exp(s)
    a = e / e.sum(-1, keepdims=True)
    ctx = (a @ v).transpose(1, 0, 2).reshape(T, D)
    return x + (ctx @ ow.T.astype(np.float64) + ob).astype(np.float32)



"""L3_attn_b: qkv = x@W.T + b (W host-transposed to (1024, 3072); b col_of (128,24));
attn[i,j] = softmax_i(q_i . k_j / 1024); out = attn @ v; y = x + out.
Implemented in transposed orientation: ST[j, i] = scores^T; softmax over free i.
"""




def build_l3_attn_b(T=2048):
    p = Prog("l3b")
    ctx = Ctx(p)
    D = 1024
    x = p.dram_in('x', (T, D), F32)
    y = p.dram_out('y', (T, D), F32)
    W = p.dram_in('W', (D, 3 * D), F32)
    b = p.dram_in('b', (128, 24), F32)

    stg = stage_input_ln(p, ctx, x, T=T)
    ln = stg['ln_bf']
    bc = load_col_vec(p, 'bc', b, 3 * D)

    pst = [p.sbuf(f"pst{jc}", [128, T], BF16) for jc in range(T // 128)]
    ident = ctx.get_ident(BF16)
    nj = T // 128
    p.open_scope('A')
    vT = [p.sbuf(f"vT{jc}", [128, D], BF16, scope='A') for jc in range(nj)]
    p.open_scope('A2')
    vf = [p.sbuf(f"vf{c}", [128, T], BF16, scope='A2') for c in range(8)]
    proj_fm(p, 'v', W[0:D, 2 * D:3 * D], ln, T, D, D,
            bias_col=bc, out_chunks=vf, bias_off=16, fetch_w=256)
    tps = p.psum("tps", [128, 128], BF16, scope='A2')
    for jc in range(nj):
        for c in range(8):
            p.op('pe', lambda e: e.transpose(
                tps[:], vf[c][:, jc * 128:(jc + 1) * 128], ident[:]),
                reads=[vf[c], ident], writes=[tps])
            p.op('dve', lambda e: e.tensor_copy(vT[jc][:, c * 128:(c + 1) * 128], tps[:]),
                 reads=[tps], writes=[(vT[jc], c)])
    p.close_scope('A2')
    p.open_scope('B')
    qf = [p.sbuf(f"qf{c}", [128, T], BF16, scope='B') for c in range(8)]
    kf = [p.sbuf(f"kf{c}", [128, T], BF16, scope='B') for c in range(8)]
    proj_fm(p, 'q', W[0:D, 0:D], ln, T, D, D, bias_col=bc, out_chunks=qf, bias_off=0, fetch_w=256)
    proj_fm(p, 'k', W[0:D, D:2 * D], ln, T, D, D, bias_col=bc, out_chunks=kf, bias_off=8, fetch_w=256)

    s_ps = p.psum("s_ps", [128, 2048], F32, scope='B')
    mrow = p.sbuf("mrow", [128, 8], F32, scope='B')
    NS = 512
    for jc in range(nj):
        for n0 in range(0, T, NS):
            nn = min(NS, T - n0)
            for kc in range(8):
                p.op('pe', lambda e: e.matmul(
                    s_ps[:, n0:n0 + nn], kf[kc][:, jc * 128:(jc + 1) * 128],
                    qf[kc][:, n0:n0 + nn], start=(kc == 0), stop=(kc == 7)),
                    reads=[kf[kc], qf[kc]], writes=[(s_ps, n0)])
        stags = [(s_ps, n0) for n0 in range(0, T, NS)]
        p.op('dve', lambda e: e.tensor_reduce(mrow[:, 0:1], s_ps[:, 0:T], AX.X, OP.max),
             reads=stags, writes=[(mrow, 'm')])
        p.op('dve', lambda e: e.tensor_scalar(mrow[:, 1:2], mrow[:, 0:1],
                                              -1.0 / 1024.0, None, OP.mult, OP.bypass),
             reads=[(mrow, 'm')], writes=[(mrow, 'nm')])
        p.op('act', lambda e: e.activation(pst[jc][:], s_ps[:, 0:T], AF.Exp,
                                           bias=mrow[:, 1:2], scale=1.0 / 1024.0,
                                           accum_out=mrow[:, 2:3]),
             reads=stags + [(mrow, 'nm')], writes=[pst[jc], (mrow, 's')])
        p.op('dve', lambda e: e.reciprocal(mrow[:, 3:4], mrow[:, 2:3]),
             reads=[(mrow, 's')], writes=[(mrow, 'r')])
        p.op('dve', lambda e: e.tensor_scalar(pst[jc][:], pst[jc][:], mrow[:, 3:4],
                                              None, OP.mult, OP.bypass),
             reads=[pst[jc], (mrow, 'r')], writes=[pst[jc]])

    p.close_scope('B')
    out_fm = [p.sbuf(f"of{c}", [128, T], BF16) for c in range(8)]
    s_ps = p.psum("s_ps2", [128, 2048], F32, scope='A')
    o_ps = [s_ps]
    for mc in range(8):
        ps = o_ps[0]
        for n0 in range(0, T, NS):
            nn = min(NS, T - n0)
            for jc in range(nj):
                p.op('pe', lambda e: e.matmul(
                    ps[:, n0:n0 + nn], vT[jc][:, mc * 128:(mc + 1) * 128],
                    pst[jc][:, n0:n0 + nn], start=(jc == 0), stop=(jc == nj - 1)),
                    reads=[(vT[jc], 'all'), pst[jc]], writes=[(ps, n0)])
        ptags = [(ps, n0) for n0 in range(0, T, NS)]
        p.op('dve', lambda e: e.tensor_copy(out_fm[mc][:], ps[:, 0:T]),
             reads=ptags, writes=[out_fm[mc]])
    p.close_scope('A')

    store_residual_out(p, ctx, out_fm, x, y, T=T, D=D, f_dt=BF16)
    return p


def ref_l3_attn_b(x, W, b):
    xs = x.astype(np.float64)
    mu = xs.mean(-1, keepdims=True)
    z = (xs - mu) / np.sqrt(xs.var(-1, keepdims=True) + 1e-5)
    qkv = z @ W.T.astype(np.float64) + b
    q, k, v = np.split(qkv, 3, axis=-1)
    attn = q @ k.T / k.shape[-1]
    attn = attn - attn.max(axis=0, keepdims=True)
    e = np.exp(attn)
    a = e / e.sum(axis=0, keepdims=True)
    return x + (a @ v).astype(np.float32)



"""L3_ffn: blockwise-FP4 (E2M1) quant-dequant SwiGLU.
w_gate, w_up (2816, 1024) NATURAL; w_down (1024, 2816) NATURAL.
All quantization on device (blocks of 16 along each tensor's last dim).
"""



MID = [0.25, 0.75, 1.25, 1.75, 2.5, 3.5, 5.0]
COEF = [0.5, 0.5, 0.5, 0.5, 1.0, 1.0, 2.0]


def make_quant_scr(p, W, scope=None):
    return dict(
        xa=p.sbuf("q_xa", [128, W], F32, scope=scope),
        sgn=p.sbuf("q_sgn", [128, W], F32, scope=scope),
        sc=p.sbuf("q_sc", [128, W // 16], F32, scope=scope),
        th=p.sbuf("q_th", [128, W // 16], F32, scope=scope),
        acc=p.sbuf("q_acc", [128, W], F32, scope=scope),
        msk=p.sbuf("q_msk", [128, W], F32, scope=scope),
    )


def fp4_quant_tile(p, src_ap, dst_ap, W, scr, reads, writes):
    nb = W // 16
    xa, sgn, sc, th, acc, msk = (scr['xa'], scr['sgn'], scr['sc'], scr['th'],
                                 scr['acc'], scr['msk'])
    p.op('act', lambda e: e.activation(xa[:, 0:W], src_ap, AF.Abs),
         reads=reads, writes=[xa])
    p.op('act', lambda e: e.activation(sgn[:, 0:W], src_ap, AF.Sign),
         reads=reads, writes=[sgn])
    p.op('dve', lambda e: e.tensor_reduce(
        sc[:, 0:nb], xa[:, 0:W].rearrange("p (b s) -> p b s", s=16), AX.X, OP.max),
        reads=[xa], writes=[sc])
    p.op('dve', lambda e: e.tensor_scalar(sc[:, 0:nb], sc[:, 0:nb], 1.0 / 6.0, 1e-12,
                                          OP.mult, OP.max),
         reads=[sc], writes=[sc])
    for k in range(7):
        p.op('dve', (lambda k_: lambda e: e.tensor_scalar(
            th[:, 0:nb], sc[:, 0:nb], MID[k_], None, OP.mult, OP.bypass))(k),
            reads=[sc], writes=[th])
        p.op('dve', lambda e: e.tensor_tensor(
            msk[:, 0:W].rearrange("p (b s) -> p b s", s=16),
            xa[:, 0:W].rearrange("p (b s) -> p b s", s=16),
            th[:, 0:nb].unsqueeze(2).broadcast_to((128, nb, 16)), OP.is_gt),
            reads=[xa, th], writes=[msk])
        if k == 0:
            p.op('dve', (lambda k_: lambda e: e.tensor_scalar(
                acc[:, 0:W], msk[:, 0:W], COEF[k_], None, OP.mult, OP.bypass))(k),
                reads=[msk], writes=[acc])
        else:
            p.op('dve', (lambda k_: lambda e: e.scalar_tensor_tensor(
                acc[:, 0:W], msk[:, 0:W], COEF[k_], acc[:, 0:W], OP.mult, OP.add))(k),
                reads=[msk, acc], writes=[acc])
    p.op('dve', lambda e: e.tensor_tensor(acc[:, 0:W], acc[:, 0:W], sgn[:, 0:W], OP.mult),
         reads=[acc, sgn], writes=[acc])
    p.op('dve', lambda e: e.tensor_tensor(
        dst_ap.rearrange("p (b s) -> p b s", s=16),
        acc[:, 0:W].rearrange("p (b s) -> p b s", s=16),
        sc[:, 0:nb].unsqueeze(2).broadcast_to((128, nb, 16)), OP.mult),
        reads=[acc, sc], writes=writes)


def build_l3_ffn(T=2048):
    p = Prog("l3f")
    ctx = Ctx(p)
    D, I = 1024, 2816
    nI = I // 128
    x = p.dram_in('x', (T, D), F32)
    y = p.dram_out('y', (T, D), F32)
    wg = p.dram_in('wg', (I, D), F32)
    wu = p.dram_in('wu', (I, D), F32)
    wd = p.dram_in('wd', (D, I), F32)

    wgq = p.dram_tmp('wgq', (D, I), BF16)
    wuq = p.dram_tmp('wuq', (D, I), BF16)
    wdq = p.dram_tmp('wdq', (I, D), BF16)

    identf = ctx.get_ident(F32)

    def quant_weight(wsrc, wdst, R, C):
        nR, nC = R // 128, C // 128
        SCP = p._uname('_wq')
        p.open_scope(SCP)
        strip = p.sbuf("wq_strip", [128, C], F32, scope=SCP)
        scr = make_quant_scr(p, C, scope=SCP)
        qd = p.sbuf("wq_qd", [128, C], F32, scope=SCP)
        asm = [p.sbuf(f"wq_asm{kc}", [128, R], BF16, scope=SCP) for kc in range(nC)]
        t_ps = p.psum("wq_tps", [128, 128], F32, scope=SCP)
        for s in range(nR):
            p.dma('sp', lambda e: e.dma_start(out=strip[:], in_=wsrc[s * 128:(s + 1) * 128, :]),
                  writes=[strip])
            fp4_quant_tile(p, strip[:], qd[:], C, scr, [strip], [qd])
            for kc in range(nC):
                p.op('pe', lambda e: e.transpose(
                    t_ps[:], qd[:, kc * 128:(kc + 1) * 128], identf[:]),
                    reads=[qd, identf], writes=[t_ps])
                p.op('dve', lambda e: e.tensor_copy(asm[kc][:, s * 128:(s + 1) * 128], t_ps[:]),
                     reads=[t_ps], writes=[(asm[kc], s)])
        for kc in range(nC):
            p.dma('sp', lambda e: e.dma_start(out=wdst[kc * 128:(kc + 1) * 128, :],
                                              in_=asm[kc][:]),
                  reads=[(asm[kc], s) for s in range(nR)])
        p.close_scope(SCP)

    quant_weight(wg, wgq, I, D)
    quant_weight(wu, wuq, I, D)
    quant_weight(wd, wdq, D, I)

    stg = stage_input_ln(p, ctx, x, T=T)
    ln = stg['ln_bf']
    xq = [p.sbuf(f"xq{c}", [128, T], BF16) for c in range(8)]
    p.open_scope('xq')
    xtm = p.sbuf("xtm", [128, D], F32, scope='xq')
    xqtm = p.sbuf("xqtm", [128, D], F32, scope='xq')
    scr = make_quant_scr(p, D, scope='xq')
    t_ps = p.psum("xq_tps", [128, 128], F32, scope='xq')
    t_psb = p.psum("xq_tpsb", [128, 128], BF16, scope='xq')
    nt = T // 128
    for ti in range(nt):
        for c in range(8):
            p.op('pe', lambda e: e.transpose(
                t_psb[:], ln[c][:, ti * 128:(ti + 1) * 128], ctx.get_ident(BF16)[:]),
                reads=[(ln[c], ti), ctx.get_ident(BF16)], writes=[t_psb])
            p.op('dve', lambda e: e.tensor_copy(xtm[:, c * 128:(c + 1) * 128], t_psb[:]),
                 reads=[t_psb], writes=[(xtm, c)])
        fp4_quant_tile(p, xtm[:], xqtm[:], D, scr,
                       [(xtm, c) for c in range(8)], [xqtm])
        for c in range(8):
            p.op('pe', lambda e: e.transpose(
                t_ps[:], xqtm[:, c * 128:(c + 1) * 128], identf[:]),
                reads=[xqtm, identf], writes=[t_ps])
            p.op('dve', lambda e: e.tensor_copy(xq[c][:, ti * 128:(ti + 1) * 128], t_ps[:]),
                 reads=[t_ps], writes=[(xq[c], ti)])
    p.close_scope('xq')

    out_fm = [p.sbuf(f"of{c}", [128, T], BF16) for c in range(8)]
    TB = min(512, T)
    ntb = TB // 128
    for tb0 in range(0, T, TB):
        p.open_scope('blk')
        hch = [p.sbuf(f"h{hc}", [128, TB], BF16, scope='blk') for hc in range(nI)]
        p.open_scope('gu')
        gch = [p.sbuf(f"g{hc}", [128, TB], BF16, scope='gu') for hc in range(nI)]

        def epi_g(mcj, ps, ptags):
            p.op('act', lambda e: e.activation(gch[mcj][:], ps[:, 0:TB], AF.Silu),
                 reads=ptags, writes=[gch[mcj]])

        def epi_u(mcj, ps, ptags):
            p.op('dve', lambda e: e.tensor_tensor(hch[mcj][:], ps[:, 0:TB], gch[mcj][:],
                                                  OP.mult),
                 reads=ptags + [gch[mcj]], writes=[hch[mcj]])

        proj_fm(p, 'pg', wgq, xq, TB, D, I, epilogue=epi_g, x_off=tb0)
        proj_fm(p, 'pu', wuq, xq, TB, D, I, epilogue=epi_u, x_off=tb0)
        p.close_scope('gu')

        p.open_scope('hq')
        htm = p.sbuf("htm", [128, I], F32, scope='hq')
        hqtm = p.sbuf("hqtm", [128, I], F32, scope='hq')
        scrh = make_quant_scr(p, I // 2, scope='hq')
        h_ps = p.psum("hq_tps", [128, 128], F32, scope='hq')
        h_psb = p.psum("hq_tpsb", [128, 128], BF16, scope='hq')
        for sj in range(ntb):
            for hc in range(nI):
                p.op('pe', lambda e: e.transpose(
                    h_psb[:], hch[hc][:, sj * 128:(sj + 1) * 128], ctx.get_ident(BF16)[:]),
                    reads=[hch[hc], ctx.get_ident(BF16)], writes=[h_psb])
                p.op('dve', lambda e: e.tensor_copy(htm[:, hc * 128:(hc + 1) * 128], h_psb[:]),
                     reads=[h_psb], writes=[(htm, hc)])
            fp4_quant_tile(p, htm[:, 0:I // 2], hqtm[:, 0:I // 2], I // 2, scrh,
                           [(htm, hc) for hc in range(nI)], [(hqtm, 0)])
            fp4_quant_tile(p, htm[:, I // 2:I], hqtm[:, I // 2:I], I // 2, scrh,
                           [(htm, hc) for hc in range(nI)], [(hqtm, 1)])
            for hc in range(nI):
                p.op('pe', lambda e: e.transpose(
                    h_ps[:], hqtm[:, hc * 128:(hc + 1) * 128], identf[:]),
                    reads=[(hqtm, 0), (hqtm, 1)], writes=[h_ps])
                p.op('dve', lambda e: e.tensor_copy(hch[hc][:, sj * 128:(sj + 1) * 128], h_ps[:]),
                     reads=[h_ps], writes=[hch[hc]])
        p.close_scope('hq')

        def epi_d(mcj, ps, ptags):
            p.op('dve', (lambda tb0_: lambda e: e.tensor_copy(
                out_fm[mcj][:, tb0_:tb0_ + TB], ps[:, 0:TB]))(tb0),
                reads=ptags, writes=[(out_fm[mcj], tb0)])

        proj_fm(p, 'pd', wdq, hch, TB, I, D, epilogue=epi_d, fetch_w=128)
        p.close_scope('blk')

    store_residual_out(p, ctx, out_fm, x, y, T=T, D=D, f_dt=BF16)
    return p


def fp4_qd_np(x):
    GRID = np.array([0.0, 0.5, 1.0, 1.5, 2.0, 3.0, 4.0, 6.0])
    MIDn = np.array(MID)
    shape = x.shape
    xb = x.reshape(shape[:-1] + (shape[-1] // 16, 16)).astype(np.float64)
    scale = np.maximum(np.abs(xb).max(-1, keepdims=True) / 6.0, 1e-12)
    xn = xb / scale
    idx = np.searchsorted(MIDn, np.abs(xn))
    q = GRID[idx] * np.sign(xn) * scale
    return q.reshape(shape)


def ref_l3_ffn(x, wg, wu, wd):
    xs = x.astype(np.float64)
    mu = xs.mean(-1, keepdims=True)
    z = (xs - mu) / np.sqrt(xs.var(-1, keepdims=True) + 1e-5)
    zq = fp4_qd_np(z)
    gate = zq @ fp4_qd_np(wg.astype(np.float64)).T
    up = zq @ fp4_qd_np(wu.astype(np.float64)).T
    h = gate / (1 + np.exp(-gate)) * up
    hq = fp4_qd_np(h)
    return x + (hq @ fp4_qd_np(wd.astype(np.float64)).T).astype(np.float32)



_PROGS = {}


def _get_prog(key, builder):
    if key not in _PROGS:
        pr = builder(T=2048)
        pr.finalize()
        _PROGS[key] = pr
    return _PROGS[key]


def _asc(a):
    return np.ascontiguousarray(np.asarray(a))


def _host_layouts(inp):
    W = {}
    W['l0a'] = {'qw': _asc(inp['L0_attn_a_Wq'].T), 'kw': _asc(inp['L0_attn_a_Wk'].T),
                'vw': _asc(inp['L0_attn_a_Wv'].T), 'ow': _asc(inp['L0_attn_a_Wo'].T),
                'qb': col_of(_asc(inp['L0_attn_a_bq'])), 'kb': col_of(_asc(inp['L0_attn_a_bk'])),
                'vb': col_of(_asc(inp['L0_attn_a_bv'])), 'ob': col_of(_asc(inp['L0_attn_a_bo']))}
    keys = _asc(inp['L0_ffn_keys'])
    W['l0f'] = {'wg': _asc(inp['L0_ffn_Wg']), 'wu': _asc(inp['L0_ffn_Wu']),
                'wd': _asc(inp['L0_ffn_Wd']), 'bgc': col_of(_asc(inp['L0_ffn_bg'])),
                'buc': col_of(_asc(inp['L0_ffn_bu'])), 'bdc': col_of(_asc(inp['L0_ffn_bd'])),
                'wq': _asc(inp['L0_ffn_Wq']),
                'keysx': np.concatenate([keys[h, :, 0, :].T for h in range(4)], 0).copy(),
                'keysy': np.concatenate([keys[h, :, 1, :].T for h in range(4)], 0).copy(),
                'det': _asc(inp['L0_ffn_down_embed'].T), 'ue': _asc(inp['L0_ffn_up_embed'])}
    W['l0b'] = {'wq': _asc(inp['L0_attn_b_Wq']), 'wk': _asc(inp['L0_attn_b_Wk']),
                'wv': _asc(inp['L0_attn_b_Wv']), 'wr': _asc(inp['L0_attn_b_Wr']),
                'wo': _asc(inp['L0_attn_b_Wo']), 'bo': col_of(_asc(inp['L0_attn_b_bo'])),
                'gcol': col_of(_asc(inp['L0_attn_b_gamma'])),
                'bcol': col_of(_asc(inp['L0_attn_b_beta'])),
                'ucol': col_of(_asc(inp['L0_attn_b_u_bias']).reshape(-1)),
                'vcol': col_of(_asc(inp['L0_attn_b_v_bias']).reshape(-1)),
                'rpe': _asc(inp['L0_attn_b_rel_pos_emb'])}
    wqkv = _asc(inp['L1_attn_a_W_qkv']).reshape(1024, 3, 1024)
    W['l1a'] = {'wqn': _asc(wqkv[:, 0]), 'wkn': _asc(wqkv[:, 1]), 'wv': _asc(wqkv[:, 2]),
                'wp': _asc(inp['L1_attn_a_W_proj']),
                'bp': col_of(_asc(inp['L1_attn_a_b_proj'])),
                'lq1': _asc(inp['L1_attn_a_lambda_q1']).reshape(1, 32),
                'lk1': _asc(inp['L1_attn_a_lambda_k1']).reshape(1, 32),
                'lq2': _asc(inp['L1_attn_a_lambda_q2']).reshape(1, 32),
                'lk2': _asc(inp['L1_attn_a_lambda_k2']).reshape(1, 32),
                'swc': np.tile(_asc(inp['L1_attn_a_subln_w']), 2).reshape(128, 1).copy()}
    wqkre = np.zeros((1024, 1024), np.float32)
    wqr = _asc(inp['L1_attn_b_w_qk_red'])
    for h in range(8):
        wqkre[:, h * 64:h * 64 + 32] = wqr[:, h * 32:(h + 1) * 32]
        wqkre[:, 512 + h * 64:512 + h * 64 + 32] = wqr[:, 256 + h * 32:256 + (h + 1) * 32]
    W['l1b'] = {'wqkvf': _asc(inp['L1_attn_b_w_qkv_full']), 'wqkre': wqkre,
                'wvr': _asc(inp['L1_attn_b_w_v_red']), 'wp': _asc(inp['L1_attn_b_w_proj'])}
    W['l1f'] = {'wg': _asc(inp['L1_ffn_w_gate'].T), 'wu': _asc(inp['L1_ffn_w_up'].T),
                'wd': _asc(inp['L1_ffn_w_down'].T)}
    W['l2a'] = {'W': _asc(inp['L2_attn_a_W'].T),
                'lgam': col_of(_asc(inp['L2_attn_a_ln_gamma'])),
                'lbet': col_of(_asc(inp['L2_attn_a_ln_beta'])),
                'eta': _asc(inp['L2_attn_a_eta']).reshape(1, 1),
                'llog': _asc(inp['L2_attn_a_lam_logit']).reshape(1, 1)}
    W['l2b'] = make_l2b_inputs(None, _asc(inp['L2_attn_b_norm_w']),
                               _asc(inp['L2_attn_b_norm_b']),
                               _asc(inp['L2_attn_b_in_proj_w']),
                               _asc(inp['L2_attn_b_conv_w']),
                               _asc(inp['L2_attn_b_conv_b']),
                               _asc(inp['L2_attn_b_x_proj_w']),
                               _asc(inp['L2_attn_b_dt_proj_w']),
                               _asc(inp['L2_attn_b_dt_proj_b']),
                               _asc(inp['L2_attn_b_A_log']),
                               _asc(inp['L2_attn_b_D_skip']),
                               _asc(inp['L2_attn_b_out_proj_w']),
                               _asc(inp['L2_attn_b_normf_w']),
                               _asc(inp['L2_attn_b_normf_b']))
    del W['l2b']['x']
    W['l2f'] = {'gw': _asc(inp['L2_ffn_gate_w'].T),
                'gb': _asc(inp['L2_ffn_gate_b']).reshape(1, 8),
                'w1': _asc(inp['L2_ffn_w1']).reshape(8 * 1024, 4096),
                'b1c': np.concatenate([col_of(_asc(inp['L2_ffn_b1'][e])) for e in range(8)], 0),
                'w2': _asc(inp['L2_ffn_w2']).reshape(8 * 4096, 1024),
                'b2c': np.concatenate([col_of(_asc(inp['L2_ffn_b2'][e])) for e in range(8)], 0)}
    W['l3a'] = {'qw': _asc(inp['L3_attn_a_q_w'].T), 'kw': _asc(inp['L3_attn_a_k_w'].T),
                'vw': _asc(inp['L3_attn_a_v_w'].T), 'ow': _asc(inp['L3_attn_a_out_w'].T),
                'qb': col_of(_asc(inp['L3_attn_a_q_b'])), 'kb': col_of(_asc(inp['L3_attn_a_k_b'])),
                'vb': col_of(_asc(inp['L3_attn_a_v_b'])), 'ob': col_of(_asc(inp['L3_attn_a_out_b']))}
    W['l3b'] = {'W': _asc(inp['L3_attn_b_W'].T), 'b': col_of(_asc(inp['L3_attn_b_b']))}
    W['l3f'] = {'wg': _asc(inp['L3_ffn_w_gate']), 'wu': _asc(inp['L3_ffn_w_up']),
                'wd': _asc(inp['L3_ffn_w_down'])}
    return W


_BUILDERS = {'l0a': build_l0_attn_a, 'l0f': build_l0_ffn, 'l0b': build_l0_attn_b,
             'l1a': build_l1_attn_a, 'l1b': build_l1_attn_b, 'l1f': build_l1_ffn,
             'l2a': build_l2_attn_a, 'l2b': build_l2_attn_b, 'l2f': build_l2_ffn,
             'l3a': build_l3_attn_a, 'l3b': build_l3_attn_b, 'l3f': build_l3_ffn}

_SEQ = ['l0a', 'l0f', 'l0b', 'l0f',
        'l1a', 'l1f', 'l1b', 'l1f',
        'l2a', 'l2f', 'l2b', 'l2f',
        'l3a', 'l3f', 'l3b', 'l3f']


def kernel(**inputs):
    from concourse.bass_utils import run_bass_kernel_spmd
    x = np.asarray(inputs['L0_attn_a_x']).astype(np.float32)
    B = x.shape[0]
    W = _host_layouts(inputs)
    xs = [np.ascontiguousarray(x[b]) for b in range(B)]
    for key in _SEQ:
        pr = _get_prog(key, _BUILDERS[key])
        wmap = W[key]
        in_maps = [dict(wmap, **pr.extra_inputs, x=xs[b]) for b in range(B)]
        res = run_bass_kernel_spmd(pr.nc, in_maps, core_ids=list(range(B)))
        xs = [np.asarray(res.results[b]['y']) for b in range(B)]
    return np.stack(xs).astype(np.float32)


# version 06
# speedup vs baseline: 1.2016x; 1.2016x over previous
import numpy as np

"""Raw-bass program-building framework with automatic semaphore edges.

Instructions are emitted EAGERLY through Prog.op / Prog.dma:
- eng in {'pe','dve','act','pool','sp'}; DMA queues {'sp','act','pool'}.
- reads/writes are resource keys: either a tensor handle or (handle, tag).
- The builder tracks last-writer/readers per resource and emits wait_ge
  edges (same-engine included), deduped with per-engine vector clocks.
"""
import contextlib
import concourse.bass as bass
import concourse.mybir as mybir
from concourse.alu_op_type import AluOpType as OP

F32 = mybir.dt.float32
BF16 = mybir.dt.bfloat16
I32 = mybir.dt.int32
U32 = mybir.dt.uint32
I16 = mybir.dt.int16
U16 = mybir.dt.uint16
U8 = mybir.dt.uint8
AF = mybir.ActivationFunctionType
AX = mybir.AxisListType

ENGS = ('pe', 'dve', 'act', 'pool', 'sp')
ENG_MAP = {'pe': 'tensor', 'dve': 'vector', 'act': 'scalar',
           'pool': 'gpsimd', 'sp': 'sync'}
DMAQ = {'sp': 'dsp', 'act': 'dact', 'pool': 'dpool', 'dve': 'ddve'}
NDMAG = 4
DMA_SEMS = [f"{q}{g}" for q in DMAQ.values() for g in range(NDMAG)]


class _Res:
    __slots__ = ('whole_w', 'whole_r', 'tag_w', 'tag_r')

    def __init__(self):
        self.whole_w = None
        self.whole_r = {}
        self.tag_w = {}
        self.tag_r = {}


class _CallTok:
    __slots__ = ('i',)


class _Rec:
    __slots__ = ('calls',)

    def __init__(self):
        object.__setattr__(self, 'calls', [])

    def __getattr__(self, name):
        calls = self.calls

        def m(*a, **k):
            t = _CallTok()
            t.i = len(calls)
            calls.append((name, a, k))
            return t
        return m


class Prog:
    def __init__(self, name="prog"):
        self.nc = bass.Bass("TRN2", target_bir_lowering=False)
        self.stack = contextlib.ExitStack()
        self.stack.__enter__()
        self.extra_inputs = {}
        self.out_names = []
        self.counts = dict.fromkeys(list(ENGS) + DMA_SEMS, 0)
        self.dma_idx = dict.fromkeys(DMAQ.values(), 0)
        self.clock = {e: {} for e in ENGS}
        self.handles = {}
        self._names = set()
        self.scopes = {}
        self.default_scopes = []
        self._dram_cache = {}
        self.fence_need = {}
        self.sems = {s: self.stack.enter_context(self.nc.semaphore(s))
                     for s in list(ENGS) + DMA_SEMS}
        self.block = self.stack.enter_context(self.nc.Block())
        self.finalized = False
        self.n_ops = 0
        self.ir_by_eng = {e: [] for e in ENGS}
        self.waited = set()
        self.reached = {}

    def _uname(self, n):
        i = 0
        nn = n
        while nn in self._names:
            i += 1
            nn = f"{n}_{i}"
        self._names.add(nn)
        return nn

    def _scope_of(self, scope):
        if scope == 'ROOT':
            return None
        if scope is None and self.default_scopes:
            return self.default_scopes[-1]
        return scope

    def _side_of(self, scope):
        if scope is None or scope in self.default_scopes:
            return 'left'
        return 'right'

    def sbuf(self, name, shape, dt=F32, scope=None):
        scope = self._scope_of(scope)
        st = self.scopes[scope] if scope else self.stack
        side = self._side_of(scope)
        return st.enter_context(self.nc.sbuf_tensor(self._uname(name), list(shape), dt, side=side))

    def psum(self, name, shape, dt=F32, scope=None):
        scope = self._scope_of(scope)
        st = self.scopes[scope] if scope else self.stack
        side = self._side_of(scope)
        return st.enter_context(self.nc.psum_tensor(self._uname(name), list(shape), dt, side=side))

    def push_default_scope(self, name):
        self.open_scope(name)
        self.default_scopes.append(name)

    def pop_default_scope(self):
        name = self.default_scopes.pop()
        self.close_scope(name)

    def dram_in_cached(self, name, shape, dt=F32):
        if name not in self._dram_cache:
            self._dram_cache[name] = self.dram_in(name, shape, dt)
        return self._dram_cache[name]

    def open_scope(self, name):
        assert name not in self.scopes
        self.scopes[name] = contextlib.ExitStack()
        self.scopes[name].__enter__()

    def close_scope(self, name):
        self._flush()
        self.scopes.pop(name).__exit__(None, None, None)
        self.fence()

    def fence(self):
        self.fence_need = {s: c for s, c in self.counts.items() if c > 0}

    def dram_in(self, name, shape, dt=F32):
        return self.nc.dram_tensor(name, list(shape), dt, kind="ExternalInput").ap()

    def dram_out(self, name, shape, dt=F32):
        self.out_names.append(name)
        return self.nc.dram_tensor(name, list(shape), dt, kind="ExternalOutput").ap()

    def dram_tmp(self, name, shape, dt=F32):
        return self.nc.dram_tensor(self._uname(name), list(shape), dt, kind="Internal").ap()

    def const(self, name, arr, dt=F32):
        name = self._uname(name)
        npdt = {F32: np.float32, I32: np.int32, I16: np.int16,
                U32: np.uint32, U16: np.uint16, U8: np.uint8}.get(dt)
        if npdt is not None:
            arr = np.ascontiguousarray(np.asarray(arr).astype(npdt))
        self.extra_inputs[name] = arr
        return self.nc.dram_tensor(name, list(arr.shape), dt, kind="ExternalInput").ap()

    def _get_res(self, h):
        k = id(h)
        r = self.handles.get(k)
        if r is None:
            r = self.handles[k] = _Res()
        return r

    @staticmethod
    def _hk(r):
        return (r[0], r[1]) if isinstance(r, tuple) else (r, None)

    def _dep_read(self, r, toks):
        h, tag = self._hk(r)
        res = self.handles.get(id(h))
        if res is None:
            return
        if res.whole_w:
            toks.append(res.whole_w)
        if tag is not None:
            t = res.tag_w.get(tag)
            if t:
                toks.append(t)
        else:
            toks.extend(res.tag_w.values())

    def _dep_write(self, r, toks):
        h, tag = self._hk(r)
        res = self.handles.get(id(h))
        if res is None:
            return
        if res.whole_w:
            toks.append(res.whole_w)
        toks.extend(res.whole_r.items())
        if tag is not None:
            t = res.tag_w.get(tag)
            if t:
                toks.append(t)
            toks.extend(res.tag_r.get(tag, {}).items())
        else:
            toks.extend(res.tag_w.values())
            for d in res.tag_r.values():
                toks.extend(d.items())

    def _note_read(self, r, tok):
        h, tag = self._hk(r)
        res = self._get_res(h)
        sname, val = tok
        if tag is None:
            if res.whole_r.get(sname, 0) < val:
                res.whole_r[sname] = val
        else:
            d = res.tag_r.setdefault(tag, {})
            if d.get(sname, 0) < val:
                d[sname] = val

    def _note_write(self, r, tok):
        h, tag = self._hk(r)
        res = self._get_res(h)
        if tag is None:
            res.whole_w = tok
            res.whole_r = {}
            res.tag_w = {}
            res.tag_r = {}
        else:
            res.tag_w[tag] = tok
            res.tag_r[tag] = {}

    def _emit(self, eng, sem_name, inc, fn, reads, writes, self_serialize=False):
        toks = []
        if self_serialize and self.counts[sem_name] > 0:
            toks.append((sem_name, self.counts[sem_name]))
        for r in reads:
            self._dep_read(r, toks)
        for w in writes:
            self._dep_write(w, toks)
        need = {}
        for (sname, val) in toks:
            if need.get(sname, -1) < val:
                need[sname] = val
        for sname, val in self.fence_need.items():
            if need.get(sname, -1) < val:
                need[sname] = val
        clk = self.clock[eng]
        waits = []
        for sname, val in need.items():
            if clk.get(sname, 0) >= val:
                continue
            clk[sname] = val
            waits.append((sname, val))
        self.counts[sem_name] += inc
        tok = (sem_name, self.counts[sem_name])
        rec = _Rec()
        r = fn(rec)
        if not isinstance(r, _CallTok):
            raise RuntimeError(f"op fn must return an engine-method result ({eng})")
        for (sname, val) in waits:
            self.waited.add((sname, val))
        self.ir_by_eng[eng].append((waits, rec.calls, sem_name, tok[1], inc, r.i))
        self.n_ops += 1
        for r in reads:
            self._note_read(r, tok)
        for w in writes:
            self._note_write(w, tok)
        return tok

    def op(self, eng, fn, reads=(), writes=()):
        assert eng in ('pe', 'dve', 'act', 'pool')
        return self._emit(eng, eng, 1, fn, reads, writes)

    def dma(self, queue, fn, reads=(), writes=()):
        assert queue in ('sp', 'act', 'pool', 'dve')
        base = DMAQ[queue]
        g = self.dma_idx[base] % NDMAG
        self.dma_idx[base] += 1
        return self._emit(queue, f"{base}{g}", 16, fn, reads, writes,
                          self_serialize=True)

    def raw(self, eng, fn):
        rec = _Rec()
        fn(rec)
        self.ir_by_eng[eng].append(([], rec.calls, None, 0, 0, -1))

    SECT = 64

    def _flush(self):
        sems = self.sems
        for eng in ENGS:
            nodes = self.ir_by_eng[eng]
            if not nodes:
                continue
            plan = []
            for (waits, calls, sname, cnt, inc, inc_idx) in nodes:
                amt = 0
                if inc == 16:
                    amt = 16
                elif inc and ((sname, cnt) in self.waited
                              or cnt == self.counts[sname]):
                    amt = cnt - self.reached.get(sname, 0)
                if amt:
                    self.reached[sname] = cnt
                plan.append((waits, calls, sname, amt, inc_idx))
            section = getattr(self.block, ENG_MAP[eng])
            for i0 in range(0, len(plan), self.SECT):
                batch = plan[i0:i0 + self.SECT]

                def body(e, batch=batch):
                    for (waits, calls, sname, amt, inc_idx) in batch:
                        for (s, v) in waits:
                            e.wait_ge(sems[s], v)
                        inc_instr = None
                        for ci, (name, a, k) in enumerate(calls):
                            instr = getattr(e, name)(*a, **k)
                            if ci == inc_idx:
                                inc_instr = instr
                        if amt:
                            inc_instr.then_inc(sems[sname], amt)
                section(body)
            self.ir_by_eng[eng] = []

    def finalize(self):
        assert not self.finalized
        self.finalized = True
        self._flush()
        sems = self.sems
        counts = dict(self.counts)

        @self.block.sync
        def _(sync):
            for sname, cnt in counts.items():
                if cnt > 0 and sname != 'sp':
                    sync.wait_ge(sems[sname], cnt)
        self.stack.__exit__(None, None, None)
        return self.nc

    def run(self, per_core_inputs, n_cores=8, trace=False):
        from concourse.bass_utils import run_bass_kernel_spmd
        if not self.finalized:
            self.finalize()
        in_maps = []
        for c in range(n_cores):
            m = dict(per_core_inputs[c])
            m.update(self.extra_inputs)
            in_maps.append(m)
        return run_bass_kernel_spmd(self.nc, in_maps, core_ids=list(range(n_cores)), trace=trace)

    def run_sim(self, inputs_core0):
        from concourse.bass_interp import MultiCoreSim
        if not self.finalized:
            self.finalize()
        sim = MultiCoreSim(self.nc, num_cores=1)
        core = sim.cores[0]
        m = dict(inputs_core0)
        m.update(self.extra_inputs)
        for k, v in m.items():
            t = core.tensor(k)
            t[...] = np.asarray(v).reshape(t.shape)
        sim.simulate()
        return {name: core.tensor(name).copy() for name in self.out_names}


def ceil_div(a, b):
    return (a + b - 1) // b


class Ctx:
    def __init__(self, p: Prog):
        self.p = p
        self.ident = {}
        self.ones_col = None

    def get_ident(self, dt=BF16):
        if dt not in self.ident:
            t = self.p.sbuf('ident', [128, 128], dt, scope='ROOT')
            ap = self.p.const("ident_c", np.eye(128, dtype=np.float32), F32)
            if dt == F32:
                self.p.dma('sp', lambda e: e.dma_start(out=t[:], in_=ap[:, :]), writes=[t])
            else:
                self.p.dma('pool', lambda e: e.dma_start(out=t[:], in_=ap[:, :]), writes=[t])
            self.ident[dt] = t
        return self.ident[dt]

    def get_ones_col(self, dt=F32):
        if self.ones_col is None:
            self.ones_col = {}
        if dt not in self.ones_col:
            t = self.p.sbuf('ones_col', [128, 1], dt, scope='ROOT')
            self.p.op('pool', lambda e: e.memset(t[:], 1.0), writes=[t])
            self.ones_col[dt] = t
        return self.ones_col[dt]

    def get_ident_block(self, hd, dt=BF16):
        key = ('blk', hd, dt)
        if key not in self.ident:
            t = self.p.sbuf(f'identb{hd}', [128, hd], dt, scope='ROOT')
            arr = np.tile(np.eye(hd, dtype=np.float32), (128 // hd, 1))
            ap = self.p.const(f"identb{hd}_c", arr, F32)
            self.p.dma('pool', lambda e: e.dma_start(out=t[:], in_=ap[:, :]), writes=[t])
            self.ident[key] = t
        return self.ident[key]

    def cval(self, val):
        if not hasattr(self, '_cvals'):
            self._cvals = {}
        if val not in self._cvals:
            t = self.p.sbuf('cv', [128, 1], F32, scope='ROOT')
            self.p.op('pool', lambda e: e.memset(t[:], float(val)), writes=[t])
            self._cvals[val] = t
        return self._cvals[val]

"""Common sublayer scaffolding: input staging + LN, streamed matmul, output path."""



def col_of(vec_np):
    n = vec_np.shape[-1]
    return np.ascontiguousarray(vec_np.reshape(n // 128, 128).T)


def load_col_vec(p: Prog, name, dram_ap, n, dt=F32):
    nc_ = ceil_div(n, 128)
    t = p.sbuf(name, [128, nc_], dt)
    p.dma('sp', lambda e: e.dma_start(out=t[:], in_=dram_ap[:, :]), writes=[t])
    return t


def stage_input_ln(p: Prog, ctx: Ctx, x_dram, T=2048, D=1024, keep_f32=False,
                   keep_x_fm=False, eps=1e-5, f32_scope=None, gb=None, eps_inner=None):
    nd = D // 128
    nt = T // 128
    ln_bf = [p.sbuf(f"lnbf{c}", [128, T], BF16) for c in range(nd)]
    ln_f32 = [p.sbuf(f"lnf{c}", [128, T], F32, scope=f32_scope) for c in range(nd)] if keep_f32 else None
    x_fm = [p.sbuf(f"xfm{c}", [128, T], F32) for c in range(nd)] if keep_x_fm else None

    ident = ctx.get_ident(F32)
    ones = ctx.get_ones_col()
    SCP = p._uname('_lnstage')
    p.open_scope(SCP)
    xt_tm = [p.sbuf(f"xt_tm{i}", [128, D], F32, scope=SCP) for i in range(2)]
    xw = [p.sbuf(f"xw{i}", [128, 128], F32, scope=SCP) for i in range(3)]
    sq = p.sbuf("sq", [128, 128], F32, scope=SCP)
    tp_ps = [p.psum(f"tp_ps{i}", [128, 128], F32, scope=SCP) for i in range(2)]
    sx_ps = p.psum("sx_ps", [128, 512], F32, scope=SCP)
    sq_ps = p.psum("sq_ps", [128, 512], F32, scope=SCP)
    bc_ps = [p.psum(f"bc_ps{i}", [128, 128], F32, scope=SCP) for i in range(2)]
    rowbuf = p.sbuf("rowbuf", [128, 512], F32, scope=SCP)
    RA, RB, RC, RD = (slice(0, 128), slice(128, 256), slice(256, 384), slice(384, 512))

    for ti in range(nt):
        tmt = xt_tm[ti % 2]
        p.dma('sp', lambda e: e.dma_start(out=tmt[:], in_=x_dram[ti * 128:(ti + 1) * 128, :]),
              reads=[x_dram], writes=[tmt])

        def get_fm(c, dest_w):
            ps = tp_ps[c % 2]
            p.op('pe', lambda e: e.transpose(ps[:], tmt[:, c * 128:(c + 1) * 128], ident[:]),
                 reads=[tmt, ident], writes=[ps])
            p.op('dve', lambda e: e.tensor_copy(dest_w[:], ps[:]),
                 reads=[ps], writes=[dest_w])

        for c in range(nd):
            w = xw[c % 2]
            get_fm(c, w)
            if keep_x_fm:
                p.op('act', lambda e: e.copy(x_fm[c][:, ti * 128:(ti + 1) * 128], w[:]),
                     reads=[w], writes=[(x_fm[c], ti)])
            p.op('pe', lambda e: e.matmul(sx_ps[0:1, 0:128], ones[:], w[:],
                                          start=(c == 0), stop=(c == nd - 1)),
                 reads=[w, ones], writes=[(sx_ps, 'acc')])
            p.op('act', lambda e: e.activation(sq[:], w[:], AF.Square),
                 reads=[w], writes=[sq])
            p.op('pe', lambda e: e.matmul(sq_ps[0:1, 0:128], ones[:], sq[:],
                                          start=(c == 0), stop=(c == nd - 1)),
                 reads=[sq, ones], writes=[(sq_ps, 'acc')])
        p.op('dve', lambda e: e.tensor_scalar(rowbuf[0:1, RC], sx_ps[0:1, 0:128],
                                              1.0 / D, None, OP.mult, OP.bypass),
             reads=[(sx_ps, 'acc')], writes=[(rowbuf, 'C')])
        p.op('dve', lambda e: e.tensor_scalar(rowbuf[0:1, RD], sq_ps[0:1, 0:128],
                                              1.0 / D, None, OP.mult, OP.bypass),
             reads=[(sq_ps, 'acc')], writes=[(rowbuf, 'D')])
        p.op('dve', lambda e: e.tensor_tensor(rowbuf[0:1, RA], rowbuf[0:1, RC],
                                              rowbuf[0:1, RC], OP.mult),
             reads=[(rowbuf, 'C')], writes=[(rowbuf, 'A')])
        p.op('dve', lambda e: e.tensor_tensor(rowbuf[0:1, RB], rowbuf[0:1, RD],
                                              rowbuf[0:1, RA], OP.subtract),
             reads=[(rowbuf, 'D'), (rowbuf, 'A')], writes=[(rowbuf, 'B')])
        epsb = ctx.cval(eps if eps_inner is None else eps_inner * eps)
        if eps_inner is None:
            p.op('act', lambda e: e.activation(rowbuf[0:1, RA], rowbuf[0:1, RB], AF.Ln,
                                               bias=epsb[0:1, :]),
                 reads=[(rowbuf, 'B'), epsb], writes=[(rowbuf, 'A')])
        else:
            p.op('act', lambda e: e.activation(rowbuf[0:1, RA], rowbuf[0:1, RB], AF.Ln,
                                               bias=epsb[0:1, :], scale=1.0 + eps_inner),
                 reads=[(rowbuf, 'B'), epsb], writes=[(rowbuf, 'A')])
        p.op('act', lambda e: e.activation(rowbuf[0:1, RA], rowbuf[0:1, RA], AF.Exp, scale=-0.5),
             reads=[(rowbuf, 'A')], writes=[(rowbuf, 'A')])
        p.op('dve', lambda e: e.scalar_tensor_tensor(rowbuf[0:1, RB], rowbuf[0:1, RC],
                                                     -1.0, rowbuf[0:1, RA], OP.mult, OP.mult),
             reads=[(rowbuf, 'C'), (rowbuf, 'A')], writes=[(rowbuf, 'B')])
        p.op('pe', lambda e: e.matmul(bc_ps[0][:, 0:128], ones[0:1, :].broadcast_to((1, 128)),
                                      rowbuf[0:1, RA], start=True, stop=True),
             reads=[(rowbuf, 'A'), ones], writes=[bc_ps[0]])
        p.op('pe', lambda e: e.matmul(bc_ps[1][:, 0:128], ones[0:1, :].broadcast_to((1, 128)),
                                      rowbuf[0:1, RB], start=True, stop=True),
             reads=[(rowbuf, 'B'), ones], writes=[bc_ps[1]])
        for c in range(nd):
            if keep_x_fm:
                src_ap = x_fm[c][:, ti * 128:(ti + 1) * 128]
                rtag = [(x_fm[c], ti)]
            else:
                w = xw[c % 2]
                get_fm(c, w)
                src_ap = w[:]
                rtag = [w]
            tmp = xw[2]
            p.op('dve', lambda e: e.tensor_tensor(tmp[:], src_ap, bc_ps[0][:, 0:128], OP.mult),
                 reads=rtag + [bc_ps[0]], writes=[tmp])
            if keep_f32:
                p.op('dve', lambda e: e.tensor_tensor(
                    ln_f32[c][:, ti * 128:(ti + 1) * 128], tmp[:], bc_ps[1][:, 0:128], OP.add),
                    reads=[tmp, bc_ps[1]], writes=[(ln_f32[c], ti)])
                p.op('act', lambda e: e.copy(ln_bf[c][:, ti * 128:(ti + 1) * 128],
                                             ln_f32[c][:, ti * 128:(ti + 1) * 128]),
                     reads=[(ln_f32[c], ti)], writes=[(ln_bf[c], ti)])
            elif gb is not None:
                gcol, bcol = gb
                tmp2 = xw[0] if keep_x_fm else xw[(c + 1) % 2]
                p.op('dve', lambda e: e.tensor_tensor(tmp2[:], tmp[:], bc_ps[1][:, 0:128],
                                                      OP.add),
                     reads=[tmp, bc_ps[1]], writes=[tmp2])
                p.op('dve', lambda e: e.tensor_scalar(
                    ln_bf[c][:, ti * 128:(ti + 1) * 128], tmp2[:], gcol[:, c:c + 1],
                    bcol[:, c:c + 1], OP.mult, OP.add),
                    reads=[tmp2, gcol, bcol], writes=[(ln_bf[c], ti)])
            else:
                p.op('dve', lambda e: e.tensor_tensor(
                    ln_bf[c][:, ti * 128:(ti + 1) * 128], tmp[:], bc_ps[1][:, 0:128], OP.add),
                    reads=[tmp, bc_ps[1]], writes=[(ln_bf[c], ti)])
    p.close_scope(SCP)
    return dict(ln_bf=ln_bf, ln_f32=ln_f32, x_fm=x_fm)


class WStream:

    def __init__(self, p: Prog, name, dram_ap, nbuf=3, slice_w=128, dt=BF16):
        self.p = p
        self.dram = dram_ap
        self.bufs = [p.sbuf(f"{name}_b{i}", [128, slice_w], dt) for i in range(nbuf)]
        self.i = 0

    def fetch(self, r0, c0, rows=128, cols=128, queue='pool'):
        b = self.bufs[self.i % len(self.bufs)]
        self.i += 1
        self.p.dma(queue, lambda e: e.dma_start(
            out=b[0:rows, 0:cols], in_=self.dram[r0:r0 + rows, c0:c0 + cols]),
            writes=[b])
        return b


def stream_matmul(p: Prog, name, w_dram, din, dout, x_chunks, T, epilogue,
                  x_tag_whole=True, queue='pool', psums=None, nbuf=3,
                  x_slice_of=None):
    nk = ceil_div(din, 128)
    nm = ceil_div(dout, 128)
    ws = WStream(p, name + "_w", w_dram, nbuf=nbuf)
    if psums is None:
        psums = [p.psum(f"{name}_ps{i}", [128, min(T, 2048)], F32) for i in range(2)]
    NS = 512
    for mc in range(nm):
        ps = psums[mc % len(psums)]
        for kc in range(nk):
            rows = min(128, din - kc * 128)
            cols = min(128, dout - mc * 128)
            wb = ws.fetch(kc * 128, mc * 128, rows, cols, queue=queue)
            xt = x_chunks[kc]
            xr = xt if x_tag_whole else (xt, 'all')
            for n0 in range(0, T, NS):
                nn = min(NS, T - n0)
                xap = xt[0:rows, n0:n0 + nn] if x_slice_of is None else x_slice_of(kc, rows, n0, nn)
                p.op('pe', lambda e: e.matmul(
                    ps[0:cols, n0:n0 + nn], wb[0:rows, 0:cols], xap,
                    start=(kc == 0), stop=(kc == nk - 1)),
                    reads=[wb, xr], writes=[(ps, n0)])
        epilogue(mc, ps)


def store_residual_out(p: Prog, ctx: Ctx, f_fm, x_dram, y_dram, T=2048, D=1024, f_dt=F32):
    nd = D // 128
    nt = T // 128
    ident = ctx.get_ident(f_dt)
    SCP = p._uname('_store')
    p.open_scope(SCP)
    xt = [p.sbuf(f"sr_x{i}", [128, D], F32, scope=SCP) for i in range(2)]
    ot = [p.sbuf(f"sr_o{i}", [128, D], F32, scope=SCP) for i in range(2)]
    tps = [p.psum(f"sr_ps{i}", [128, 128], f_dt, scope=SCP) for i in range(2)]
    for ti in range(nt):
        x_t = xt[ti % 2]
        o_t = ot[ti % 2]
        p.dma('sp', lambda e: e.dma_start(out=x_t[:], in_=x_dram[ti * 128:(ti + 1) * 128, :]),
              reads=[x_dram], writes=[x_t])
        for c in range(nd):
            ps = tps[c % 2]
            p.op('pe', lambda e: e.transpose(
                ps[:], f_fm[c][:, ti * 128:(ti + 1) * 128], ident[:]),
                reads=[(f_fm[c], ti), ident], writes=[ps])
            p.op('dve', lambda e: e.tensor_tensor(
                o_t[:, c * 128:(c + 1) * 128], ps[:], x_t[:, c * 128:(c + 1) * 128], OP.add),
                reads=[ps, x_t], writes=[(o_t, c)])
        p.dma('sp', lambda e: e.dma_start(out=y_dram[ti * 128:(ti + 1) * 128, :], in_=o_t[:]),
              reads=[(o_t, c) for c in range(nd)], writes=[(y_dram, ti)])
    p.close_scope(SCP)


def proj_fm(p: Prog, name, w_dram, x_chunks, T, din, dout, bias_col=None,
            out_dt=BF16, act=None, act_scale=1.0, queue='pool', out_chunks=None,
            fetch_w=512, x_reads=None, epilogue=None, bias_off=0, x_off=0):
    nk = ceil_div(din, 128)
    nm = ceil_div(dout, 128)
    if out_chunks is None and epilogue is None:
        out_chunks = [p.sbuf(f"{name}_o{mc}", [128, T], out_dt) for mc in range(nm)]
    SCP = p._uname(f'_proj_{name}')
    p.open_scope(SCP)
    bw = fetch_w // 128
    wstrip = [[p.sbuf(f"{name}_w{b}_{kc}", [128, fetch_w], BF16, scope=SCP)
               for kc in range(nk)] for b in range(2)]
    psums = [p.psum(f"{name}_ps{i}", [128, min(T, 2048)], F32, scope=SCP) for i in range(2)]
    NS = 512
    mc = 0
    blki = 0
    while mc < nm:
        blk = min(bw, nm - mc)
        ws = wstrip[blki % 2]
        blki += 1
        for kc in range(nk):
            p.dma(queue, lambda e: e.dma_start(
                out=ws[kc][0:128, 0:blk * 128],
                in_=w_dram[kc * 128:(kc + 1) * 128, mc * 128:(mc + blk) * 128]),
                writes=[ws[kc]])
        for j in range(blk):
            ps = psums[(mc + j) % 2]
            for kc in range(nk):
                xr = [x_chunks[kc]] if x_reads is None else x_reads(kc)
                for n0 in range(0, T, NS):
                    nn = min(NS, T - n0)
                    p.op('pe', lambda e: e.matmul(
                        ps[:, n0:n0 + nn], ws[kc][:, j * 128:(j + 1) * 128],
                        x_chunks[kc][:, x_off + n0:x_off + n0 + nn],
                        start=(kc == 0), stop=(kc == nk - 1)),
                        reads=[ws[kc]] + xr, writes=[(ps, n0)])
            ptags = [(ps, n0) for n0 in range(0, T, NS)]
            mcj = mc + j
            if epilogue is not None:
                epilogue(mcj, ps, ptags)
            elif act is not None:
                if bias_col is not None:
                    p.op('act', lambda e: e.activation(
                        out_chunks[mcj][:], ps[:, 0:T], act, scale=act_scale,
                        bias=bias_col[:, bias_off + mcj:bias_off + mcj + 1]),
                        reads=ptags + [bias_col], writes=[out_chunks[mcj]])
                else:
                    p.op('act', lambda e: e.activation(
                        out_chunks[mcj][:], ps[:, 0:T], act, scale=act_scale),
                        reads=ptags, writes=[out_chunks[mcj]])
            elif bias_col is not None:
                p.op('dve', lambda e: e.tensor_scalar(
                    out_chunks[mcj][:], ps[:, 0:T], bias_col[:, bias_off + mcj:bias_off + mcj + 1], None,
                    OP.add, OP.bypass),
                    reads=ptags + [bias_col], writes=[out_chunks[mcj]])
            else:
                p.op('dve', lambda e: e.tensor_copy(out_chunks[mcj][:], ps[:, 0:T]),
                     reads=ptags, writes=[out_chunks[mcj]])
        mc += blk
    p.close_scope(SCP)
    return out_chunks


def ln_fm_sbuf(p, ctx, src, out_chunks, T, D, gcol=None, bcol=None, eps=1e-5,
               src_reads=None, ti0=0, ti1=None, out_off0=False):
    ones = ctx.get_ones_col()
    onesb = ctx.get_ones_col(BF16)
    SCP = p._uname('_lnfm')
    p.open_scope(SCP)
    nd = D // 128
    nt = T // 128
    if ti1 is None:
        ti1 = nt
    sq = p.sbuf("sq", [128, 128], F32, scope=SCP)
    sx_ps = p.psum("sx_ps", [128, 512], F32, scope=SCP)
    sq_ps = p.psum("sq_ps", [128, 512], F32, scope=SCP)
    bc_ps = [p.psum(f"bc_ps{i}", [128, 128], F32, scope=SCP) for i in range(2)]
    rowbuf = p.sbuf("rowbuf", [128, 512], F32, scope=SCP)
    tmp = p.sbuf("tmp", [128, 128], F32, scope=SCP)
    RA, RB, RC, RD = (slice(0, 128), slice(128, 256), slice(256, 384), slice(384, 512))
    epsb = ctx.cval(eps)
    for ti in range(ti0, ti1):
        for c in range(nd):
            sl = src[c][:, ti * 128:(ti + 1) * 128]
            rd = [src[c]] if src_reads is None else src_reads(c, ti)
            p.op('pe', (lambda sl_, c_: lambda e: e.matmul(
                sx_ps[0:1, 0:128], onesb[:], sl_, start=(c_ == 0), stop=(c_ == nd - 1)))(sl, c),
                reads=rd + [onesb], writes=[(sx_ps, 'a')])
            p.op('act', (lambda sl_: lambda e: e.activation(sq[:], sl_, AF.Square))(sl),
                 reads=rd, writes=[sq])
            p.op('pe', (lambda c_: lambda e: e.matmul(
                sq_ps[0:1, 0:128], ones[:], sq[:], start=(c_ == 0), stop=(c_ == nd - 1)))(c),
                reads=[sq, ones], writes=[(sq_ps, 'a')])
        p.op('dve', lambda e: e.tensor_scalar(rowbuf[0:1, RC], sx_ps[0:1, 0:128],
                                              1.0 / D, None, OP.mult, OP.bypass),
             reads=[(sx_ps, 'a')], writes=[(rowbuf, 'C')])
        p.op('dve', lambda e: e.tensor_scalar(rowbuf[0:1, RD], sq_ps[0:1, 0:128],
                                              1.0 / D, None, OP.mult, OP.bypass),
             reads=[(sq_ps, 'a')], writes=[(rowbuf, 'D')])
        p.op('dve', lambda e: e.tensor_tensor(rowbuf[0:1, RA], rowbuf[0:1, RC],
                                              rowbuf[0:1, RC], OP.mult),
             reads=[(rowbuf, 'C')], writes=[(rowbuf, 'A')])
        p.op('dve', lambda e: e.tensor_tensor(rowbuf[0:1, RB], rowbuf[0:1, RD],
                                              rowbuf[0:1, RA], OP.subtract),
             reads=[(rowbuf, 'D'), (rowbuf, 'A')], writes=[(rowbuf, 'B')])
        p.op('act', lambda e: e.activation(rowbuf[0:1, RA], rowbuf[0:1, RB], AF.Ln,
                                           bias=epsb[0:1, :]),
             reads=[(rowbuf, 'B'), epsb], writes=[(rowbuf, 'A')])
        p.op('act', lambda e: e.activation(rowbuf[0:1, RA], rowbuf[0:1, RA], AF.Exp,
                                           scale=-0.5),
             reads=[(rowbuf, 'A')], writes=[(rowbuf, 'A')])
        p.op('dve', lambda e: e.scalar_tensor_tensor(rowbuf[0:1, RB], rowbuf[0:1, RC],
                                                     -1.0, rowbuf[0:1, RA], OP.mult, OP.mult),
             reads=[(rowbuf, 'C'), (rowbuf, 'A')], writes=[(rowbuf, 'B')])
        p.op('pe', lambda e: e.matmul(bc_ps[0][:, 0:128], ones[0:1, :].broadcast_to((1, 128)),
                                      rowbuf[0:1, RA], start=True, stop=True),
             reads=[(rowbuf, 'A'), ones], writes=[bc_ps[0]])
        p.op('pe', lambda e: e.matmul(bc_ps[1][:, 0:128], ones[0:1, :].broadcast_to((1, 128)),
                                      rowbuf[0:1, RB], start=True, stop=True),
             reads=[(rowbuf, 'B'), ones], writes=[bc_ps[1]])
        for c in range(nd):
            sl = src[c][:, ti * 128:(ti + 1) * 128]
            rd = [src[c]] if src_reads is None else src_reads(c, ti)
            p.op('dve', (lambda sl_: lambda e: e.tensor_tensor(
                tmp[:], sl_, bc_ps[0][:, 0:128], OP.mult))(sl),
                reads=rd + [bc_ps[0]], writes=[tmp])
            p.op('dve', lambda e: e.tensor_tensor(tmp[:], tmp[:], bc_ps[1][:, 0:128], OP.add),
                 reads=[tmp, bc_ps[1]], writes=[tmp])
            to = (ti - ti0) if out_off0 else ti
            if gcol is not None:
                p.op('dve', (lambda c_, to_: lambda e: e.tensor_scalar(
                    out_chunks[c_][:, to_ * 128:(to_ + 1) * 128], tmp[:],
                    gcol[:, c_:c_ + 1], bcol[:, c_:c_ + 1], OP.mult, OP.add))(c, to),
                    reads=[tmp, gcol, bcol], writes=[(out_chunks[c], to)])
            else:
                p.op('act', (lambda c_, to_: lambda e: e.copy(
                    out_chunks[c_][:, to_ * 128:(to_ + 1) * 128], tmp[:]))(c, to),
                    reads=[tmp], writes=[(out_chunks[c], to)])
    p.close_scope(SCP)



"""Multi-head attention cores, feature-major layout."""



def attention_heads(p: Prog, ctx: Ctx, name, heads, get_q, get_k, get_v, put_out,
                    T=2048, hd_qk=64, hd_v=64, scale=1.0, kv_len=None,
                    kblock_range=None, mask_bias=None, prepare=None,
                    sps_holder=None):
    kv_len = kv_len or T
    ident = ctx.get_ident(BF16)
    identb = ctx.get_ident_block(hd_v, BF16)
    SCP = p._uname(f'_attn_{name}')
    p.open_scope(SCP)
    nq = T // 128
    nk = kv_len // 128

    vT = p.sbuf(f"{name}_vT", [128, nk * hd_v], BF16, scope=SCP)
    tps = p.psum(f"{name}_tps", [128, 128], BF16, scope=SCP)
    s_ps = p.psum(f"{name}_sps", [128, 2048], F32, scope=SCP)
    o_ps = p.psum(f"{name}_ops", [128, 128], F32, scope=SCP)
    pt_ps = p.psum(f"{name}_ptps", [128, 128], BF16, scope=SCP)
    pbf = p.sbuf(f"{name}_pbf", [128, 2048], BF16, scope=SCP)
    ptb = [p.sbuf(f"{name}_ptb{i}", [128, 128], BF16, scope=SCP) for i in range(2)]
    mrow = p.sbuf(f"{name}_mrow", [128, 8], F32, scope=SCP)
    if sps_holder is not None:
        sps_holder['sps'] = s_ps

    NS = 512
    for h in range(heads):
        if prepare is not None:
            prepare(h)
        qt, qr0, qrd = get_q(h)
        kt, kr0, krd = get_k(h)
        vt, vr0, vrd = get_v(h)
        for kc in range(nk):
            p.op('pe', lambda e: e.transpose(
                tps[0:128, 0:hd_v], vt[vr0:vr0 + hd_v, kc * 128:(kc + 1) * 128],
                identb[vr0:vr0 + hd_v, 0:hd_v]),
                reads=vrd + [identb], writes=[tps])
            p.op('dve', lambda e: e.tensor_copy(vT[:, kc * hd_v:(kc + 1) * hd_v],
                                                tps[0:128, 0:hd_v]),
                 reads=[tps], writes=[(vT, kc)])
        for ti in range(nq):
            kb0, kb1 = (0, nk) if kblock_range is None else kblock_range(ti)
            kw = (kb1 - kb0) * 128
            for n0 in range(kb0 * 128, kb1 * 128, NS):
                nn = min(NS, kb1 * 128 - n0)
                p.op('pe', lambda e: e.matmul(
                    s_ps[:, n0 - kb0 * 128:n0 - kb0 * 128 + nn],
                    qt[qr0:qr0 + hd_qk, ti * 128:(ti + 1) * 128],
                    kt[kr0:kr0 + hd_qk, n0:n0 + nn], start=True, stop=True),
                    reads=qrd + krd, writes=[(s_ps, n0)])
            sp_tags = [(s_ps, n0) for n0 in range(kb0 * 128, kb1 * 128, NS)]
            if mask_bias is not None:
                for kc in range(kb0, kb1):
                    m = mask_bias(ti, kc)
                    if m is None:
                        continue
                    ap_, hdl = m
                    c0 = (kc - kb0) * 128
                    p.op('dve', (lambda ap2, c0_: lambda e: e.tensor_tensor(
                        s_ps[:, c0_:c0_ + 128], s_ps[:, c0_:c0_ + 128], ap2, OP.add))(ap_, c0),
                        reads=sp_tags + [hdl], writes=sp_tags)
            p.op('dve', lambda e: e.tensor_reduce(mrow[:, 0:1], s_ps[:, 0:kw],
                                                  AX.X, OP.max),
                 reads=sp_tags, writes=[(mrow, 'm')])
            p.op('dve', lambda e: e.tensor_scalar(mrow[:, 1:2], mrow[:, 0:1],
                                                  -scale, None, OP.mult, OP.bypass),
                 reads=[(mrow, 'm')], writes=[(mrow, 'nm')])
            p.op('act', lambda e: e.activation(pbf[:, 0:kw], s_ps[:, 0:kw], AF.Exp,
                                               bias=mrow[:, 1:2], scale=scale,
                                               accum_out=mrow[:, 2:3]),
                 reads=sp_tags + [(mrow, 'nm')], writes=[pbf, (mrow, 's')])
            p.op('dve', lambda e: e.reciprocal(mrow[:, 3:4], mrow[:, 2:3]),
                 reads=[(mrow, 's')], writes=[(mrow, 'r')])
            p.op('dve', lambda e: e.tensor_scalar(pbf[:, 0:kw], pbf[:, 0:kw],
                                                  mrow[:, 3:4], None, OP.mult, OP.bypass),
                 reads=[pbf, (mrow, 'r')], writes=[pbf])
            nkc = kb1 - kb0
            for j, kc in enumerate(range(kb0, kb1)):
                pt = ptb[j % 2]
                p.op('pe', (lambda kc_: lambda e: e.transpose(
                    pt_ps[:], pbf[:, (kc_ - kb0) * 128:(kc_ - kb0 + 1) * 128], ident[:]))(kc),
                    reads=[pbf, ident], writes=[pt_ps])
                p.op('dve', (lambda pt_: lambda e: e.tensor_copy(pt_[:], pt_ps[:]))(pt),
                     reads=[pt_ps], writes=[pt])
                p.op('pe', (lambda kc_, pt_, j_: lambda e: e.matmul(
                    o_ps[0:hd_v, 0:128], vT[:, kc_ * hd_v:(kc_ + 1) * hd_v], pt_[:],
                    start=(j_ == 0), stop=(j_ == nkc - 1)))(kc, pt, j),
                    reads=[(vT, kc), pt], writes=[o_ps])
            put_out(h, ti, o_ps[0:hd_v, 0:128], [o_ps])
    p.close_scope(SCP)


def attention_full(p: Prog, ctx: Ctx, name, qf, kf, vf, out_chunks, T=2048,
                   heads=16, hd=64, scale=1.0, kv_len=None,
                   kblock_range=None, mask_bias=None):
    per_chunk = 128 // hd

    def mk(lst):
        def g(h):
            ch, r0 = h // per_chunk, (h % per_chunk) * hd
            return lst[ch], r0, [lst[ch]]
        return g

    def put_out(h, ti, ps_ap, reads):
        ch, r0 = h // per_chunk, (h % per_chunk) * hd
        p.op('dve', lambda e: e.tensor_copy(
            out_chunks[ch][r0:r0 + hd, ti * 128:(ti + 1) * 128], ps_ap),
            reads=reads, writes=[(out_chunks[ch], (h, ti))])

    attention_heads(p, ctx, name, heads, mk(qf), mk(kf), mk(vf), put_out,
                    T=T, hd_qk=hd, hd_v=hd, scale=scale, kv_len=kv_len,
                    kblock_range=kblock_range, mask_bias=mask_bias)

"""L0_attn_a: MHA with RoPE (16 heads, hd=64), biases on q/k/v/o, scale 1/8.
Weights host-transposed (in,out); biases col_of.
"""




def rope_tables(T, hd=64):
    inv_freq = 1.0 / 10000.0 ** (np.arange(0, hd, 2, dtype=np.float32) / hd)
    t = np.arange(T, dtype=np.float32)[:, None]
    fr = t * inv_freq[None, :]
    emb = np.concatenate([fr, fr], 1)
    return np.cos(emb).T.copy(), np.sin(emb).T.copy()


def apply_rope(p, ctx, name, chunks, T, cosf, sinf_signed, swap_st):
    SCP = p._uname(f'_rope_{name}')
    p.open_scope(SCP)
    sw_ps = p.psum(f"{name}_swps", [128, 2048], F32, scope=SCP)
    tmp = p.sbuf(f"{name}_tmp", [128, T], BF16, scope=SCP)
    NS = 512
    for c, q in enumerate(chunks):
        for n0 in range(0, T, NS):
            nn = min(NS, T - n0)
            p.op('pe', lambda e: e.matmul(sw_ps[:, n0:n0 + nn], swap_st[:],
                                          q[:, n0:n0 + nn], start=True, stop=True),
                 reads=[q, swap_st], writes=[(sw_ps, n0)])
        stags = [(sw_ps, n0) for n0 in range(0, T, NS)]
        p.op('dve', lambda e: e.tensor_tensor(tmp[:], sw_ps[:, 0:T], sinf_signed[:], OP.mult),
             reads=stags + [sinf_signed], writes=[tmp])
        p.op('dve', lambda e: e.tensor_tensor(q[:], q[:], cosf[:], OP.mult),
             reads=[q, cosf], writes=[q])
        p.op('dve', lambda e: e.tensor_tensor(q[:], q[:], tmp[:], OP.add),
             reads=[q, tmp], writes=[q])
    p.close_scope(SCP)


def emit_l0_attn_a(p, ctx, x, y, pf='', T=2048):
    D = 1024
    qw = p.dram_in_cached(pf + 'qw', (D, D), F32)
    kw = p.dram_in_cached(pf + 'kw', (D, D), F32)
    vw = p.dram_in_cached(pf + 'vw', (D, D), F32)
    ow = p.dram_in_cached(pf + 'ow', (D, D), F32)
    qb = p.dram_in_cached(pf + 'qb', (128, 8), F32)
    kb = p.dram_in_cached(pf + 'kb', (128, 8), F32)
    vb = p.dram_in_cached(pf + 'vb', (128, 8), F32)
    ob = p.dram_in_cached(pf + 'ob', (128, 8), F32)

    cos64, sin64 = rope_tables(T)
    cos_full = np.tile(cos64, (2, 1))
    sin_sgn = np.concatenate([-sin64[:32], sin64[32:]], 0)
    sin_full = np.tile(sin_sgn, (2, 1))
    sw = np.zeros((128, 128), np.float32)
    for m in range(128):
        base = (m // 64) * 64
        off = m % 64
        sw_src = base + ((off + 32) % 64)
        sw[sw_src, m] = 1.0
    cosf = p.sbuf("cosf", [128, T], BF16)
    sinf = p.sbuf("sinf", [128, T], BF16)
    swap_st = p.sbuf("swapst", [128, 128], BF16)
    cg = p.const('cosc', cos_full, F32)
    sg = p.const('sinc', sin_full, F32)
    swc = p.const('swapc', sw, F32)
    p.dma('pool', lambda e: e.dma_start(out=cosf[:], in_=cg[:, :]), writes=[cosf])
    p.dma('pool', lambda e: e.dma_start(out=sinf[:], in_=sg[:, :]), writes=[sinf])
    p.dma('pool', lambda e: e.dma_start(out=swap_st[:], in_=swc[:, :]), writes=[swap_st])

    stg = stage_input_ln(p, ctx, x, T=T)
    ln = stg['ln_bf']
    qbc = load_col_vec(p, 'qbc', qb, D)
    kbc = load_col_vec(p, 'kbc', kb, D)
    vbc = load_col_vec(p, 'vbc', vb, D)
    obc = load_col_vec(p, 'obc', ob, D)

    ao = [p.sbuf(f"ao{c}", [128, T], BF16) for c in range(8)]
    p.open_scope('qkv')
    qf = [p.sbuf(f"qf{c}", [128, T], BF16, scope='qkv') for c in range(8)]
    kf = [p.sbuf(f"kf{c}", [128, T], BF16, scope='qkv') for c in range(8)]
    vf = [p.sbuf(f"vf{c}", [128, T], BF16, scope='qkv') for c in range(8)]
    proj_fm(p, 'q', qw, ln, T, D, D, bias_col=qbc, out_chunks=qf)
    proj_fm(p, 'k', kw, ln, T, D, D, bias_col=kbc, out_chunks=kf)
    proj_fm(p, 'v', vw, ln, T, D, D, bias_col=vbc, out_chunks=vf)
    apply_rope(p, ctx, 'rq', qf, T, cosf, sinf, swap_st)
    apply_rope(p, ctx, 'rk', kf, T, cosf, sinf, swap_st)
    attention_full(p, ctx, 'at', qf, kf, vf, ao, T=T, heads=16, hd=64, scale=0.125)
    p.close_scope('qkv')

    out_fm = proj_fm(p, 'o', ow, ao, T, D, D, bias_col=obc, out_dt=BF16)
    store_residual_out(p, ctx, out_fm, x, y, T=T, D=D, f_dt=BF16)




def build_l0_attn_a(T=2048):
    p = Prog("x")
    ctx = Ctx(p)
    x = p.dram_in('x', (T, 1024), F32)
    y = p.dram_out('y', (T, 1024), F32)
    emit_l0_attn_a(p, ctx, x, y, '', T)
    return p

def ref_l0_attn_a(x, qw, qb, kw, kb, vw, vb, ow, ob):
    xs = x.astype(np.float64)
    mu = xs.mean(-1, keepdims=True)
    z = (xs - mu) / np.sqrt(xs.var(-1, keepdims=True) + 1e-5)
    T, D = x.shape
    H, hd = 16, 64
    q = (z @ qw.T.astype(np.float64) + qb).reshape(T, H, hd)
    k = (z @ kw.T.astype(np.float64) + kb).reshape(T, H, hd)
    v = (z @ vw.T.astype(np.float64) + vb).reshape(T, H, hd)
    cos, sin = rope_tables(T)
    cos = cos.T[:, None, :]
    sin = sin.T[:, None, :]

    def rot(t):
        t1, t2 = t[..., :32], t[..., 32:]
        rh = np.concatenate([-t2, t1], -1)
        return t * cos + rh * sin
    q, k = rot(q), rot(k)
    q = q.transpose(1, 0, 2) / np.sqrt(hd)
    k = k.transpose(1, 0, 2)
    v = v.transpose(1, 0, 2)
    s = q @ k.transpose(0, 2, 1)
    e = np.exp(s - s.max(-1, keepdims=True))
    a = e / e.sum(-1, keepdims=True)
    ctx = (a @ v).transpose(1, 0, 2).reshape(T, D)
    return x + (ctx @ ow.T.astype(np.float64) + ob).astype(np.float32)



"""L0_attn_b: Transformer-XL relative attention.
Inner LN(gamma,beta) fused with outer LN. q,k,v,r projections (no bias);
AC = (q+u)k^T;  BD = (q+v_b)r^T rel-shifted via DRAM bounce;  softmax((AC+BDs)/8).
Host: wq/wk/wv/wr (1024, 1024) natural (x@W); wo (1024, 1024) natural (out@Wo);
bo col (128, 8); gcol/bcol (128, 8); ucol/vcol (128, 8); rel_pos_emb (10001, 1024).
"""



MAX_LEN = 5000


def emit_l0_attn_b(p, ctx, x, y, pf='', T=2048):
    D = 1024
    L = T
    wq = p.dram_in_cached(pf + 'wq', (D, D), F32)
    wk = p.dram_in_cached(pf + 'wk', (D, D), F32)
    wv = p.dram_in_cached(pf + 'wv', (D, D), F32)
    wr = p.dram_in_cached(pf + 'wr', (D, D), F32)
    wo = p.dram_in_cached(pf + 'wo', (D, D), F32)
    bo = p.dram_in_cached(pf + 'bo', (128, 8), F32)
    gcol_d = p.dram_in_cached(pf + 'gcol', (128, 8), F32)
    bcol_d = p.dram_in_cached(pf + 'bcol', (128, 8), F32)
    ucol_d = p.dram_in_cached(pf + 'ucol', (128, 8), F32)
    vcol_d = p.dram_in_cached(pf + 'vcol', (128, 8), F32)
    rpe = p.dram_in_cached(pf + 'rpe', (10001, D), F32)

    NREL = 2 * L
    WIN = L + 128
    PITCH = L + 256
    r0_idx = MAX_LEN - (L - 1)

    gcol = load_col_vec(p, 'gc', gcol_d, D)
    bcol = load_col_vec(p, 'bc', bcol_d, D)
    ucol = load_col_vec(p, 'uc', ucol_d, D)
    vcol = load_col_vec(p, 'vc', vcol_d, D)
    boc = load_col_vec(p, 'boc', bo, D)

    stg = stage_input_ln(p, ctx, x, T=T, gb=(gcol, bcol), eps_inner=1e-6)
    ln = stg['ln_bf']
    identf = ctx.get_ident(F32)
    identb = ctx.get_ident(BF16)
    identblk = ctx.get_ident_block(64, BF16)

    r_dram = p.dram_tmp('r_dram', (D, NREL), BF16)
    p.open_scope('rproj')
    re_tm = p.sbuf("re_tm", [128, D], F32, scope='rproj')
    re_fm = [p.sbuf(f"re_fm{c}", [128, 128], BF16, scope='rproj') for c in range(8)]
    t_ps = p.psum("t_ps", [128, 128], F32, scope='rproj')
    rp2 = [p.psum(f"rp2_{i}", [128, 128], F32, scope='rproj') for i in range(2)]
    r_sb = [p.sbuf(f"r_sb{i}", [128, 128], BF16, scope='rproj') for i in range(2)]
    wr_all = [p.sbuf(f"wr_a{kc}", [128, 1024], BF16, scope='rproj') for kc in range(8)]
    for kc in range(8):
        p.dma('pool', (lambda kc_: lambda e: e.dma_start(
            out=wr_all[kc_][:], in_=wr[kc_ * 128:(kc_ + 1) * 128, :]))(kc),
            writes=[wr_all[kc]])
    nrt = NREL // 128
    for rt in range(nrt):
        p.dma('sp', (lambda rt_: lambda e: e.dma_start(
            out=re_tm[:], in_=rpe[r0_idx + rt_ * 128:r0_idx + (rt_ + 1) * 128, :]))(rt),
            writes=[re_tm])
        for c in range(8):
            p.op('pe', (lambda c_: lambda e: e.transpose(
                t_ps[:], re_tm[:, c_ * 128:(c_ + 1) * 128], identf[:]))(c),
                reads=[re_tm, identf], writes=[t_ps])
            p.op('dve', (lambda c_: lambda e: e.tensor_copy(re_fm[c_][:], t_ps[:]))(c),
                 reads=[t_ps], writes=[re_fm[c]])
        for mc in range(8):
            ps = rp2[mc % 2]
            sb = r_sb[mc % 2]
            for kc in range(8):
                p.op('pe', (lambda mc_, kc_, ps_: lambda e: e.matmul(
                    ps_[:, 0:128], wr_all[kc_][:, mc_ * 128:(mc_ + 1) * 128],
                    re_fm[kc_][:], start=(kc_ == 0), stop=(kc_ == 7)))(mc, kc, ps),
                    reads=[wr_all[kc], re_fm[kc]], writes=[ps])
            p.op('act', (lambda ps_, sb_: lambda e: e.copy(sb_[:], ps_[:, 0:128]))(ps, sb),
                 reads=[ps], writes=[sb])
            p.dma('sp', (lambda mc_, rt_, sb_: lambda e: e.dma_start(
                out=r_dram[mc_ * 128:(mc_ + 1) * 128, rt_ * 128:(rt_ + 1) * 128],
                in_=sb_[:]))(mc, rt, sb),
                reads=[sb])
    p.close_scope('rproj')

    p.open_scope('QKV')
    qf = [p.sbuf(f"qf{c}", [128, T], BF16, scope='QKV') for c in range(8)]
    kf = [p.sbuf(f"kf{c}", [128, T], BF16, scope='QKV') for c in range(8)]
    vfc = [p.sbuf(f"vf{c}", [128, T], BF16, scope='QKV') for c in range(8)]
    proj_fm(p, 'q', wq, ln, T, D, D, out_chunks=qf, fetch_w=256)
    proj_fm(p, 'k', wk, ln, T, D, D, out_chunks=kf, fetch_w=256)
    proj_fm(p, 'v', wv, ln, T, D, D, out_chunks=vfc, fetch_w=256)
    for c in range(8):
        p.op('dve', (lambda c_: lambda e: e.tensor_scalar(
            qf[c_][:], qf[c_][:], ucol[:, c_:c_ + 1], None, OP.add, OP.bypass))(c),
            reads=[qf[c], ucol], writes=[qf[c]])
    dvu = p.sbuf("dvu", [128, 8], F32)
    p.op('dve', lambda e: e.tensor_tensor(dvu[:], vcol_d if False else vcol[:], ucol[:],
                                          OP.subtract),
         reads=[vcol, ucol], writes=[dvu])

    ao = [p.sbuf(f"ao{c}", [128, T], BF16) for c in range(8)]
    bd_dram = p.dram_tmp('bd_dram', (2, 128 * PITCH), BF16)

    p.open_scope('at')
    vT = p.sbuf("at_vT", [128, 16 * 64], BF16, scope='at')
    tps = p.psum("at_tps", [128, 128], BF16, scope='at')
    s_ps = p.psum("at_sps", [128, 2048], F32, scope='at')
    o_ps = p.psum("at_ops", [128, 128], F32, scope='at')
    pt_ps = p.psum("at_ptps", [128, 128], BF16, scope='at')
    bd2_ps = p.psum("at_bd2", [128, 128], F32, scope='at')
    pbf = p.sbuf("at_pbf", [128, 2048], BF16, scope='at')
    ptb = [p.sbuf(f"at_ptb{i}", [128, 128], BF16, scope='at') for i in range(2)]
    mrow = p.sbuf("at_mrow", [128, 8], F32, scope='at')
    qv_t = p.sbuf("at_qv", [64, 128], BF16, scope='at')
    bdtmp = p.sbuf("at_bdtmp", [128, PITCH], BF16, scope='at')
    bdsh = [p.sbuf(f"at_bdsh{i}", [128, T], BF16, scope='at') for i in range(2)]
    rwin = [p.sbuf(f"at_rw{i}", [64, WIN], BF16, scope='at') for i in range(2)]
    NS = min(512, T)
    nq = T // 128
    for h in range(16):
        ch, hr = h // 2, (h % 2) * 64
        for kc in range(nq):
            p.op('pe', (lambda kc_: lambda e: e.transpose(
                tps[0:128, 0:64], vfc[ch][hr:hr + 64, kc_ * 128:(kc_ + 1) * 128],
                identblk[hr:hr + 64, 0:64]))(kc),
                reads=[vfc[ch], identblk], writes=[tps])
            p.op('dve', (lambda kc_: lambda e: e.tensor_copy(
                vT[:, kc_ * 64:(kc_ + 1) * 64], tps[0:128, 0:64]))(kc),
                reads=[tps], writes=[(vT, kc)])
        for ti in range(nq):
            par = ti % 2
            w0 = (nq - 1) * 128 - ti * 128
            p.op('dve', (lambda ti_: lambda e: e.tensor_scalar(
                qv_t[0:64, :], qf[ch][hr:hr + 64, ti_ * 128:(ti_ + 1) * 128],
                dvu[hr:hr + 64, ch:ch + 1], None, OP.add, OP.bypass))(ti),
                reads=[qf[ch], dvu], writes=[qv_t])
            rw = rwin[par]
            p.dma('act', (lambda rw_, w0_: lambda e: e.dma_start(
                out=rw_[0:64, 0:WIN],
                in_=r_dram[h * 64:(h + 1) * 64, w0_:w0_ + WIN]))(rw, w0),
                writes=[rw])
            for n0 in range(0, T, NS):
                nn = min(NS, T - n0)
                p.op('pe', (lambda n0_, nn_, rw_: lambda e: e.matmul(
                    s_ps[:, n0_:n0_ + nn_], qv_t[0:64, :], rw_[0:64, n0_:n0_ + nn_],
                    start=True, stop=True))(n0, nn, rw),
                    reads=[qv_t, rw], writes=[(s_ps, n0)])
            p.op('pe', (lambda rw_: lambda e: e.matmul(
                bd2_ps[:, 0:128], qv_t[0:64, :], rw_[0:64, T:T + 128],
                start=True, stop=True))(rw),
                reads=[qv_t, rw], writes=[bd2_ps])
            p.op('act', lambda e: e.copy(bdtmp[:, 0:T], s_ps[:, 0:T]),
                 reads=[(s_ps, n0) for n0 in range(0, T, NS)], writes=[(bdtmp, 'a')])
            p.op('act', lambda e: e.copy(bdtmp[:, T:T + 128], bd2_ps[:, 0:128]),
                 reads=[bd2_ps], writes=[(bdtmp, 'b')])
            p.dma('sp', (lambda par_: lambda e: e.dma_start(
                out=bd_dram[par_:par_ + 1, :].rearrange(
                    "o (p c) -> (o p) c", p=128)[:, 0:T + 128],
                in_=bdtmp[:, 0:T + 128]))(par),
                reads=[(bdtmp, 'a'), (bdtmp, 'b')], writes=[(bd_dram, par)])
            for n0 in range(0, T, NS):
                p.op('pe', (lambda n0_, ti_: lambda e: e.matmul(
                    s_ps[:, n0_:n0_ + NS], qf[ch][hr:hr + 64, ti_ * 128:(ti_ + 1) * 128],
                    kf[ch][hr:hr + 64, n0_:n0_ + NS], start=True, stop=True))(n0, ti),
                    reads=[qf[ch], kf[ch]], writes=[(s_ps, n0)])
            bsh = bdsh[par]
            import concourse.bass as bass_mod
            sh_ap = bass_mod.AP(tensor=bd_dram.tensor, offset=par * 128 * PITCH + 127,
                                ap=[[PITCH - 1, 128], [1, T]])
            p.dma('sp', (lambda bsh_, sh_ap_: lambda e: e.dma_start(
                out=bsh_[:], in_=sh_ap_))(bsh, sh_ap),
                reads=[(bd_dram, par)], writes=[bsh])
            sp_tags = [(s_ps, n0) for n0 in range(0, T, NS)]
            p.op('dve', (lambda bsh_: lambda e: e.tensor_tensor(
                s_ps[:, 0:T], s_ps[:, 0:T], bsh_[:], OP.add))(bsh),
                reads=sp_tags + [bsh], writes=sp_tags)
            p.op('dve', lambda e: e.tensor_reduce(mrow[:, 0:1], s_ps[:, 0:T], AX.X, OP.max),
                 reads=sp_tags, writes=[(mrow, 'm')])
            p.op('dve', lambda e: e.tensor_scalar(mrow[:, 1:2], mrow[:, 0:1], -0.125,
                                                  None, OP.mult, OP.bypass),
                 reads=[(mrow, 'm')], writes=[(mrow, 'nm')])
            p.op('act', lambda e: e.activation(pbf[:, 0:T], s_ps[:, 0:T], AF.Exp,
                                               bias=mrow[:, 1:2], scale=0.125,
                                               accum_out=mrow[:, 2:3]),
                 reads=sp_tags + [(mrow, 'nm')], writes=[pbf, (mrow, 's')])
            p.op('dve', lambda e: e.reciprocal(mrow[:, 3:4], mrow[:, 2:3]),
                 reads=[(mrow, 's')], writes=[(mrow, 'r')])
            p.op('dve', lambda e: e.tensor_scalar(pbf[:, 0:T], pbf[:, 0:T], mrow[:, 3:4],
                                                  None, OP.mult, OP.bypass),
                 reads=[pbf, (mrow, 'r')], writes=[pbf])
            for j in range(nq):
                pt = ptb[j % 2]
                p.op('pe', (lambda j_: lambda e: e.transpose(
                    pt_ps[:], pbf[:, j_ * 128:(j_ + 1) * 128], identb[:]))(j),
                    reads=[pbf, identb], writes=[pt_ps])
                p.op('dve', (lambda pt_: lambda e: e.tensor_copy(pt_[:], pt_ps[:]))(pt),
                     reads=[pt_ps], writes=[pt])
                p.op('pe', (lambda j_, pt_: lambda e: e.matmul(
                    o_ps[0:64, 0:128], vT[:, j_ * 64:(j_ + 1) * 64], pt_[:],
                    start=(j_ == 0), stop=(j_ == nq - 1)))(j, pt),
                    reads=[(vT, j), pt], writes=[o_ps])
            p.op('dve', (lambda ti_: lambda e: e.tensor_copy(
                ao[ch][hr:hr + 64, ti_ * 128:(ti_ + 1) * 128], o_ps[0:64, 0:128]))(ti),
                reads=[o_ps], writes=[(ao[ch], (h, ti))])
    p.close_scope('at')
    p.close_scope('QKV')

    out_fm = proj_fm(p, 'o', wo, ao, T, D, D, bias_col=boc, out_dt=BF16, fetch_w=256)
    store_residual_out(p, ctx, out_fm, x, y, T=T, D=D, f_dt=BF16)




def build_l0_attn_b(T=2048):
    p = Prog("x")
    ctx = Ctx(p)
    x = p.dram_in('x', (T, 1024), F32)
    y = p.dram_out('y', (T, 1024), F32)
    emit_l0_attn_b(p, ctx, x, y, '', T)
    return p

def ref_l0_attn_b(x, gamma, beta, Wq, Wk, Wv, Wr, u_bias, v_bias, Wo, bo, rpe):
    xs = x.astype(np.float64)
    mu = xs.mean(-1, keepdims=True)
    z = (xs - mu) / np.sqrt(xs.var(-1, keepdims=True) + 1e-5)
    mu2 = z.mean(-1, keepdims=True)
    v2 = ((z - mu2) ** 2).mean(-1, keepdims=True)
    xn = (z - mu2) / np.sqrt(v2 + 1e-6) * gamma + beta
    T, D = x.shape
    H, hd = 16, 64

    def heads(t):
        return t.reshape(T, H, hd).transpose(1, 0, 2)
    q = heads(xn @ Wq.astype(np.float64))
    k = heads(xn @ Wk.astype(np.float64))
    v = heads(xn @ Wv.astype(np.float64))
    idx = np.arange(2 * T - 1) - (T - 1) + MAX_LEN
    r_emb = rpe[idx].astype(np.float64)
    r = (r_emb @ Wr.astype(np.float64)).reshape(2 * T - 1, H, hd).transpose(1, 0, 2)
    AC = np.einsum('hld,hmd->hlm', q + u_bias[:, None, :], k)
    BD = np.einsum('hld,hrd->hlr', q + v_bias[:, None, :], r)
    Bp = np.pad(BD, ((0, 0), (0, 0), (1, 0)))
    Bp = Bp.reshape(H, 2 * T, T)
    BDs = Bp[:, 1:, :].reshape(H, T, 2 * T - 1)[:, :, :T]
    s = (AC + BDs) / np.sqrt(hd)
    e = np.exp(s - s.max(-1, keepdims=True))
    a = e / e.sum(-1, keepdims=True)
    out = np.einsum('hlm,hmd->hld', a, v).transpose(1, 0, 2).reshape(T, D)
    return x + (out @ Wo.astype(np.float64) + bo).astype(np.float32)



"""L0_ffn: SwiGLU MLP + product-key expert retrieval (PEER), exact dense-gate.
Host: wg/wu (1024, 4096) natural; wd (4096, 1024) natural; bg/bu cols (128, 32);
bd col (128, 8); wq (1024, 512) natural; keysx/keysy (4*64, 64) [keys[h,:,p,:]
transposed to (n, k) per head -> stacked (256, 64)]; det (1024, 4096)
[down_embed.T]; ue (4096, 1024) natural.
"""




def emit_l0_ffn(p, ctx, x, y, pf='', T=2048):
    D, HID, NE = 1024, 4096, 4096
    wg = p.dram_in_cached(pf + 'wg', (D, HID), F32)
    wu = p.dram_in_cached(pf + 'wu', (D, HID), F32)
    wd = p.dram_in_cached(pf + 'wd', (HID, D), F32)
    bgc_d = p.dram_in_cached(pf + 'bgc', (128, 32), F32)
    buc_d = p.dram_in_cached(pf + 'buc', (128, 32), F32)
    bdc_d = p.dram_in_cached(pf + 'bdc', (128, 8), F32)
    wq = p.dram_in_cached(pf + 'wq', (D, 512), F32)
    keysx = p.dram_in_cached(pf + 'keysx', (256, 64), F32)
    keysy = p.dram_in_cached(pf + 'keysy', (256, 64), F32)
    det = p.dram_in_cached(pf + 'det', (D, NE), F32)
    ue = p.dram_in_cached(pf + 'ue', (NE, D), F32)

    stg = stage_input_ln(p, ctx, x, T=T)
    ln = stg['ln_bf']
    identb = ctx.get_ident(BF16)
    bgc = load_col_vec(p, 'bg', bgc_d, HID)
    buc = load_col_vec(p, 'bu', buc_d, HID)
    bdc = load_col_vec(p, 'bd', bdc_d, D)

    kx = p.sbuf("kx", [128, 2 * 64], BF16)
    kxs = [p.sbuf(f"kxs{i}", [128, 64], BF16) for i in range(2)]
    kys = [p.sbuf(f"kys{i}", [128, 64], BF16) for i in range(2)]
    for i in range(2):
        p.dma('pool', (lambda i_: lambda e: e.dma_start(
            out=kxs[i_][:], in_=keysx[i_ * 128:(i_ + 1) * 128, :]))(i), writes=[kxs[i]])
        p.dma('pool', (lambda i_: lambda e: e.dma_start(
            out=kys[i_][:], in_=keysy[i_ * 128:(i_ + 1) * 128, :]))(i), writes=[kys[i]])

    qf = [p.sbuf(f"qf{c}", [128, T], BF16) for c in range(4)]
    proj_fm(p, 'q', wq, ln, T, D, 512, out_chunks=qf)

    out_fm = [p.sbuf(f"of{c}", [128, T], BF16) for c in range(8)]
    TB = min(512, T)
    ntb = TB // 128
    nh = HID // 128

    for tb0 in range(0, T, TB):
        p.open_scope('blk')
        hch = [p.sbuf(f"h{hc}", [128, TB], BF16, scope='blk') for hc in range(nh)]
        p.open_scope('gu')
        sil = p.sbuf("sil", [128, TB], BF16, scope='gu')

        def epi_g(m, ps, ptags):
            p.op('act', lambda e: e.activation(
                sil[:], ps[:, 0:TB], AF.Silu, bias=bgc[:, m:m + 1]),
                reads=ptags + [bgc], writes=[sil])

        def epi_u(m, ps, ptags):
            p.op('dve', lambda e: e.tensor_scalar(
                hch[m][:], ps[:, 0:TB], buc[:, m:m + 1], None, OP.add, OP.bypass),
                reads=ptags + [buc], writes=[(hch[m], 'u')])
            p.op('dve', lambda e: e.tensor_tensor(
                hch[m][:], hch[m][:], sil[:], OP.mult),
                reads=[(hch[m], 'u'), sil], writes=[(hch[m], 'h')])

        for hc in range(nh):
            proj_fm(p, f'pg{hc}', wg[0:D, hc * 128:(hc + 1) * 128], ln, TB, D, 128,
                    epilogue=lambda mcj, ps, ptags: epi_g(hc, ps, ptags),
                    x_off=tb0, fetch_w=128)
            proj_fm(p, f'pu{hc}', wu[0:D, hc * 128:(hc + 1) * 128], ln, TB, D, 128,
                    epilogue=lambda mcj, ps, ptags: epi_u(hc, ps, ptags),
                    x_off=tb0, fetch_w=128)
        p.close_scope('gu')
        p.open_scope('dn')

        def epi_d(m, ps, ptags):
            p.op('dve', lambda e: e.tensor_scalar(
                out_fm[m][:, tb0:tb0 + TB], ps[:, 0:TB], bdc[:, m:m + 1], None,
                OP.add, OP.bypass),
                reads=ptags + [bdc], writes=[(out_fm[m], tb0)])

        proj_fm(p, 'pd', wd, hch, TB, HID, D, epilogue=epi_d, fetch_w=128)
        p.close_scope('dn')
        p.close_scope('blk')

    for tb0 in range(0, T, TB):
        p.open_scope('pe_blk')
        gT = [p.sbuf(f"gT{ec}", [128, TB], BF16, scope='pe_blk') for ec in range(32)]
        p.open_scope('pe_ps')
        sim_ps = p.psum("sim_ps", [128, 512], F32, scope='pe_ps')
        w_ps = p.psum("w_ps", [128, 512], F32, scope='pe_ps')
        gt_ps = p.psum("gt_ps", [128, 128], BF16, scope='pe_ps')
        srow = p.sbuf("srow", [128, 64], F32, scope='pe_blk')
        sx8 = p.sbuf("sx8", [128, 8], F32, scope='pe_blk')
        sy8 = p.sbuf("sy8", [128, 8], F32, scope='pe_blk')
        gg8 = p.sbuf("gg8", [128, 8], F32, scope='pe_blk')
        g64 = p.sbuf("g64", [128, 64], F32, scope='pe_blk')
        gx = p.sbuf("gx", [128, 64], F32, scope='pe_blk')
        gy = p.sbuf("gy", [128, 64], F32, scope='pe_blk')
        mrow = p.sbuf("mrowp", [128, 8], F32, scope='pe_blk')
        pacc = p.sbuf("pacc", [128, NE], F32, scope='pe_blk')
        outer = p.sbuf("outer", [128, NE], F32, scope='pe_blk')
        wsil = p.sbuf("wsil", [128, NE], BF16, scope='pe_blk')
        gds = p.sbuf("gds", [128, NE], BF16, scope='pe_blk')
        ws = [p.sbuf(f"dws{b}", [128, 512], BF16, scope='pe_blk') for b in range(3)]
        for sj in range(ntb):
            ti = tb0 // 128 + sj
            for h in range(4):
                hcx = h // 2
                hrx = (h % 2) * 64
                for pi, (ksrc, s8, gdst) in enumerate([(kxs, sx8, gx), (kys, sy8, gy)]):
                    qch = qf[hcx] if pi == 0 else qf[2 + hcx]
                    p.op('pe', (lambda qch_, hrx_, ti_, ks_: lambda e: e.matmul(
                        sim_ps[:, 0:64], qch_[hrx_:hrx_ + 64, ti_ * 128:(ti_ + 1) * 128],
                        ks_[hcx][hrx_:hrx_ + 64, 0:64], start=True, stop=True))(
                            qch, hrx, ti, ksrc),
                        reads=[qch, ksrc[hcx]], writes=[sim_ps])
                    p.op('dve', lambda e: e.tensor_copy(srow[:, 0:64], sim_ps[:, 0:64]),
                         reads=[sim_ps], writes=[srow])
                    p.op('dve', (lambda s8_: lambda e: e.max(s8_[:], srow[:, 0:64]))(s8),
                         reads=[srow], writes=[s8])
                    p.op('dve', (lambda s8_: lambda e: e.tensor_scalar(
                        mrow[:, 0:1], s8_[:, 0:1], -1.0, None, OP.mult, OP.bypass))(s8),
                        reads=[s8], writes=[(mrow, 'nm')])
                    p.op('act', (lambda gd_: lambda e: e.activation(
                        gd_[:], srow[:, 0:64], AF.Exp, bias=mrow[:, 0:1]))(gdst),
                        reads=[srow, (mrow, 'nm')], writes=[(gdst, 'e')])
                    p.op('dve', (lambda s8_: lambda e: e.tensor_scalar(
                        g64[:], srow[:, 0:64], s8_[:, 7:8], None, OP.is_ge, OP.bypass))(s8),
                        reads=[srow, s8], writes=[g64])
                    p.op('dve', (lambda gd_: lambda e: e.tensor_tensor(
                        gd_[:], gd_[:], g64[:], OP.mult))(gdst),
                        reads=[(gdst, 'e'), g64], writes=[gdst])
                p.op('dve', lambda e: e.tensor_tensor(
                    g64[:].rearrange("p (a b) -> p a b", b=8),
                    sx8[:].unsqueeze(2).broadcast_to((128, 8, 8)),
                    sy8[:].unsqueeze(1).broadcast_to((128, 8, 8)), OP.add),
                    reads=[sx8, sy8], writes=[g64])
                p.op('dve', lambda e: e.max(gg8[:], g64[:]), reads=[g64], writes=[gg8])
                p.op('dve', lambda e: e.tensor_scalar(mrow[:, 1:2], gg8[:, 0:1], -1.0,
                                                      None, OP.mult, OP.bypass),
                     reads=[gg8], writes=[(mrow, 'nm2')])
                p.op('act', lambda e: e.activation(srow[:, 0:8], gg8[:], AF.Exp,
                                                   bias=mrow[:, 1:2],
                                                   accum_out=mrow[:, 2:3]),
                     reads=[gg8, (mrow, 'nm2')], writes=[(srow, 'z8'), (mrow, 'Z')])
                p.op('dve', lambda e: e.reciprocal(mrow[:, 3:4], mrow[:, 2:3]),
                     reads=[(mrow, 'Z')], writes=[(mrow, 'rZ')])
                p.op('dve', lambda e: e.tensor_scalar(gx[:], gx[:], mrow[:, 3:4], None,
                                                      OP.mult, OP.bypass),
                     reads=[gx, (mrow, 'rZ')], writes=[gx])
                p.op('dve', lambda e: e.tensor_tensor(mrow[:, 4:5], srow[:, 7:8],
                                                      mrow[:, 3:4], OP.mult),
                     reads=[(srow, 'z8'), (mrow, 'rZ')], writes=[(mrow, 'tz')])
                p.op('dve', lambda e: e.tensor_tensor(
                    outer[:].rearrange("p (a b) -> p a b", b=64),
                    gx[:].unsqueeze(2).broadcast_to((128, 64, 64)),
                    gy[:].unsqueeze(1).broadcast_to((128, 64, 64)), OP.mult),
                    reads=[gx, gy], writes=[outer])
                if h == 0:
                    p.op('dve', lambda e: e.scalar_tensor_tensor(
                        pacc[:], outer[:], mrow[:, 4:5], outer[:], OP.is_ge, OP.mult),
                        reads=[outer, (mrow, 'tz')], writes=[pacc])
                else:
                    p.op('dve', lambda e: e.scalar_tensor_tensor(
                        outer[:], outer[:], mrow[:, 4:5], outer[:], OP.is_ge, OP.mult),
                        reads=[outer, (mrow, 'tz')], writes=[outer])
                    p.op('dve', lambda e: e.tensor_tensor(pacc[:], pacc[:], outer[:], OP.add),
                         reads=[pacc, outer], writes=[pacc])
            for es in range(NE // 512):
                for kc in range(8):
                    wb = ws[kc % 3]
                    p.dma('pool', (lambda wb_, kc_, es_: lambda e: e.dma_start(
                        out=wb_[:], in_=det[kc_ * 128:(kc_ + 1) * 128,
                                            es_ * 512:(es_ + 1) * 512]))(wb, kc, es),
                        writes=[wb])
                    p.op('pe', (lambda wb_, kc_, ti_: lambda e: e.matmul(
                        w_ps[:, 0:512], ln[kc_][:, ti_ * 128:(ti_ + 1) * 128], wb_[:],
                        start=(kc_ == 0), stop=(kc_ == 7)))(wb, kc, ti),
                        reads=[wb, ln[kc]], writes=[w_ps])
                p.op('act', (lambda es_: lambda e: e.activation(
                    wsil[:, es_ * 512:(es_ + 1) * 512], w_ps[:, 0:512], AF.Silu))(es),
                    reads=[w_ps], writes=[(wsil, es)])
            p.op('dve', lambda e: e.tensor_tensor(gds[:], wsil[:], pacc[:], OP.mult),
                 reads=[pacc] + [(wsil, es) for es in range(8)], writes=[gds])
            for ec in range(32):
                p.op('pe', (lambda ec_: lambda e: e.transpose(
                    gt_ps[:], gds[:, ec_ * 128:(ec_ + 1) * 128], identb[:]))(ec),
                    reads=[gds, identb], writes=[gt_ps])
                p.op('dve', (lambda ec_, sj_: lambda e: e.tensor_copy(
                    gT[ec_][:, sj_ * 128:(sj_ + 1) * 128], gt_ps[:]))(ec, sj),
                    reads=[gt_ps], writes=[(gT[ec], sj)])
        p.close_scope('pe_ps')
        p.open_scope('ue2')
        ue_ps = [p.psum(f"u2_ps{m}", [128, TB], F32, scope='ue2') for m in range(8)]
        uws = [p.sbuf(f"u2ws{b}", [128, 1024], BF16, scope='ue2') for b in range(3)]
        for ec in range(32):
            wb = uws[ec % 3]
            p.dma('pool', (lambda wb_, ec_: lambda e: e.dma_start(
                out=wb_[:], in_=ue[ec_ * 128:(ec_ + 1) * 128, :]))(wb, ec),
                writes=[wb])
            for mc in range(8):
                p.op('pe', (lambda wb_, ec_, mc_: lambda e: e.matmul(
                    ue_ps[mc_][:, 0:TB], wb_[:, mc_ * 128:(mc_ + 1) * 128],
                    gT[ec_][:, 0:TB], start=(ec_ == 0), stop=(ec_ == 31)))(wb, ec, mc),
                    reads=[wb] + [(gT[ec], sj) for sj in range(ntb)],
                    writes=[ue_ps[mc]])
        for mc in range(8):
            p.op('dve', (lambda mc_: lambda e: e.tensor_tensor(
                out_fm[mc_][:, tb0:tb0 + TB], out_fm[mc_][:, tb0:tb0 + TB],
                ue_ps[mc_][:, 0:TB], OP.add))(mc),
                reads=[(out_fm[mc], tb0), ue_ps[mc]], writes=[(out_fm[mc], tb0)])
        p.close_scope('ue2')
        p.close_scope('pe_blk')

    store_residual_out(p, ctx, out_fm, x, y, T=T, D=D, f_dt=BF16)




def build_l0_ffn(T=2048):
    p = Prog("x")
    ctx = Ctx(p)
    x = p.dram_in('x', (T, 1024), F32)
    y = p.dram_out('y', (T, 1024), F32)
    emit_l0_ffn(p, ctx, x, y, '', T)
    return p

def ref_l0_ffn(x, Wg, bg, Wu, bu, Wd, bd, Wq, keys, down_embed, up_embed):
    xs = x.astype(np.float64)
    mu = xs.mean(-1, keepdims=True)
    z = (xs - mu) / np.sqrt(xs.var(-1, keepdims=True) + 1e-5)
    T = x.shape[0]
    H, KPH, NK, RET = 4, 8, 64, 128
    q = (z @ Wq.astype(np.float64)).reshape(T, 2, H, RET // 2)
    sim = np.einsum('tphn,hkpn->pthk', q, keys.astype(np.float64))
    sx = -np.sort(-sim[0], -1)[..., :8]
    ixx = np.argsort(-sim[0], -1, kind='stable')[..., :8]
    sy = -np.sort(-sim[1], -1)[..., :8]
    ixy = np.argsort(-sim[1], -1, kind='stable')[..., :8]
    all_sc = (sx[..., :, None] + sy[..., None, :]).reshape(T, H, 64)
    all_ix = (ixx[..., :, None] * NK + ixy[..., None, :]).reshape(T, H, 64)
    order = np.argsort(-all_sc, -1, kind='stable')[..., :8]
    scores = np.take_along_axis(all_sc, order, -1)
    idx = np.take_along_axis(all_ix, order, -1)
    de = down_embed.astype(np.float64)[idx]
    ue_ = up_embed.astype(np.float64)[idx]
    w = np.einsum('td,thkd->thk', z, de)
    e = np.exp(scores - scores.max(-1, keepdims=True))
    sm = e / e.sum(-1, keepdims=True)
    w = w / (1 + np.exp(-w)) * sm
    experts = np.einsum('thk,thkd->td', w, ue_)
    gate = z @ Wg.astype(np.float64) + bg
    mlp = (gate / (1 + np.exp(-gate)) * (z @ Wu.astype(np.float64) + bu)) @ Wd.astype(np.float64) + bd
    return x + (mlp + experts).astype(np.float32)



"""L1_attn_a: differential attention.
Host inputs: Wq_ext, Wk_ext (1024, 2048): ext-halfhead j at cols [j*64, j*64+32),
upper 32 cols zero;  Wv (1024, 1024) [v part of W_qkv];  Wp (1024, 1024) [W_proj];
bp col (128, 8); lamq1/lamk1/lamq2/lamk2 (1, 32); swc (128, 1) subln*(1-LI) col.
"""



LAMBDA_INIT = 0.8 - 0.6 * float(np.exp(-0.3 * 12))


def emit_l1_attn_a(p, ctx, x, y, pf='', T=2048):
    D = 1024
    wqn = p.dram_in_cached(pf + 'wqn', (D, D), F32)
    wkn = p.dram_in_cached(pf + 'wkn', (D, D), F32)
    wv = p.dram_in_cached(pf + 'wv', (D, D), F32)
    wp = p.dram_in_cached(pf + 'wp', (D, D), F32)
    bp = p.dram_in_cached(pf + 'bp', (128, 8), F32)
    lq1 = p.dram_in_cached(pf + 'lq1', (1, 32), F32)
    lk1 = p.dram_in_cached(pf + 'lk1', (1, 32), F32)
    lq2 = p.dram_in_cached(pf + 'lq2', (1, 32), F32)
    lk2 = p.dram_in_cached(pf + 'lk2', (1, 32), F32)
    swc = p.dram_in_cached(pf + 'swc', (128, 1), F32)

    stg = stage_input_ln(p, ctx, x, T=T)
    ln = stg['ln_bf']
    bpc = load_col_vec(p, 'bpc', bp, D)
    swt = p.sbuf("swt", [128, 1], F32)
    p.dma('sp', lambda e: e.dma_start(out=swt[:], in_=swc[:, :]), writes=[swt])

    lrow = p.sbuf("lrow", [1, 128], F32)
    for nm, ap_ in [('lq1', lq1), ('lk1', lk1), ('lq2', lq2), ('lk2', lk2)]:
        t = p.sbuf(nm + "_t", [1, 32], F32)
        p.dma('sp', (lambda t_, a_: lambda e: e.dma_start(out=t_[:], in_=a_[:, :]))(t, ap_),
              writes=[t])
        locals()[nm + '_t'] = t
        if nm == 'lq1':
            lq1t = t
        elif nm == 'lk1':
            lk1t = t
        elif nm == 'lq2':
            lq2t = t
        else:
            lk2t = t
    lw = p.sbuf("lw", [1, 40], F32)
    p.op('dve', lambda e: e.tensor_tensor(lw[0:1, 0:32], lq1t[:], lk1t[:], OP.mult),
         reads=[lq1t, lk1t], writes=[(lw, 'a')])
    p.op('dve', lambda e: e.tensor_reduce(lw[0:1, 32:33], lw[0:1, 0:32], AX.X, OP.add),
         reads=[(lw, 'a')], writes=[(lw, 's1')])
    p.op('dve', lambda e: e.tensor_tensor(lw[0:1, 0:32], lq2t[:], lk2t[:], OP.mult),
         reads=[lq2t, lk2t], writes=[(lw, 'a')])
    p.op('dve', lambda e: e.tensor_reduce(lw[0:1, 33:34], lw[0:1, 0:32], AX.X, OP.add),
         reads=[(lw, 'a')], writes=[(lw, 's2')])
    p.op('act', lambda e: e.activation(lw[0:1, 34:35], lw[0:1, 32:33], AF.Exp),
         reads=[(lw, 's1')], writes=[(lw, 'e1')])
    p.op('act', lambda e: e.activation(lw[0:1, 35:36], lw[0:1, 33:34], AF.Exp),
         reads=[(lw, 's2')], writes=[(lw, 'e2')])
    p.op('dve', lambda e: e.tensor_tensor(lw[0:1, 36:37], lw[0:1, 35:36], lw[0:1, 34:35],
                                          OP.subtract),
         reads=[(lw, 'e1'), (lw, 'e2')], writes=[(lw, 'd')])
    p.op('dve', lambda e: e.tensor_scalar(lw[0:1, 37:38], lw[0:1, 36:37], -LAMBDA_INIT,
                                          None, OP.add, OP.bypass),
         reads=[(lw, 'd')], writes=[(lw, 'nl')])
    ones = ctx.get_ones_col()
    nl_col = p.sbuf("nl_col", [128, 1], F32)
    p.open_scope('nl')
    nl_ps = p.psum("nl_ps", [128, 128], F32, scope='nl')
    p.op('pe', lambda e: e.matmul(nl_ps[:, 0:1], ones[0:1, :].broadcast_to((1, 128)),
                                  lw[0:1, 37:38], start=True, stop=True),
         reads=[(lw, 'nl'), ones], writes=[nl_ps])
    p.op('dve', lambda e: e.tensor_copy(nl_col[:], nl_ps[:, 0:1]),
         reads=[nl_ps], writes=[nl_col])
    p.close_scope('nl')

    vf = [p.sbuf(f"vf{c}", [128, T], BF16) for c in range(8)]
    proj_fm(p, 'v', wv, ln, T, D, D, out_chunks=vf)
    ao1 = [p.sbuf(f"ao1{c}", [128, T], BF16) for c in range(8)]
    NSq = min(512, T)
    for gi in range(2):
        p.open_scope('g')
        if gi == 1:
            ao2 = [p.sbuf(f"ao2{c}", [128, T], BF16, scope='g') for c in range(8)]
        ao = ao1 if gi == 0 else ao2
        wq_g = [p.sbuf(f"wq_g{kc}", [128, 512], BF16, scope='g') for kc in range(8)]
        wk_g = [p.sbuf(f"wk_g{kc}", [128, 512], BF16, scope='g') for kc in range(8)]
        for kc in range(8):
            p.dma('pool', (lambda kc_: lambda e: e.dma_start(
                out=wq_g[kc_][:], in_=wqn[kc_ * 128:(kc_ + 1) * 128,
                                          gi * 512:(gi + 1) * 512]))(kc),
                writes=[wq_g[kc]])
            p.dma('pool', (lambda kc_: lambda e: e.dma_start(
                out=wk_g[kc_][:], in_=wkn[kc_ * 128:(kc_ + 1) * 128,
                                          gi * 512:(gi + 1) * 512]))(kc),
                writes=[wk_g[kc]])
        qb = p.sbuf("qb", [32, T], BF16, scope='g')
        kb = p.sbuf("kb", [32, T], BF16, scope='g')
        sps_holder = {}

        def prepare(h):
            sps = sps_holder['sps']
            for wsrc, dst in ((wq_g, qb), (wk_g, kb)):
                for kc in range(8):
                    for n0 in range(0, T, NSq):
                        nn = min(NSq, T - n0)
                        p.op('pe', (lambda w_, kc_, n0_, nn_, h_: lambda e: e.matmul(
                            sps[0:32, n0_:n0_ + nn_],
                            w_[kc_][:, h_ * 32:(h_ + 1) * 32],
                            ln[kc_][:, n0_:n0_ + nn_],
                            start=(kc_ == 0), stop=(kc_ == 7)))(wsrc, kc, n0, nn, h),
                            reads=[wsrc[kc], ln[kc]], writes=[(sps, n0)])
                p.op('dve', (lambda dst_: lambda e: e.tensor_copy(
                    dst_[0:32, :], sps[0:32, 0:T]))(dst),
                    reads=[(sps, n0) for n0 in range(0, T, NSq)], writes=[dst])

        def put_out(h, ti, ps_ap, reads):
            ch, r0 = h // 2, (h % 2) * 64
            p.op('dve', lambda e: e.tensor_copy(
                ao[ch][r0:r0 + 64, ti * 128:(ti + 1) * 128], ps_ap),
                reads=reads, writes=[(ao[ch], (h, ti))])

        def get_v(h):
            return vf[h // 2], (h % 2) * 64, [vf[h // 2]]

        attention_heads(p, ctx, f'at{gi}', 16,
                        lambda h: (qb, 0, [qb]), lambda h: (kb, 0, [kb]),
                        get_v, put_out, T=T, hd_qk=32, hd_v=64,
                        scale=float(32 ** -0.5), prepare=prepare,
                        sps_holder=sps_holder)
        if gi == 1:
            for c in range(8):
                p.op('dve', (lambda c_: lambda e: e.scalar_tensor_tensor(
                    ao1[c_][:], ao2[c_][:], nl_col[:], ao1[c_][:], OP.mult, OP.add))(c),
                    reads=[ao2[c], nl_col, ao1[c]], writes=[ao1[c]])
        p.close_scope('g')

    blk1 = np.zeros((128, 2), np.float32)
    blk1[0:64, 0] = 1
    blk1[64:128, 1] = 1
    bsel = np.zeros((2, 128), np.float32)
    bsel[0, 0:64] = 1
    bsel[1, 64:128] = 1
    b1t = p.sbuf("b1t", [128, 2], F32)
    bst = p.sbuf("bst", [2, 128], F32)
    p.dma('sp', lambda e: e.dma_start(out=b1t[:], in_=p.const('b1c', blk1, F32)[:, :]),
          writes=[b1t])
    p.dma('sp', lambda e: e.dma_start(out=bst[:], in_=p.const('bsc', bsel, F32)[:, :]),
          writes=[bst])
    out_in = ao1
    p.open_scope('rms')
    of = p.sbuf("of", [128, T], F32, scope='rms')
    sqf = p.sbuf("sqf", [128, T], F32, scope='rms')
    ssum = p.sbuf("ssum", [2, T], F32, scope='rms')
    s_ps = p.psum("s_ps", [128, 2048], F32, scope='rms')
    r_ps = p.psum("r_ps", [128, 2048], F32, scope='rms')
    NS = min(512, T)
    epsb = ctx.cval(1e-5)
    for c in range(8):
        p.op('dve', lambda e: e.tensor_copy(of[:], ao1[c][:]),
             reads=[ao1[c]], writes=[of])
        p.op('act', lambda e: e.activation(sqf[:], of[:], AF.Square),
             reads=[of], writes=[sqf])
        for n0 in range(0, T, NS):
            p.op('pe', lambda e: e.matmul(s_ps[0:2, n0:n0 + NS], b1t[:],
                                          sqf[:, n0:n0 + NS], start=True, stop=True),
                 reads=[sqf, b1t], writes=[(s_ps, n0)])
        p.op('dve', lambda e: e.tensor_scalar(ssum[0:2, :], s_ps[0:2, 0:T], 1.0 / 64.0,
                                              None, OP.mult, OP.bypass),
             reads=[(s_ps, n0) for n0 in range(0, T, NS)], writes=[ssum])
        p.op('act', lambda e: e.activation(ssum[0:2, :], ssum[0:2, :], AF.Ln,
                                           bias=epsb[0:2, :]),
             reads=[ssum, epsb], writes=[ssum])
        p.op('act', lambda e: e.activation(ssum[0:2, :], ssum[0:2, :], AF.Exp, scale=-0.5),
             reads=[ssum], writes=[ssum])
        for n0 in range(0, T, NS):
            p.op('pe', lambda e: e.matmul(r_ps[:, n0:n0 + NS], bst[0:2, :],
                                          ssum[0:2, n0:n0 + NS], start=True, stop=True),
                 reads=[ssum, bst], writes=[(r_ps, n0)])
        p.op('dve', lambda e: e.tensor_tensor(of[:], of[:], r_ps[:, 0:T], OP.mult),
             reads=[of] + [(r_ps, n0) for n0 in range(0, T, NS)], writes=[of])
        p.op('dve', lambda e: e.tensor_scalar(out_in[c][:], of[:], swt[:],
                                              1.0 - LAMBDA_INIT, OP.mult, OP.mult),
             reads=[of, swt], writes=[out_in[c]])
    p.close_scope('rms')
    out_fm = proj_fm(p, 'o', wp, out_in, T, D, D, bias_col=bpc, out_dt=BF16)
    store_residual_out(p, ctx, out_fm, x, y, T=T, D=D, f_dt=BF16)




def build_l1_attn_a(T=2048):
    p = Prog("x")
    ctx = Ctx(p)
    x = p.dram_in('x', (T, 1024), F32)
    y = p.dram_out('y', (T, 1024), F32)
    emit_l1_attn_a(p, ctx, x, y, '', T)
    return p

def ref_l1_attn_a(x, W_qkv, W_proj, b_proj, lq1, lk1, lq2, lk2, subln_w):
    xs = x.astype(np.float64)
    mu = xs.mean(-1, keepdims=True)
    z = (xs - mu) / np.sqrt(xs.var(-1, keepdims=True) + 1e-5)
    T, C = x.shape
    H, hd = 16, 32
    qkv = (z @ W_qkv.astype(np.float64)).reshape(T, 3, H, 64)
    q, k, v = qkv[:, 0], qkv[:, 1], qkv[:, 2]
    q = q.reshape(T, 2 * H, hd).transpose(1, 0, 2)
    k = k.reshape(T, 2 * H, hd).transpose(1, 0, 2)
    v = v.transpose(1, 0, 2)
    scale = hd ** -0.5

    def sdpa(qh, kh):
        s = np.einsum('hqd,hkd->hqk', qh, kh) * scale
        e = np.exp(s - s.max(-1, keepdims=True))
        a = e / e.sum(-1, keepdims=True)
        return np.einsum('hqk,hkd->hqd', a, v)
    a1 = sdpa(q[:H], k[:H])
    a2 = sdpa(q[H:], k[H:])
    lam = np.exp((lq1 * lk1).sum()) - np.exp((lq2 * lk2).sum()) + LAMBDA_INIT
    o = a1 - lam * a2
    o = o / np.sqrt((o * o).mean(-1, keepdims=True) + 1e-5)
    o = o * subln_w * (1 - LAMBDA_INIT)
    o = o.transpose(1, 0, 2).reshape(T, C)
    return x + (o @ W_proj.astype(np.float64) + b_proj).astype(np.float32)


def make_ext(Wpart):
    W = np.zeros((1024, 2048), np.float32)
    for j in range(32):
        W[:, j * 64:j * 64 + 32] = Wpart[:, j * 32:(j + 1) * 32]
    return W



"""L1_attn_b: windowed attention, 8 full heads (hd 64, win 1024) + 8 reduced
heads (qk 32 zero-padded to 64, v 64, win 256), concat, proj.
Host: wqkvf (1024, 1536) [w_qkv_full.T]; wqkre (1024, 1024) [qk_red ext-padded];
wvr (1024, 512) [w_v_red.T]; wp (1024, 1024) [w_proj.T].
"""




def emit_l1_attn_b(p, ctx, x, y, pf='', T=2048):
    D = 1024
    wqkvf = p.dram_in_cached(pf + 'wqkvf', (D, 1536), F32)
    wqkre = p.dram_in_cached(pf + 'wqkre', (D, 1024), F32)
    wvr = p.dram_in_cached(pf + 'wvr', (D, 512), F32)
    wp = p.dram_in_cached(pf + 'wp', (D, D), F32)

    stg = stage_input_ln(p, ctx, x, T=T)
    ln = stg['ln_bf']

    mt = np.where(np.arange(128)[None, :] > np.arange(128)[:, None], -1e30, 0.0)
    mtile = p.sbuf("mtile", [128, 128], F32)
    p.dma('sp', lambda e: e.dma_start(out=mtile[:], in_=p.const('mtc', mt, F32)[:, :]),
          writes=[mtile])
    mt2 = np.where(np.arange(128)[:, None] >= np.arange(128)[None, :], -1e30, 0.0)
    mtile2 = p.sbuf("mtile2", [128, 128], F32)
    p.dma('sp', lambda e: e.dma_start(out=mtile2[:], in_=p.const('mtc2', mt2, F32)[:, :]),
          writes=[mtile2])

    ao = [p.sbuf(f"ao{c}", [128, T], BF16) for c in range(8)]

    def mk_mask(wb):
        def mask_bias(ti, kc):
            if kc == ti:
                return (mtile[:], mtile)
            if kc == ti - wb:
                return (mtile2[:], mtile2)
            return None
        return mask_bias

    p.open_scope('F')
    qkvf = [p.sbuf(f"qkvf{c}", [128, T], BF16, scope='F') for c in range(12)]
    proj_fm(p, 'f', wqkvf, ln, T, D, 1536, out_chunks=qkvf)
    attention_full(p, ctx, 'atf', qkvf[0:4], qkvf[4:8], qkvf[8:12], ao[0:4],
                   T=T, heads=8, hd=64, scale=0.125,
                   kblock_range=lambda ti: (max(0, ti - 8), ti + 1),
                   mask_bias=mk_mask(8))
    p.close_scope('F')

    p.open_scope('R')
    qkre = [p.sbuf(f"qkre{c}", [128, T], BF16, scope='R') for c in range(8)]
    vr = [p.sbuf(f"vr{c}", [128, T], BF16, scope='R') for c in range(4)]
    proj_fm(p, 'r', wqkre, ln, T, D, 1024, out_chunks=qkre)
    proj_fm(p, 'vr', wvr, ln, T, D, 512, out_chunks=vr)
    
    def get_q(h):
        return qkre[h // 2], (h % 2) * 64, [qkre[h // 2]]

    def get_k(h):
        return qkre[4 + h // 2], (h % 2) * 64, [qkre[4 + h // 2]]

    def get_v(h):
        return vr[h // 2], (h % 2) * 64, [vr[h // 2]]

    def put_out(h, ti, ps_ap, reads):
        ch, r0 = 4 + h // 2, (h % 2) * 64
        p.op('dve', lambda e: e.tensor_copy(
            ao[ch][r0:r0 + 64, ti * 128:(ti + 1) * 128], ps_ap),
            reads=reads, writes=[(ao[ch], (h, ti))])

    attention_heads(p, ctx, 'atr', 8, get_q, get_k, get_v, put_out,
                    T=T, hd_qk=64, hd_v=64, scale=float(32 ** -0.5),
                    kblock_range=lambda ti: (max(0, ti - 2), ti + 1),
                    mask_bias=mk_mask(2))
    p.close_scope('R')

    out_fm = proj_fm(p, 'o', wp, ao, T, D, D, out_dt=BF16)
    store_residual_out(p, ctx, out_fm, x, y, T=T, D=D, f_dt=BF16)




def build_l1_attn_b(T=2048):
    p = Prog("x")
    ctx = Ctx(p)
    x = p.dram_in('x', (T, 1024), F32)
    y = p.dram_out('y', (T, 1024), F32)
    emit_l1_attn_b(p, ctx, x, y, '', T)
    return p

def ref_l1_attn_b(x, w_qkv_full, w_qk_red, w_v_red, w_proj):
    xs = x.astype(np.float64)
    mu = xs.mean(-1, keepdims=True)
    z = (xs - mu) / np.sqrt(xs.var(-1, keepdims=True) + 1e-5)
    T, C = x.shape

    def windowed(q, k, v, win):
        d = q.shape[-1]
        s = np.einsum('hqd,hkd->hqk', q, k) / np.sqrt(d)
        i = np.arange(q.shape[1])[:, None]
        j = np.arange(k.shape[1])[None, :]
        s = np.where((i - j < 0) | (i - j >= win), -np.inf, s)
        e = np.exp(s - s.max(-1, keepdims=True))
        a = e / e.sum(-1, keepdims=True)
        return np.einsum('hqk,hkd->hqd', a, v)

    def split(t, d):
        return t.reshape(T, -1, d).transpose(1, 0, 2)
    q, k, v = np.split(z @ w_qkv_full.astype(np.float64), 3, axis=-1)
    yf = windowed(split(q, 64), split(k, 64), split(v, 64), 1024)
    qr, kr = np.split(z @ w_qk_red.astype(np.float64), 2, axis=-1)
    vr_ = z @ w_v_red.astype(np.float64)
    yr = windowed(split(qr, 32), split(kr, 32), split(vr_, 64), 256)
    yy = np.concatenate([yf, yr], axis=0).transpose(1, 0, 2).reshape(T, C)
    return x + (yy @ w_proj.astype(np.float64)).astype(np.float32)



"""L1_ffn: int8-dequant SwiGLU.  y = x + FFN(LN(x)).
FFN(z) = (silu(z @ wgT*s) * (z @ wuT*s)) @ wdT*s,  s = 1/127.
Weight inputs here are HOST-TRANSPOSED: wg (1024, 3072) int32, wu same,
wd (3072, 1024) int32.
"""



S127 = 1.0 / 127.0


def emit_l1_ffn(p, ctx, x, y, pf='', T=2048):
    D, H = 1024, 3072
    wg = p.dram_in_cached(pf + 'wg', (D, H), I32)
    wu = p.dram_in_cached(pf + 'wu', (D, H), I32)
    wd = p.dram_in_cached(pf + 'wd', (H, D), I32)

    stg = stage_input_ln(p, ctx, x, T=T)
    ln = stg['ln_bf']

    nh = H // 128
    h_chunks = [p.sbuf(f"h{c}", [128, T], BF16) for c in range(nh)]

    p.open_scope('up')
    wsg = WStream(p, "wg", wg, nbuf=3)
    wsu = WStream(p, "wu", wu, nbuf=3)
    ps_g = [p.psum(f"psg{i}", [128, min(T, 2048)], F32, scope='up') for i in range(1)]
    ps_u = [p.psum(f"psu{i}", [128, min(T, 2048)], F32, scope='up') for i in range(1)]
    silu_t = p.sbuf("silu_t", [128, T], F32, scope='up')
    u_t = p.sbuf("u_t", [128, T], F32, scope='up')
    nk = D // 128
    NS = 512
    for hc in range(nh):
        pg = ps_g[0]
        pu = ps_u[0]
        for kc in range(nk):
            wbg = wsg.fetch(kc * 128, hc * 128)
            wbu = wsu.fetch(kc * 128, hc * 128)
            for n0 in range(0, T, NS):
                nn = min(NS, T - n0)
                p.op('pe', lambda e: e.matmul(pg[:, n0:n0 + nn], wbg[:],
                                              ln[kc][:, n0:n0 + nn],
                                              start=(kc == 0), stop=(kc == nk - 1)),
                     reads=[wbg, ln[kc]], writes=[(pg, n0)])
                p.op('pe', lambda e: e.matmul(pu[:, n0:n0 + nn], wbu[:],
                                              ln[kc][:, n0:n0 + nn],
                                              start=(kc == 0), stop=(kc == nk - 1)),
                     reads=[wbu, ln[kc]], writes=[(pu, n0)])
        ptags = [(pg, n0) for n0 in range(0, T, NS)]
        utags = [(pu, n0) for n0 in range(0, T, NS)]
        p.op('act', lambda e: e.activation(silu_t[:], pg[:], AF.Silu, scale=S127),
             reads=ptags, writes=[silu_t])
        p.op('dve', lambda e: e.tensor_scalar(u_t[:], pu[:], S127, None, OP.mult, OP.bypass),
             reads=utags, writes=[u_t])
        p.op('dve', lambda e: e.tensor_tensor(h_chunks[hc][:], silu_t[:], u_t[:], OP.mult),
             reads=[silu_t, u_t], writes=[h_chunks[hc]])
    p.close_scope('up')

    out_fm = [p.sbuf(f"of{c}", [128, T], BF16) for c in range(D // 128)]
    p.open_scope('down')
    wsd = WStream(p, "wd", wd, nbuf=3)
    ps_d = [p.psum(f"psd{i}", [128, min(T, 2048)], F32, scope='down') for i in range(2)]
    for mc in range(D // 128):
        ps = ps_d[mc % 2]
        for kc in range(nh):
            wb = wsd.fetch(kc * 128, mc * 128)
            for n0 in range(0, T, NS):
                nn = min(NS, T - n0)
                p.op('pe', lambda e: e.matmul(ps[:, n0:n0 + nn], wb[:],
                                              h_chunks[kc][:, n0:n0 + nn],
                                              start=(kc == 0), stop=(kc == nh - 1)),
                     reads=[wb, h_chunks[kc]], writes=[(ps, n0)])
        ptags = [(ps, n0) for n0 in range(0, T, NS)]
        p.op('dve', lambda e: e.tensor_scalar(out_fm[mc][:], ps[:], S127, None,
                                              OP.mult, OP.bypass),
             reads=ptags, writes=[out_fm[mc]])
    p.close_scope('down')

    store_residual_out(p, ctx, out_fm, x, y, T=T, D=D, f_dt=BF16)




def build_l1_ffn(T=2048):
    p = Prog("x")
    ctx = Ctx(p)
    x = p.dram_in('x', (T, 1024), F32)
    y = p.dram_out('y', (T, 1024), F32)
    emit_l1_ffn(p, ctx, x, y, '', T)
    return p

def ref_l1_ffn(x, wg_nat, wu_nat, wd_nat):
    xs = x.astype(np.float64)
    mu = xs.mean(-1, keepdims=True)
    var = xs.var(-1, keepdims=True)
    z = (xs - mu) / np.sqrt(var + 1e-5)
    s = 1.0 / 127.0
    wgf = wg_nat.astype(np.float64) * s
    wuf = wu_nat.astype(np.float64) * s
    wdf = wd_nat.astype(np.float64) * s
    gate = z @ wgf.T
    up = z @ wuf.T
    sig = 1.0 / (1.0 + np.exp(-gate))
    h = gate * sig * up
    return x + (h @ wdf.T).astype(np.float32)



"""L2_attn_a: gated delta rule (chunked WY, C=128) + static path + final LN.
Host: W (1024, 1024) [W.T so x@W.T works as fm matmul]; lgam/lbet cols (128, 8);
eta (1,1); lam_logit (1,1).
"""



C = 128


def emit_l2_attn_a(p, ctx, x, y, pf='', T=2048):
    D = 1024
    nq = T // C
    W = p.dram_in_cached(pf + 'W', (D, D), F32)
    lgam = p.dram_in_cached(pf + 'lgam', (128, 8), F32)
    lbet = p.dram_in_cached(pf + 'lbet', (128, 8), F32)
    eta_in = p.dram_in_cached(pf + 'eta', (1, 1), F32)
    llog = p.dram_in_cached(pf + 'llog', (1, 1), F32)

    stg = stage_input_ln(p, ctx, x, T=T)
    ln = stg['ln_bf']
    identf = ctx.get_ident(F32)
    identb = ctx.get_ident(BF16)
    identblk = ctx.get_ident_block(64, BF16)
    ones = ctx.get_ones_col()
    gcol = load_col_vec(p, 'gc', lgam, D)
    bcol = load_col_vec(p, 'bc', lbet, D)

    sc = p.sbuf("sc", [1, 16], F32)
    p.dma('sp', lambda e: e.dma_start(out=sc[0:1, 0:1], in_=llog[:, :]), writes=[(sc, 'l')])
    p.dma('sp', lambda e: e.dma_start(out=sc[0:1, 1:2], in_=eta_in[:, :]), writes=[(sc, 'e')])
    p.op('act', lambda e: e.activation(sc[0:1, 2:3], sc[0:1, 0:1], AF.Sigmoid),
         reads=[(sc, 'l')], writes=[(sc, 'lam')])
    p.op('act', lambda e: e.activation(sc[0:1, 3:4], sc[0:1, 2:3], AF.Ln),
         reads=[(sc, 'lam')], writes=[(sc, 'lnl')])
    p.op('dve', lambda e: e.tensor_scalar(sc[0:1, 4:5], sc[0:1, 2:3], -1.0, 1.0,
                                          OP.mult, OP.add),
         reads=[(sc, 'lam')], writes=[(sc, '1ml')])
    p.op('dve', lambda e: e.tensor_tensor(sc[0:1, 5:6], sc[0:1, 4:5], sc[0:1, 1:2], OP.mult),
         reads=[(sc, '1ml'), (sc, 'e')], writes=[(sc, 'be0')])
    p.op('dve', lambda e: e.tensor_scalar(sc[0:1, 6:7], sc[0:1, 5:6], 1.0 / 64.0, None,
                                          OP.mult, OP.bypass),
         reads=[(sc, 'be0')], writes=[(sc, 'beta')])
    bccol = p.sbuf("bccol", [128, 4], F32)
    p.open_scope('bc0')
    b_ps = p.psum("b_ps", [128, 128], F32, scope='bc0')
    p.op("pe", lambda e: e.matmul(b_ps[:, 0:1], ones[0:1, :].broadcast_to((1, 128)),
                                  sc[0:1, 3:5] if False else sc[0:1, 3:4], start=True, stop=True),
         reads=[(sc, 'lnl'), ones], writes=[(b_ps, 0)])
    p.op('pe', lambda e: e.matmul(b_ps[:, 2:3], ones[0:1, :].broadcast_to((1, 128)),
                                  sc[0:1, 6:7], start=True, stop=True),
         reads=[(sc, 'beta'), ones], writes=[(b_ps, 1)])
    p.op('dve', lambda e: e.tensor_copy(bccol[:, 0:1], b_ps[:, 0:1]),
         reads=[(b_ps, 0)], writes=[(bccol, 'lnl')])
    p.op('dve', lambda e: e.tensor_copy(bccol[:, 1:2], b_ps[:, 2:3]),
         reads=[(b_ps, 1)], writes=[(bccol, 'beta')])
    p.close_scope('bc0')
    lnl_col = bccol[:, 0:1]
    beta_col = bccol[:, 1:2]

    sr = np.arange(C)
    diff = sr[:, None] - sr[None, :]
    stril = (diff > 0).astype(np.float32)
    stril_i = (diff >= 0).astype(np.float32)
    dgrid = p.const('dgrid', diff.astype(np.float32), F32)
    mask_l = p.const('mask_l', stril, F32)
    mask_li = p.const('mask_li', stril_i, F32)
    scol_c = p.const('scol', np.arange(1, C + 1, dtype=np.float32).reshape(128, 1), F32)
    srev_c = p.const('srev', np.arange(C - 1, -1, -1, dtype=np.float32).reshape(128, 1), F32)
    dg = p.sbuf("dg", [128, 128], F32)
    ml = p.sbuf("ml", [128, 128], F32)
    mli = p.sbuf("mli", [128, 128], F32)
    scolv = p.sbuf("scolv", [128, 2], F32)
    p.dma('sp', lambda e: e.dma_start(out=dg[:], in_=dgrid[:, :]), writes=[dg])
    p.dma('sp', lambda e: e.dma_start(out=ml[:], in_=mask_l[:, :]), writes=[ml])
    p.dma('sp', lambda e: e.dma_start(out=mli[:], in_=mask_li[:, :]), writes=[mli])
    p.dma('sp', lambda e: e.dma_start(out=scolv[:, 0:1], in_=scol_c[:, :]), writes=[(scolv, 0)])
    p.dma('sp', lambda e: e.dma_start(out=scolv[:, 1:2], in_=srev_c[:, :]), writes=[(scolv, 1)])
    Td = p.sbuf("Td", [128, 128], F32)
    TdT = p.sbuf("TdT", [128, 128], F32)
    Tdi = p.sbuf("Tdi", [128, 128], F32)
    TdiT = p.sbuf("TdiT", [128, 128], F32)
    tmpd = p.sbuf("tmpd", [128, 128], F32)
    p.op('dve', lambda e: e.tensor_scalar(tmpd[:], dg[:], lnl_col, None, OP.mult, OP.bypass),
         reads=[dg, (bccol, 'lnl')], writes=[tmpd])
    p.op('act', lambda e: e.activation(tmpd[:], tmpd[:], AF.Exp),
         reads=[tmpd], writes=[tmpd])
    p.op('dve', lambda e: e.tensor_tensor(Td[:], tmpd[:], ml[:], OP.mult),
         reads=[tmpd, ml], writes=[Td])
    p.op('dve', lambda e: e.tensor_tensor(Tdi[:], tmpd[:], mli[:], OP.mult),
         reads=[tmpd, mli], writes=[Tdi])
    p.op('dve', lambda e: e.tensor_scalar(tmpd[:], dg[:], lnl_col, None, OP.mult, OP.bypass),
         reads=[dg, (bccol, 'lnl')], writes=[tmpd])
    p.op('act', lambda e: e.activation(tmpd[:], tmpd[:], AF.Exp, scale=-1.0),
         reads=[tmpd], writes=[tmpd])
    mlT = p.sbuf("mlT", [128, 128], F32)
    mliT = p.sbuf("mliT", [128, 128], F32)
    p.dma('sp', lambda e: e.dma_start(out=mlT[:], in_=p.const(
        'mask_u', stril.T.copy(), F32)[:, :]), writes=[mlT])
    p.dma('sp', lambda e: e.dma_start(out=mliT[:], in_=p.const(
        'mask_ui', stril_i.T.copy(), F32)[:, :]), writes=[mliT])
    p.op('dve', lambda e: e.tensor_tensor(TdT[:], tmpd[:], mlT[:], OP.mult),
         reads=[tmpd, mlT], writes=[TdT])
    p.op('dve', lambda e: e.tensor_tensor(TdiT[:], tmpd[:], mliT[:], OP.mult),
         reads=[tmpd, mliT], writes=[TdiT])
    lcols = p.sbuf("lcols", [128, 4], F32)
    p.op('dve', lambda e: e.tensor_tensor(lcols[:, 0:1], scolv[:, 0:1], lnl_col, OP.mult),
         reads=[(scolv, 0), (bccol, 'lnl')], writes=[(lcols, 0)])
    p.op('act', lambda e: e.activation(lcols[:, 0:1], lcols[:, 0:1], AF.Exp),
         reads=[(lcols, 0)], writes=[(lcols, 0)])
    p.op('dve', lambda e: e.tensor_tensor(lcols[:, 1:2], scolv[:, 1:2], lnl_col, OP.mult),
         reads=[(scolv, 1), (bccol, 'lnl')], writes=[(lcols, 1)])
    p.op('act', lambda e: e.activation(lcols[:, 1:2], lcols[:, 1:2], AF.Exp),
         reads=[(lcols, 1)], writes=[(lcols, 1)])
    p.op('dve', lambda e: e.tensor_scalar(lcols[:, 2:3], lnl_col, float(C), None,
                                          OP.mult, OP.bypass),
         reads=[(bccol, 'lnl')], writes=[(lcols, 2)])
    p.op('act', lambda e: e.activation(lcols[:, 2:3], lcols[:, 2:3], AF.Exp),
         reads=[(lcols, 2)], writes=[(lcols, 2)])
    lam_s = lcols[:, 0:1]
    lam_rev = lcols[:, 1:2]
    lam_C = lcols[:, 2:3]

    xact = [p.sbuf(f"xact{c}", [128, T], BF16) for c in range(8)]
    for c in range(8):
        p.op('act', (lambda c_: lambda e: e.activation(xact[c_][:], ln[c_][:], AF.Tanh))(c),
             reads=[ln[c]], writes=[xact[c]])
    so = [p.sbuf(f"so{c}", [128, T], BF16) for c in range(8)]
    proj_fm(p, 'so', W, xact, T, D, D, out_chunks=so)

    p.open_scope('dr')
    G_ps = p.psum("G_ps", [128, 128], F32, scope='dr')
    n_ps = [p.psum(f"n_ps{i}", [128, 128], F32, scope='dr') for i in range(3)]
    u_ps = p.psum("u_ps", [128, 128], F32, scope='dr')
    tb_ps = p.psum("tb_ps", [128, 128], BF16, scope='dr')
    A = p.sbuf("A", [128, 128], BF16, scope='dr')
    At = p.sbuf("At", [128, 128], BF16, scope='dr')
    Xm = p.sbuf("Xm", [128, 128], BF16, scope='dr')
    Xt = p.sbuf("Xt", [128, 128], BF16, scope='dr')
    Mt = p.sbuf("Mt", [128, 128], BF16, scope='dr')
    Pn = p.sbuf("Pn", [128, 128], BF16, scope='dr')
    MoT = p.sbuf("MoT", [128, 128], BF16, scope='dr')
    Ktm = p.sbuf("Ktm", [128, 64], BF16, scope='dr')
    Vtm = p.sbuf("Vtm", [128, 64], BF16, scope='dr')
    Wm = p.sbuf("Wm", [128, 64], BF16, scope='dr')
    Uu = p.sbuf("Uu", [128, 64], BF16, scope='dr')
    Urev = p.sbuf("Urev", [128, 64], BF16, scope='dr')
    P0sb = p.sbuf("P0sb", [128, 64], F32, scope='dr')
    Osb = p.sbuf("Osb", [128, 64], BF16, scope='dr')
    S_sb = p.sbuf("S_sb", [64, 64], F32, scope='dr')
    S_bf = p.sbuf("S_bf", [128, 64], BF16, scope='dr')
    idf_sb = p.sbuf("idf_sb", [128, 128], F32, scope='dr')
    p.dma('sp', lambda e: e.dma_start(out=idf_sb[:], in_=p.const(
        'ident128', np.eye(128, dtype=np.float32), F32)[:, :]), writes=[idf_sb])

    for h in range(16):
        ch, hr = h // 2, (h % 2) * 64
        p.op('pool', lambda e: e.memset(S_sb[:], 0.0), writes=[S_sb])
        for cidx in range(nq):
            c0 = cidx * C
            kf_sl = xact[ch][hr:hr + 64, c0:c0 + C]
            vf_sl = so[ch][hr:hr + 64, c0:c0 + C]
            p.op('pe', lambda e: e.matmul(G_ps[:, 0:C], kf_sl, kf_sl, start=True, stop=True),
                 reads=[xact[ch]], writes=[G_ps])
            p.op('dve', lambda e: e.tensor_tensor(tmpd[:], G_ps[:, 0:C], Td[:], OP.mult),
                 reads=[G_ps, Td], writes=[tmpd])
            p.op('dve', lambda e: e.tensor_scalar(A[:], tmpd[:], beta_col, None,
                                                  OP.mult, OP.bypass),
                 reads=[tmpd, (bccol, 'beta')], writes=[A])
            p.op('dve', lambda e: e.tensor_tensor(tmpd[:], G_ps[:, 0:C], TdT[:], OP.mult),
                 reads=[G_ps, TdT], writes=[tmpd])
            p.op('dve', lambda e: e.tensor_scalar(At[:], tmpd[:], beta_col, None,
                                                  OP.mult, OP.bypass),
                 reads=[tmpd, (bccol, 'beta')], writes=[At])
            p.op('dve', lambda e: e.tensor_tensor(MoT[:], G_ps[:, 0:C], TdiT[:], OP.mult),
                 reads=[G_ps, TdiT], writes=[MoT])
            p.op('dve', lambda e: e.tensor_tensor(Mt[:], idf_sb[:], At[:], OP.add),
                 reads=[idf_sb, At], writes=[Mt])
            p.op('dve', lambda e: e.tensor_tensor(Xm[:], idf_sb[:], A[:], OP.subtract),
                 reads=[idf_sb, A], writes=[Xm])
            p.op('dve', lambda e: e.tensor_tensor(Xt[:], idf_sb[:], At[:], OP.subtract),
                 reads=[idf_sb, At], writes=[Xt])
            for it in range(6):
                p.op('pe', lambda e: e.matmul(n_ps[0][:, 0:C], Mt[:], Xm[:],
                                              start=True, stop=True),
                     reads=[Mt, Xm], writes=[n_ps[0]])
                p.op('dve', lambda e: e.tensor_copy(Pn[:], n_ps[0][:, 0:C]),
                     reads=[n_ps[0]], writes=[Pn])
                p.op('pe', lambda e: e.matmul(n_ps[1][:, 0:C], Xt[:], Pn[:],
                                              start=True, stop=True),
                     reads=[Xt, Pn], writes=[n_ps[1]])
                p.op('pe', lambda e: e.matmul(n_ps[2][:, 0:C], Pn[:], Xt[:],
                                              start=True, stop=True),
                     reads=[Pn, Xt], writes=[n_ps[2]])
                p.op('dve', lambda e: e.scalar_tensor_tensor(
                    Xm[:], Xm[:], 2.0, n_ps[1][:, 0:C], OP.mult, OP.subtract),
                    reads=[Xm, n_ps[1]], writes=[Xm])
                p.op('dve', lambda e: e.scalar_tensor_tensor(
                    Xt[:], Xt[:], 2.0, n_ps[2][:, 0:C], OP.mult, OP.subtract),
                    reads=[Xt, n_ps[2]], writes=[Xt])
            p.op('pe', lambda e: e.transpose(tb_ps[0:128, 0:64], kf_sl,
                                             identblk[hr:hr + 64, 0:64]),
                 reads=[xact[ch], identblk], writes=[tb_ps])
            p.op('dve', lambda e: e.tensor_copy(Ktm[:, 0:64], tb_ps[0:128, 0:64]),
                 reads=[tb_ps], writes=[Ktm])
            p.op('pe', lambda e: e.transpose(tb_ps[0:128, 0:64], vf_sl,
                                             identblk[hr:hr + 64, 0:64]),
                 reads=[so[ch], identblk], writes=[tb_ps])
            p.op('dve', lambda e: e.tensor_copy(Vtm[:, 0:64], tb_ps[0:128, 0:64]),
                 reads=[tb_ps], writes=[Vtm])
            p.op('act', (lambda hr_: lambda e: e.copy(S_bf[hr_:hr_ + 64, :], S_sb[:]))(hr),
                 reads=[S_sb], writes=[S_bf])
            p.op('pe', (lambda hr_: lambda e: e.matmul(
                u_ps[0:C, 0:64], kf_sl, S_bf[hr_:hr_ + 64, :],
                start=True, stop=True))(hr),
                 reads=[xact[ch], S_bf], writes=[u_ps])
            p.op('dve', lambda e: e.tensor_copy(P0sb[:, 0:64], u_ps[0:C, 0:64]),
                 reads=[u_ps], writes=[P0sb])
            p.op('dve', lambda e: e.scalar_tensor_tensor(
                Wm[:, 0:64], P0sb[:, 0:64], lam_s, Vtm[:, 0:64], OP.mult, OP.subtract),
                reads=[P0sb, (lcols, 0), Vtm], writes=[(Wm, 'pre')])
            p.op('dve', lambda e: e.tensor_scalar(Wm[:, 0:64], Wm[:, 0:64], beta_col,
                                                  -1.0, OP.mult, OP.mult),
                 reads=[(Wm, 'pre'), (bccol, 'beta')], writes=[Wm])
            p.op('pe', lambda e: e.matmul(u_ps[0:C, 0:64], Xt[:], Wm[:, 0:64],
                                          start=True, stop=True),
                 reads=[Xt, Wm], writes=[u_ps])
            p.op('dve', lambda e: e.tensor_copy(Uu[:, 0:64], u_ps[0:C, 0:64]),
                 reads=[u_ps], writes=[Uu])
            p.op('pe', lambda e: e.matmul(u_ps[0:C, 0:64], MoT[:], Uu[:, 0:64],
                                          start=True, stop=True),
                 reads=[MoT, Uu], writes=[u_ps])
            p.op('dve', lambda e: e.scalar_tensor_tensor(
                Osb[:, 0:64], P0sb[:, 0:64], lam_s, u_ps[0:C, 0:64], OP.mult, OP.add),
                reads=[P0sb, (lcols, 0), u_ps], writes=[Osb])
            p.op('pe', lambda e: e.transpose(tb_ps[0:64, 0:128], Osb[:, 0:64], identb[:]),
                 reads=[Osb, identb], writes=[tb_ps])
            p.op('dve', (lambda c0_: lambda e: e.tensor_tensor(
                so[ch][hr:hr + 64, c0_:c0_ + C], so[ch][hr:hr + 64, c0_:c0_ + C],
                tb_ps[0:64, 0:128], OP.add))(c0),
                reads=[so[ch], tb_ps], writes=[so[ch]])
            p.op('dve', lambda e: e.tensor_scalar(Urev[:, 0:64], Uu[:, 0:64], lam_rev,
                                                  None, OP.mult, OP.bypass),
                 reads=[Uu, (lcols, 1)], writes=[Urev])
            p.op('pe', lambda e: e.matmul(u_ps[0:64, 0:64], Ktm[:, 0:64], Urev[:, 0:64],
                                          start=True, stop=True),
                 reads=[Ktm, Urev], writes=[u_ps])
            p.op('dve', lambda e: e.scalar_tensor_tensor(
                S_sb[:], S_sb[:], lam_C[0:64, :], u_ps[0:64, 0:64], OP.mult, OP.add),
                reads=[S_sb, (lcols, 2), u_ps], writes=[S_sb])
    p.close_scope('dr')

    out_fm = [p.sbuf(f"of{c}", [128, T], BF16) for c in range(8)]
    ln_fm_sbuf(p, ctx, so, out_fm, T, D, gcol=gcol, bcol=bcol, eps=1e-5)
    store_residual_out(p, ctx, out_fm, x, y, T=T, D=D, f_dt=BF16)




def build_l2_attn_a(T=2048):
    p = Prog("x")
    ctx = Ctx(p)
    x = p.dram_in('x', (T, 1024), F32)
    y = p.dram_out('y', (T, 1024), F32)
    emit_l2_attn_a(p, ctx, x, y, '', T)
    return p

def ref_l2_attn_a(x, W, ln_gamma, ln_beta, eta, lam_logit):
    xs = x.astype(np.float64)
    mu = xs.mean(-1, keepdims=True)
    z = (xs - mu) / np.sqrt(xs.var(-1, keepdims=True) + 1e-5)
    T, D = x.shape
    H, d = 16, 64
    lam = 1 / (1 + np.exp(-lam_logit))
    beta = eta * (1 - lam) / d
    xa = np.tanh(z)
    so = xa @ W.T.astype(np.float64)

    def heads(t):
        return t.reshape(T, H, d).transpose(1, 0, 2)
    q = heads(xa)
    v = heads(so)
    o = np.zeros_like(v)
    for h in range(H):
        S = np.zeros((d, d))
        for t in range(T):
            kt, vt, qt = q[h, t], v[h, t], q[h, t]
            S = lam * S
            u = beta * (vt - S.T @ kt)
            S = S + np.outer(kt, u)
            o[h, t] = S.T @ qt
    yv = so + o.transpose(1, 0, 2).reshape(T, D)
    mu2 = yv.mean(-1, keepdims=True)
    v2 = ((yv - mu2) ** 2).mean(-1, keepdims=True)
    return x + ((yv - mu2) / np.sqrt(v2 + 1e-5) * ln_gamma + ln_beta).astype(np.float32)



"""L2_attn_b: 4 stacked Mamba mixers + final LN, selective scan via
tensor_tensor_scan.  See host layout builder make_l2b_inputs below.
"""



DI = 2048
NST = 16
DTR = 64
NM = 4


def emit_l2_attn_b(p, ctx, x, y, pf='', T=2048):
    D = 1024
    ndc = DI // 128
    nwc_d = p.dram_in_cached(pf + 'nwc', (NM * 128, 8), F32)
    nbc_d = p.dram_in_cached(pf + 'nbc', (NM * 128, 8), F32)
    inpw = p.dram_in_cached(pf + 'inpw', (NM * D, 2 * DI), F32)
    convc_d = p.dram_in_cached(pf + 'convc', (NM * 128, 4 * ndc), F32)
    convbc_d = p.dram_in_cached(pf + 'convbc', (NM * 128, ndc), F32)
    xpw = p.dram_in_cached(pf + 'xpw', (NM * DI, 96), F32)
    dtw = p.dram_in_cached(pf + 'dtw', (NM * DTR, DI), F32)
    dtbc_d = p.dram_in_cached(pf + 'dtbc', (NM * 128, ndc), F32)
    alogc_d = p.dram_in_cached(pf + 'alogc', (NM * 128, NST * ndc), F32)
    dskc_d = p.dram_in_cached(pf + 'dskc', (NM * 128, ndc), F32)
    outw = p.dram_in_cached(pf + 'outw', (NM * DI, D), F32)
    nfwc_d = p.dram_in_cached(pf + 'nfwc', (128, 8), F32)
    nfbc_d = p.dram_in_cached(pf + 'nfbc', (128, 8), F32)

    stg = stage_input_ln(p, ctx, x, T=T)
    hf = stg['ln_bf']
    identb = ctx.get_ident(BF16)
    nfw = load_col_vec(p, 'nfw', nfwc_d, D)
    nfb = load_col_vec(p, 'nfb', nfbc_d, D)
    selB_np = np.zeros((32, 16 * 128), np.float32)
    selC_np = np.zeros((32, 16 * 128), np.float32)
    for n in range(16):
        selB_np[n, n * 128:(n + 1) * 128] = 1
        selC_np[16 + n, n * 128:(n + 1) * 128] = 1
    selB = p.sbuf("selB", [32, 16 * 128], BF16)
    selC = p.sbuf("selC", [32, 16 * 128], BF16)
    p.dma('pool', lambda e: e.dma_start(out=selB[:], in_=p.const('selBc', selB_np, F32)[:, :]),
          writes=[selB])
    p.dma('pool', lambda e: e.dma_start(out=selC[:], in_=p.const('selCc', selC_np, F32)[:, :]),
          writes=[selC])

    rf = [p.sbuf(f"rf{c}", [128, T], BF16) for c in range(8)]
    for c in range(8):
        p.op('pool', (lambda c_: lambda e: e.memset(rf[c_][:], 0.0))(c), writes=[rf[c]])

    TB = min(512, T)
    ntb = T // TB
    nsub = TB // 128

    for m in range(NM):
        for c in range(8):
            p.op('dve', (lambda c_: lambda e: e.tensor_tensor(
                rf[c_][:], rf[c_][:], hf[c_][:], OP.add))(c),
                reads=[rf[c], hf[c]], writes=[rf[c]])
        p.open_scope('mx')
        nwcol = p.sbuf("nwcol", [128, 8], F32, scope='mx')
        nbcol = p.sbuf("nbcol", [128, 8], F32, scope='mx')
        convw = p.sbuf("convw", [128, 4 * ndc], F32, scope='mx')
        convb = p.sbuf("convb", [128, ndc], F32, scope='mx')
        dtb = p.sbuf("dtb", [128, ndc], F32, scope='mx')
        dsk = p.sbuf("dsk", [128, ndc], F32, scope='mx')
        alog = p.sbuf("alog", [128, NST * ndc], F32, scope='mx')
        expa = p.sbuf("expa", [128, NST * ndc], F32, scope='mx')
        for t_, src_ in [(nwcol, nwc_d), (nbcol, nbc_d), (convw, convc_d),
                         (convb, convbc_d), (dtb, dtbc_d), (dsk, dskc_d),
                         (alog, alogc_d)]:
            p.dma('sp', (lambda t2, s2: lambda e: e.dma_start(
                out=t2[:], in_=s2[m * 128:(m + 1) * 128, :]))(t_, src_), writes=[t_])
        p.op('act', lambda e: e.activation(expa[:], alog[:], AF.Exp),
             reads=[alog], writes=[expa])
        xpw_sb = [p.sbuf(f"xpw{kc}", [128, 96], BF16, scope='mx') for kc in range(ndc)]
        for kc in range(ndc):
            p.dma('pool', (lambda kc_: lambda e: e.dma_start(
                out=xpw_sb[kc_][:], in_=xpw[m * DI + kc_ * 128:m * DI + (kc_ + 1) * 128, :]))(kc),
                writes=[xpw_sb[kc]])
        dtw_sb = p.sbuf("dtw_sb", [DTR, DI], F32, scope='mx')
        p.dma('sp', lambda e: e.dma_start(out=dtw_sb[:], in_=dtw[m * DTR:(m + 1) * DTR, :]),
              writes=[dtw_sb])
        state = p.sbuf("state", [128, ndc * NST], F32, scope='mx')
        p.op('pool', lambda e: e.memset(state[:], 0.0), writes=[state])
        ctail = p.sbuf("ctail", [128, 4 * ndc], BF16, scope='mx')
        p.op('pool', lambda e: e.memset(ctail[:], 0.0), writes=[ctail])

        for tbi in range(ntb):
            tb0 = tbi * TB
            p.open_scope('tb')
            xm = [p.sbuf(f"xm{c}", [128, TB], BF16, scope='tb') for c in range(8)]
            ln_fm_sbuf(p, ctx, rf, xm, T, D, gcol=nwcol, bcol=nbcol, eps=1e-5,
                       ti0=tb0 // 128, ti1=tb0 // 128 + nsub, out_off0=True)
            xcr = [p.sbuf(f"xcr{dc}", [128, TB], BF16, scope='tb') for dc in range(ndc)]
            bc32 = p.sbuf("bc32", [32, TB], BF16, scope='tb')
            dt_sb = [p.sbuf(f"dt{dc}", [128, TB], BF16, scope='tb') for dc in range(ndc)]
            zsil = [p.sbuf(f"zs{dc}", [128, TB], BF16, scope='tb') for dc in range(ndc)]

            def epi_in(mcj, ps, ptags):
                if mcj < ndc:
                    p.op('dve', (lambda mm: lambda e: e.tensor_copy(
                        xcr[mm][:], ps[:, 0:TB]))(mcj), reads=ptags, writes=[xcr[mcj]])
                else:
                    p.op('act', (lambda mm: lambda e: e.activation(
                        zsil[mm][:], ps[:, 0:TB], AF.Silu))(mcj - ndc),
                        reads=ptags, writes=[zsil[mcj - ndc]])

            proj_fm(p, 'ip', inpw[m * D:(m + 1) * D, :], xm, TB, D, 2 * DI,
                    epilogue=epi_in, fetch_w=512)
            xcs = [p.sbuf(f"xcs{dc}", [128, TB], BF16, scope='tb') for dc in range(ndc)]
            p.open_scope('cv')
            cpad = p.sbuf("cpad", [128, TB + 4], BF16, scope='cv')
            cacc = p.sbuf("cacc", [128, TB], F32, scope='cv')
            for dc in range(ndc):
                p.op('dve', (lambda dc_: lambda e: e.tensor_copy(
                    cpad[:, 0:3], ctail[:, dc_ * 4:dc_ * 4 + 3]))(dc),
                    reads=[(ctail, dc)], writes=[(cpad, 'tail')])
                p.op('dve', (lambda dc_: lambda e: e.tensor_copy(
                    cpad[:, 3:TB + 3], xcr[dc_][:]))(dc),
                    reads=[xcr[dc]], writes=[(cpad, 'm')])
                p.op('dve', (lambda dc_: lambda e: e.tensor_scalar(
                    cacc[:], cpad[:, 0:TB], convw[:, dc_ * 4:dc_ * 4 + 1],
                    convb[:, dc_:dc_ + 1], OP.mult, OP.add))(dc),
                    reads=[(cpad, 'tail'), (cpad, 'm'), convw, convb], writes=[cacc])
                for k in range(1, 4):
                    p.op('dve', (lambda dc_, k_: lambda e: e.scalar_tensor_tensor(
                        cacc[:], cpad[:, k_:k_ + TB], convw[:, dc_ * 4 + k_:dc_ * 4 + k_ + 1],
                        cacc[:], OP.mult, OP.add))(dc, k),
                        reads=[(cpad, 'tail'), (cpad, 'm'), convw, cacc], writes=[cacc])
                p.op('act', (lambda dc_: lambda e: e.activation(
                    xcs[dc_][:], cacc[:], AF.Silu))(dc),
                    reads=[cacc], writes=[xcs[dc]])
                p.op('dve', (lambda dc_: lambda e: e.tensor_copy(
                    ctail[:, dc_ * 4:dc_ * 4 + 3], xcr[dc_][:, TB - 3:TB]))(dc),
                    reads=[xcr[dc]], writes=[(ctail, dc)])
            p.close_scope('cv')
            p.open_scope('xp')
            xp_ps = p.psum("xp_ps", [128, TB], F32, scope='xp')
            xdb = p.sbuf("xdb", [96, TB], F32, scope='xp')
            for kc in range(ndc):
                p.op('pe', (lambda kc_: lambda e: e.matmul(
                    xp_ps[0:96, 0:TB], xpw_sb[kc_][:], xcs[kc_][:],
                    start=(kc_ == 0), stop=(kc_ == ndc - 1)))(kc),
                    reads=[xpw_sb[kc], xcs[kc]], writes=[xp_ps])
            p.op('dve', lambda e: e.tensor_copy(xdb[:], xp_ps[0:96, 0:TB]),
                 reads=[xp_ps], writes=[xdb])
            p.op('dve', lambda e: e.tensor_copy(bc32[:], xdb[64:96, 0:TB]),
                 reads=[xdb], writes=[bc32])
            sp1 = p.sbuf("sp1", [128, TB], F32, scope='xp')
            sp2 = p.sbuf("sp2", [128, TB], F32, scope='xp')
            one_b = ctx.cval(1.0)
            for dc in range(ndc):
                p.op('pe', (lambda dc_: lambda e: e.matmul(
                    xp_ps[:, 0:TB], dtw_sb[0:DTR, dc_ * 128:(dc_ + 1) * 128],
                    xdb[0:DTR, 0:TB], start=True, stop=True))(dc),
                    reads=[dtw_sb, xdb], writes=[xp_ps])
                p.op('dve', (lambda dc_: lambda e: e.tensor_scalar(
                    sp1[:], xp_ps[:, 0:TB], dtb[:, dc_:dc_ + 1], None, OP.add, OP.bypass))(dc),
                    reads=[xp_ps, dtb], writes=[sp1])
                p.op('act', lambda e: e.activation(sp2[:], sp1[:], AF.Abs),
                     reads=[sp1], writes=[(sp2, 'a')])
                p.op('act', lambda e: e.activation(sp2[:], sp2[:], AF.Exp, scale=-1.0),
                     reads=[(sp2, 'a')], writes=[(sp2, 'e')])
                p.op('act', lambda e: e.activation(sp2[:], sp2[:], AF.Ln, bias=one_b[:, 0:1]),
                     reads=[(sp2, 'e'), one_b], writes=[(sp2, 'l')])
                p.op('dve', lambda e: e.tensor_scalar(sp1[:], sp1[:], 0.0, None,
                                                      OP.max, OP.bypass),
                     reads=[sp1], writes=[(sp1, 'mx')])
                p.op('dve', (lambda dc_: lambda e: e.tensor_tensor(
                    dt_sb[dc_][:], sp1[:], sp2[:], OP.add))(dc),
                    reads=[(sp1, 'mx'), (sp2, 'l')], writes=[dt_sb[dc]])
                p.op('dve', (lambda dc_: lambda e: e.tensor_tensor(
                    xcr[dc_][:], dt_sb[dc_][:], xcs[dc_][:], OP.mult))(dc),
                    reads=[dt_sb[dc], xcs[dc]], writes=[xcr[dc]])
            p.close_scope('xp')
            p.open_scope('sc')
            yacc = [p.sbuf(f"ya{dc}", [128, TB], BF16, scope='sc') for dc in range(ndc)]
            b_ps = p.psum("b_ps", [128, TB], F32, scope='sc')
            c_ps = p.psum("c_ps", [128, TB], F32, scope='sc')
            a_sb = p.sbuf("a_sb", [128, TB], F32, scope='sc')
            b_sb = p.sbuf("b_sb", [128, TB], BF16, scope='sc')
            h_sb = p.sbuf("h_sb", [128, TB], F32, scope='sc')
            z_sb = p.sbuf("z_sb", [128, TB], BF16, scope='sc')
            for n in range(NST):
                p.op('pe', (lambda n_: lambda e: e.matmul(
                    b_ps[:, 0:TB], selB[:, n_ * 128:(n_ + 1) * 128], bc32[:],
                    start=True, stop=True))(n),
                    reads=[selB, bc32], writes=[b_ps])
                p.op('pe', (lambda n_: lambda e: e.matmul(
                    c_ps[:, 0:TB], selC[:, n_ * 128:(n_ + 1) * 128], bc32[:],
                    start=True, stop=True))(n),
                    reads=[selC, bc32], writes=[c_ps])
                for dc in range(ndc):
                    col = dc * NST + n
                    p.op('dve', (lambda dc_, n_: lambda e: e.tensor_scalar(
                        a_sb[:], dt_sb[dc_][:], expa[:, n_ * ndc + dc_:n_ * ndc + dc_ + 1],
                        None, OP.mult, OP.bypass))(dc, n),
                        reads=[dt_sb[dc], expa], writes=[(a_sb, 'm')])
                    p.op('act', lambda e: e.activation(a_sb[:], a_sb[:], AF.Exp, scale=-1.0),
                         reads=[(a_sb, 'm')], writes=[(a_sb, 'e')])
                    p.op('dve', (lambda dc_: lambda e: e.tensor_tensor(
                        b_sb[:], xcr[dc_][:], b_ps[:, 0:TB], OP.mult))(dc),
                        reads=[xcr[dc], b_ps], writes=[b_sb])
                    p.op('dve', (lambda col_: lambda e: e.tensor_tensor_scan(
                        h_sb[:], a_sb[:], b_sb[:], state[:, col_:col_ + 1],
                        OP.mult, OP.add))(col),
                        reads=[(a_sb, 'e'), b_sb, (state, col)], writes=[h_sb])
                    p.op('dve', (lambda col_: lambda e: e.tensor_copy(
                        state[:, col_:col_ + 1], h_sb[:, TB - 1:TB]))(col),
                        reads=[h_sb], writes=[(state, col)])
                    p.op('dve', lambda e: e.tensor_tensor(z_sb[:], h_sb[:],
                                                          c_ps[:, 0:TB], OP.mult),
                         reads=[h_sb, c_ps], writes=[z_sb])
                    if n == 0:
                        p.op('dve', (lambda dc_: lambda e: e.tensor_copy(
                            yacc[dc_][:], z_sb[:]))(dc),
                            reads=[z_sb], writes=[yacc[dc]])
                    else:
                        p.op('dve', (lambda dc_: lambda e: e.tensor_tensor(
                            yacc[dc_][:], yacc[dc_][:], z_sb[:], OP.add))(dc),
                            reads=[yacc[dc], z_sb], writes=[yacc[dc]])
            for dc in range(ndc):
                p.op('dve', (lambda dc_: lambda e: e.scalar_tensor_tensor(
                    h_sb[:], xcs[dc_][:], dsk[:, dc_:dc_ + 1], yacc[dc_][:],
                    OP.mult, OP.add))(dc),
                    reads=[xcs[dc], dsk, yacc[dc]], writes=[h_sb])
                p.op('dve', (lambda dc_: lambda e: e.tensor_tensor(
                    xcs[dc_][:], h_sb[:], zsil[dc_][:], OP.mult))(dc),
                    reads=[h_sb, zsil[dc]], writes=[xcs[dc]])
            p.close_scope('sc')
            p.open_scope('op')

            def epi_out(mcj, ps, ptags):
                p.op('dve', (lambda mm: lambda e: e.tensor_copy(
                    hf[mm][:, tb0:tb0 + TB], ps[:, 0:TB]))(mcj),
                    reads=ptags, writes=[(hf[mcj], tb0)])

            proj_fm(p, 'op', outw[m * DI:(m + 1) * DI, :], xcs, TB, DI, D,
                    epilogue=epi_out, fetch_w=256)
            p.close_scope('op')
            p.close_scope('tb')
        p.close_scope('mx')

    for c in range(8):
        p.op('dve', (lambda c_: lambda e: e.tensor_tensor(
            rf[c_][:], rf[c_][:], hf[c_][:], OP.add))(c),
            reads=[rf[c], hf[c]], writes=[rf[c]])
    out_fm = [p.sbuf(f"of{c}", [128, T], BF16) for c in range(8)]
    ln_fm_sbuf(p, ctx, rf, out_fm, T, D, gcol=nfw, bcol=nfb, eps=1e-5)
    store_residual_out(p, ctx, out_fm, x, y, T=T, D=D, f_dt=BF16)




def build_l2_attn_b(T=2048):
    p = Prog("x")
    ctx = Ctx(p)
    x = p.dram_in('x', (T, 1024), F32)
    y = p.dram_out('y', (T, 1024), F32)
    emit_l2_attn_b(p, ctx, x, y, '', T)
    return p

def make_l2b_inputs(x, norm_w, norm_b, in_proj_w, conv_w, conv_b, x_proj_w,
                    dt_proj_w, dt_proj_b, A_log, D_skip, out_proj_w, normf_w, normf_b):
    NMl = 4
    ins = {'x': x}
    ins['nwc'] = np.concatenate([col_of(norm_w[m]) for m in range(NMl)], 0)
    ins['nbc'] = np.concatenate([col_of(norm_b[m]) for m in range(NMl)], 0)
    ins['inpw'] = np.concatenate([np.ascontiguousarray(in_proj_w[m].T) for m in range(NMl)], 0)
    cc = np.zeros((NMl * 128, 4 * 16), np.float32)
    for m in range(NMl):
        for k in range(4):
            cw = col_of(conv_w[m, :, k])
            for dc in range(16):
                cc[m * 128:(m + 1) * 128, dc * 4 + k] = cw[:, dc]
    ins['convc'] = cc
    ins['convbc'] = np.concatenate([col_of(conv_b[m]) for m in range(NMl)], 0)
    ins['xpw'] = np.concatenate([np.ascontiguousarray(x_proj_w[m].T) for m in range(NMl)], 0)
    ins['dtw'] = np.concatenate([np.ascontiguousarray(dt_proj_w[m].T) for m in range(NMl)], 0)
    ins['dtbc'] = np.concatenate([col_of(dt_proj_b[m]) for m in range(NMl)], 0)
    al = np.zeros((NMl * 128, 16 * 16), np.float32)
    for m in range(NMl):
        for n in range(16):
            ac = col_of(A_log[m, :, n])
            al[m * 128:(m + 1) * 128, n * 16:(n + 1) * 16] = ac
    ins['alogc'] = al
    ins['dskc'] = np.concatenate([col_of(D_skip[m]) for m in range(NMl)], 0)
    ins['outw'] = np.concatenate([np.ascontiguousarray(out_proj_w[m].T) for m in range(NMl)], 0)
    ins['nfwc'] = col_of(normf_w)
    ins['nfbc'] = col_of(normf_b)
    return ins


def ref_l2_attn_b(x, norm_w, norm_b, in_proj_w, conv_w, conv_b, x_proj_w,
                  dt_proj_w, dt_proj_b, A_log, D_skip, out_proj_w, normf_w, normf_b):
    xs = x.astype(np.float64)
    mu = xs.mean(-1, keepdims=True)
    z0 = (xs - mu) / np.sqrt(xs.var(-1, keepdims=True) + 1e-5)
    EPS = 1e-5

    def lnorm(v, w, b):
        mu2 = v.mean(-1, keepdims=True)
        var = ((v - mu2) ** 2).mean(-1, keepdims=True)
        return (v - mu2) / np.sqrt(var + EPS) * w + b

    def mixer(xm, m):
        xz = xm @ in_proj_w[m].T.astype(np.float64)
        xc, zz = np.split(xz, 2, axis=-1)
        K = 4
        xp = np.pad(xc, ((K - 1, 0), (0, 0)))
        yv = conv_b[m].astype(np.float64).copy()
        acc = np.zeros_like(xc) + conv_b[m]
        for k in range(K):
            acc = acc + xp[k:k + xc.shape[0], :] * conv_w[m, :, k]
        xcs = acc / (1 + np.exp(-acc))
        xdb = xcs @ x_proj_w[m].T.astype(np.float64)
        pre = xdb[:, :64] @ dt_proj_w[m].T.astype(np.float64) + dt_proj_b[m]
        dt = np.log1p(np.exp(-np.abs(pre))) + np.maximum(pre, 0)
        Bm = xdb[:, 64:80]
        Cm = xdb[:, 80:96]
        A = -np.exp(A_log[m].astype(np.float64))
        Tn, Di = xc.shape
        h = np.zeros((Di, 16))
        ys = np.zeros((Tn, Di))
        for t in range(Tn):
            dA = np.exp(dt[t][:, None] * A)
            h = dA * h + (dt[t] * xcs[t])[:, None] * Bm[t][None, :]
            ys[t] = h @ Cm[t]
        yv = (ys + xcs * D_skip[m]) * (zz / (1 + np.exp(-zz)))
        return yv @ out_proj_w[m].T.astype(np.float64)

    h = z0
    r = np.zeros_like(z0)
    for m in range(4):
        r = h + r
        xm = lnorm(r, norm_w[m], norm_b[m])
        h = mixer(xm, m)
    out = lnorm(h + r, normf_w, normf_b)
    return x + out.astype(np.float32)



"""L2_ffn: dense MoE, 8 experts, exact top-2 gating via dense weighted sum.
Host layout: gw (1024, 8) [gate_w.T]; gb (1, 8); w1 (8*1024, 4096) natural;
b1c (8*128, 32) [col_of per expert]; w2 (8*4096, 1024) natural; b2c (8*128, 8).
"""




def emit_l2_ffn(p, ctx, x, y, pf='', T=2048):
    D, H, E = 1024, 4096, 8
    gw = p.dram_in_cached(pf + 'gw', (D, E), F32)
    gb = p.dram_in_cached(pf + 'gb', (1, E), F32)
    w1 = p.dram_in_cached(pf + 'w1', (E * D, H), F32)
    b1c = p.dram_in_cached(pf + 'b1c', (E * 128, H // 128), F32)
    w2 = p.dram_in_cached(pf + 'w2', (E * H, D), F32)
    b2c = p.dram_in_cached(pf + 'b2c', (E * 128, D // 128), F32)

    p.open_scope('f32')
    stg = stage_input_ln(p, ctx, x, T=T, keep_f32=True, f32_scope='f32')
    ln = stg['ln_bf']
    lnf = stg['ln_f32']
    ident = ctx.get_ident(BF16)
    ones = ctx.get_ones_col()
    nt = T // 128

    gdram = p.dram_tmp('gdram', (E, T), F32)
    p.open_scope('gate')
    gwt = p.sbuf("gwt", [128, 8 * E], F32, scope='gate')
    for kc in range(8):
        p.dma('pool', lambda e: e.dma_start(out=gwt[:, kc * E:(kc + 1) * E],
                                            in_=gw[kc * 128:(kc + 1) * 128, :]),
              writes=[(gwt, kc)])
    gbt = p.sbuf("gbt", [1, E], F32, scope='gate')
    p.dma('sp', lambda e: e.dma_start(out=gbt[:], in_=gb[:, :]), writes=[gbt])
    g_ps = p.psum("g_ps", [128, 512], F32, scope='gate')
    t_ps = p.psum("t_ps", [128, 128], F32, scope='gate')
    grow = p.sbuf("grow", [128, 4], F32, scope='gate')
    t8 = p.sbuf("t8", [128, 8], F32, scope='gate')
    msk = p.sbuf("msk", [128, 8], F32, scope='gate')
    gtile = p.sbuf("gtile", [128, 8], F32, scope='gate')
    g8 = p.sbuf("g8", [8, T], F32, scope='gate')
    identf = ctx.get_ident(F32)
    gbias_bc = p.sbuf("gbias_bc", [128, 8], F32, scope='gate')
    p.op('pe', lambda e: e.matmul(g_ps[:, 8:8 + E], ones[0:1, :].broadcast_to((1, 128)),
                                  gbt[:], start=True, stop=True),
         reads=[gbt, ones], writes=[(g_ps, 'b')])
    p.op('dve', lambda e: e.tensor_copy(gbias_bc[:], g_ps[:, 8:8 + E]),
         reads=[(g_ps, 'b')], writes=[gbias_bc])
    for ti in range(nt):
        for kc in range(8):
            p.op('pe', lambda e: e.matmul(
                g_ps[:, 0:E], lnf[kc][:, ti * 128:(ti + 1) * 128], gwt[:, kc * E:(kc + 1) * E],
                start=(kc == 0), stop=(kc == 7)),
                reads=[(lnf[kc], ti), (gwt, kc)], writes=[(g_ps, 'g')])
        p.op('dve', lambda e: e.tensor_tensor(gtile[:], g_ps[:, 0:E], gbias_bc[:], OP.add),
             reads=[(g_ps, 'g'), gbias_bc], writes=[gtile])
        p.op('dve', lambda e: e.tensor_reduce(grow[:, 0:1], gtile[:], AX.X, OP.max),
             reads=[gtile], writes=[(grow, 'm')])
        p.op('dve', lambda e: e.tensor_scalar(grow[:, 1:2], grow[:, 0:1], -1.0, None,
                                              OP.mult, OP.bypass),
             reads=[(grow, 'm')], writes=[(grow, 'nm')])
        p.op('act', lambda e: e.activation(gtile[:], gtile[:], AF.Exp, bias=grow[:, 1:2],
                                           accum_out=grow[:, 2:3]),
             reads=[gtile, (grow, 'nm')], writes=[gtile, (grow, 's')])
        p.op('dve', lambda e: e.reciprocal(grow[:, 3:4], grow[:, 2:3]),
             reads=[(grow, 's')], writes=[(grow, 'r')])
        p.op('dve', lambda e: e.tensor_scalar(gtile[:], gtile[:], grow[:, 3:4], None,
                                              OP.mult, OP.bypass),
             reads=[gtile, (grow, 'r')], writes=[gtile])
        p.op('dve', lambda e: e.max(t8[:], gtile[:]), reads=[gtile], writes=[t8])
        p.op('dve', lambda e: e.tensor_scalar(msk[:], gtile[:], t8[:, 1:2], None,
                                              OP.is_ge, OP.bypass),
             reads=[gtile, t8], writes=[msk])
        p.op('dve', lambda e: e.tensor_tensor(gtile[:], gtile[:], msk[:], OP.mult),
             reads=[gtile, msk], writes=[gtile])
        p.op('pe', lambda e: e.transpose(t_ps[0:8, :], gtile[:], identf[:]),
             reads=[gtile, identf], writes=[t_ps])
        p.op('dve', lambda e: e.tensor_copy(g8[0:8, ti * 128:(ti + 1) * 128], t_ps[0:8, :]),
             reads=[t_ps], writes=[(g8, ti)])
    p.dma('sp', lambda e: e.dma_start(out=gdram[:, :], in_=g8[:]),
          reads=[(g8, ti) for ti in range(nt)])
    p.close_scope('gate')
    p.close_scope('f32')

    b1t = p.sbuf("b1t", [128, E * (H // 128)], F32)
    b2t = p.sbuf("b2t", [128, E * (D // 128)], F32)
    for e in range(E):
        p.dma('sp', lambda en: en.dma_start(
            out=b1t[:, e * (H // 128):(e + 1) * (H // 128)],
            in_=b1c[e * 128:(e + 1) * 128, :]), writes=[(b1t, e)])
        p.dma('sp', lambda en: en.dma_start(
            out=b2t[:, e * (D // 128):(e + 1) * (D // 128)],
            in_=b2c[e * 128:(e + 1) * 128, :]), writes=[(b2t, e)])

    acc = [p.sbuf(f"acc{mc}", [128, T], F32) for mc in range(D // 128)]
    first = [True] * (D // 128)
    TB = min(512, T)
    nh = H // 128
    for e in range(E):
        p.open_scope('bce')
        bc_sb = p.sbuf("bc_sb", [128, T], F32, scope='bce')
        gerow_e = p.sbuf("gerow", [1, T], F32, scope='bce')
        p.dma('sp', lambda en: en.dma_start(out=gerow_e[:], in_=gdram[e:e + 1, :]),
              writes=[gerow_e])
        p.open_scope('bcps')
        bc_ps = p.psum("bc_ps", [128, 2048], F32, scope='bcps')
        for n0 in range(0, T, 512):
            nn = min(512, T - n0)
            p.op('pe', lambda en: en.matmul(bc_ps[:, n0:n0 + nn],
                                            ones[0:1, :].broadcast_to((1, 128)),
                                            gerow_e[0:1, n0:n0 + nn], start=True, stop=True),
                 reads=[gerow_e, ones], writes=[(bc_ps, n0)])
        p.op('dve', lambda en: en.tensor_copy(bc_sb[:], bc_ps[:, 0:T]),
             reads=[(bc_ps, n0) for n0 in range(0, T, 512)], writes=[bc_sb])
        p.close_scope('bcps')
        hch = [p.sbuf(f"h{hc}", [128, TB], BF16, scope='bce') for hc in range(nh)]
        for tb0 in range(0, T, TB):
            p.open_scope('ph1')
            ps1 = p.psum("ps1", [128, TB], F32, scope='ph1')
            ws1 = [[p.sbuf(f"w1s{b}_{kc}", [128, 512], BF16, scope='ph1')
                    for kc in range(8)] for b in range(2)]
            for hb in range(0, nh, 4):
                ws = ws1[(hb // 4) % 2]
                for kc in range(8):
                    p.dma('pool', lambda en: en.dma_start(
                        out=ws[kc][:],
                        in_=w1[e * D + kc * 128:e * D + (kc + 1) * 128,
                               hb * 128:(hb + 4) * 128]), writes=[ws[kc]])
                for j in range(4):
                    hc = hb + j
                    for kc in range(8):
                        p.op('pe', lambda en: en.matmul(
                            ps1[:, 0:TB], ws[kc][:, j * 128:(j + 1) * 128],
                            ln[kc][:, tb0:tb0 + TB], start=(kc == 0), stop=(kc == 7)),
                            reads=[ws[kc], ln[kc]], writes=[ps1])
                    p.op('act', lambda en: en.activation(
                        hch[hc][:], ps1[:, 0:TB], AF.Relu,
                        bias=b1t[:, e * nh + hc:e * nh + hc + 1]),
                        reads=[ps1, (b1t, e)], writes=[hch[hc]])
            p.close_scope('ph1')
            p.open_scope('ph2')
            ps2 = [p.psum(f"ps2_{mc}", [128, TB], F32, scope='ph2') for mc in range(8)]
            ws2 = [p.sbuf(f"w2s{b}", [128, 1024], BF16, scope='ph2') for b in range(3)]
            for hc in range(nh):
                wb = ws2[hc % 3]
                p.dma('pool', lambda en: en.dma_start(
                    out=wb[:], in_=w2[e * H + hc * 128:e * H + (hc + 1) * 128, :]),
                    writes=[wb])
                for mc in range(8):
                    p.op('pe', lambda en: en.matmul(
                        ps2[mc][:, 0:TB], wb[:, mc * 128:(mc + 1) * 128],
                        hch[hc][:, 0:TB], start=(hc == 0), stop=(hc == nh - 1)),
                        reads=[wb, hch[hc]], writes=[ps2[mc]])
            tmp = p.sbuf("tmp2", [128, TB], F32, scope='ph2')
            for mc in range(8):
                p.op('dve', lambda en: en.scalar_tensor_tensor(
                    tmp[:], ps2[mc][:, 0:TB], b2t[:, e * 8 + mc:e * 8 + mc + 1],
                    bc_sb[:, tb0:tb0 + TB], OP.add, OP.mult),
                    reads=[ps2[mc], (b2t, e), bc_sb], writes=[tmp])
                if first[mc]:
                    p.op('act', lambda en: en.copy(acc[mc][:, tb0:tb0 + TB], tmp[:]),
                         reads=[tmp], writes=[(acc[mc], tb0)])
                else:
                    p.op('dve', lambda en: en.tensor_tensor(
                        acc[mc][:, tb0:tb0 + TB], acc[mc][:, tb0:tb0 + TB], tmp[:], OP.add),
                        reads=[(acc[mc], tb0), tmp], writes=[(acc[mc], tb0)])
            p.close_scope('ph2')
        first = [False] * 8
        p.close_scope('bce')

    store_residual_out(p, ctx, acc, x, y, T=T, D=D, f_dt=F32)




def build_l2_ffn(T=2048):
    p = Prog("x")
    ctx = Ctx(p)
    x = p.dram_in('x', (T, 1024), F32)
    y = p.dram_out('y', (T, 1024), F32)
    emit_l2_ffn(p, ctx, x, y, '', T)
    return p

def ref_l2_ffn(x, gw, gb, w1, b1, w2, b2):
    xs = x.astype(np.float64)
    mu = xs.mean(-1, keepdims=True)
    z = (xs - mu) / np.sqrt(xs.var(-1, keepdims=True) + 1e-5)
    logits = z @ gw.T.astype(np.float64) + gb
    eexp = np.exp(logits - logits.max(-1, keepdims=True))
    probs = eexp / eexp.sum(-1, keepdims=True)
    order = np.argsort(-probs, axis=-1, kind='stable')
    gate = np.zeros_like(probs)
    np.put_along_axis(gate, order[:, :2],
                      np.take_along_axis(probs, order[:, :2], -1), -1)
    out = np.zeros_like(xs[:, :1024])
    for e in range(8):
        h = np.maximum(z @ w1[e].astype(np.float64) + b1[e], 0)
        ye = h @ w2[e].astype(np.float64) + b2[e]
        out += gate[:, e:e + 1] * ye
    return x + out.astype(np.float32)



"""L3_attn_a: standard MHA (16 heads, d=64, non-causal, q scaled by 1/8).
Weights host-transposed to (1024, 1024); biases host-reshaped via col_of.
y = x + (softmax((q/8) k^T) v) @ owT + ob
"""




def emit_l3_attn_a(p, ctx, x, y, pf='', T=2048):
    D = 1024
    qw = p.dram_in_cached(pf + 'qw', (D, D), F32)
    kw = p.dram_in_cached(pf + 'kw', (D, D), F32)
    vw = p.dram_in_cached(pf + 'vw', (D, D), F32)
    ow = p.dram_in_cached(pf + 'ow', (D, D), F32)
    qb = p.dram_in_cached(pf + 'qb', (128, 8), F32)
    kb = p.dram_in_cached(pf + 'kb', (128, 8), F32)
    vb = p.dram_in_cached(pf + 'vb', (128, 8), F32)
    ob = p.dram_in_cached(pf + 'ob', (128, 8), F32)

    stg = stage_input_ln(p, ctx, x, T=T)
    ln = stg['ln_bf']
    qbc = load_col_vec(p, 'qbc', qb, D)
    kbc = load_col_vec(p, 'kbc', kb, D)
    vbc = load_col_vec(p, 'vbc', vb, D)
    obc = load_col_vec(p, 'obc', ob, D)

    ao = [p.sbuf(f"ao{c}", [128, T], BF16) for c in range(8)]
    p.open_scope('qkv')
    qf = [p.sbuf(f"qf{c}", [128, T], BF16, scope='qkv') for c in range(8)]
    kf = [p.sbuf(f"kf{c}", [128, T], BF16, scope='qkv') for c in range(8)]
    vf = [p.sbuf(f"vf{c}", [128, T], BF16, scope='qkv') for c in range(8)]
    proj_fm(p, 'q', qw, ln, T, D, D, bias_col=qbc, out_chunks=qf)
    proj_fm(p, 'k', kw, ln, T, D, D, bias_col=kbc, out_chunks=kf)
    proj_fm(p, 'v', vw, ln, T, D, D, bias_col=vbc, out_chunks=vf)
    attention_full(p, ctx, 'at', qf, kf, vf, ao, T=T, heads=16, hd=64, scale=0.125)
    p.close_scope('qkv')

    out_fm = proj_fm(p, 'o', ow, ao, T, D, D, bias_col=obc, out_dt=BF16)
    store_residual_out(p, ctx, out_fm, x, y, T=T, D=D, f_dt=BF16)




def build_l3_attn_a(T=2048):
    p = Prog("x")
    ctx = Ctx(p)
    x = p.dram_in('x', (T, 1024), F32)
    y = p.dram_out('y', (T, 1024), F32)
    emit_l3_attn_a(p, ctx, x, y, '', T)
    return p

def ref_l3_attn_a(x, qw, qb, kw, kb, vw, vb, ow, ob):
    xs = x.astype(np.float64)
    mu = xs.mean(-1, keepdims=True)
    z = (xs - mu) / np.sqrt(xs.var(-1, keepdims=True) + 1e-5)
    T, D = x.shape
    H, hd = 16, 64
    q = (z @ qw.T.astype(np.float64) + qb) / np.sqrt(hd)
    k = z @ kw.T.astype(np.float64) + kb
    v = z @ vw.T.astype(np.float64) + vb
    q = q.reshape(T, H, hd).transpose(1, 0, 2)
    k = k.reshape(T, H, hd).transpose(1, 0, 2)
    v = v.reshape(T, H, hd).transpose(1, 0, 2)
    s = q @ k.transpose(0, 2, 1)
    s = s - s.max(-1, keepdims=True)
    e = np.exp(s)
    a = e / e.sum(-1, keepdims=True)
    ctx = (a @ v).transpose(1, 0, 2).reshape(T, D)
    return x + (ctx @ ow.T.astype(np.float64) + ob).astype(np.float32)



"""L3_attn_b: qkv = x@W.T + b (W host-transposed to (1024, 3072); b col_of (128,24));
attn[i,j] = softmax_i(q_i . k_j / 1024); out = attn @ v; y = x + out.
Implemented in transposed orientation: ST[j, i] = scores^T; softmax over free i.
"""




def emit_l3_attn_b(p, ctx, x, y, pf='', T=2048):
    D = 1024
    W = p.dram_in_cached(pf + 'W', (D, 3 * D), F32)
    b = p.dram_in_cached(pf + 'b', (128, 24), F32)

    stg = stage_input_ln(p, ctx, x, T=T)
    ln = stg['ln_bf']
    bc = load_col_vec(p, 'bc', b, 3 * D)

    pst = [p.sbuf(f"pst{jc}", [128, T], BF16) for jc in range(T // 128)]
    ident = ctx.get_ident(BF16)
    nj = T // 128
    p.open_scope('A')
    vT = [p.sbuf(f"vT{jc}", [128, D], BF16, scope='A') for jc in range(nj)]
    p.open_scope('A2')
    vf = [p.sbuf(f"vf{c}", [128, T], BF16, scope='A2') for c in range(8)]
    proj_fm(p, 'v', W[0:D, 2 * D:3 * D], ln, T, D, D,
            bias_col=bc, out_chunks=vf, bias_off=16, fetch_w=256)
    tps = p.psum("tps", [128, 128], BF16, scope='A2')
    for jc in range(nj):
        for c in range(8):
            p.op('pe', lambda e: e.transpose(
                tps[:], vf[c][:, jc * 128:(jc + 1) * 128], ident[:]),
                reads=[vf[c], ident], writes=[tps])
            p.op('dve', lambda e: e.tensor_copy(vT[jc][:, c * 128:(c + 1) * 128], tps[:]),
                 reads=[tps], writes=[(vT[jc], c)])
    p.close_scope('A2')
    p.open_scope('B')
    qf = [p.sbuf(f"qf{c}", [128, T], BF16, scope='B') for c in range(8)]
    kf = [p.sbuf(f"kf{c}", [128, T], BF16, scope='B') for c in range(8)]
    proj_fm(p, 'q', W[0:D, 0:D], ln, T, D, D, bias_col=bc, out_chunks=qf, bias_off=0, fetch_w=256)
    proj_fm(p, 'k', W[0:D, D:2 * D], ln, T, D, D, bias_col=bc, out_chunks=kf, bias_off=8, fetch_w=256)

    s_ps = p.psum("s_ps", [128, 2048], F32, scope='B')
    mrow = p.sbuf("mrow", [128, 8], F32, scope='B')
    NS = 512
    for jc in range(nj):
        for n0 in range(0, T, NS):
            nn = min(NS, T - n0)
            for kc in range(8):
                p.op('pe', lambda e: e.matmul(
                    s_ps[:, n0:n0 + nn], kf[kc][:, jc * 128:(jc + 1) * 128],
                    qf[kc][:, n0:n0 + nn], start=(kc == 0), stop=(kc == 7)),
                    reads=[kf[kc], qf[kc]], writes=[(s_ps, n0)])
        stags = [(s_ps, n0) for n0 in range(0, T, NS)]
        p.op('dve', lambda e: e.tensor_reduce(mrow[:, 0:1], s_ps[:, 0:T], AX.X, OP.max),
             reads=stags, writes=[(mrow, 'm')])
        p.op('dve', lambda e: e.tensor_scalar(mrow[:, 1:2], mrow[:, 0:1],
                                              -1.0 / 1024.0, None, OP.mult, OP.bypass),
             reads=[(mrow, 'm')], writes=[(mrow, 'nm')])
        p.op('act', lambda e: e.activation(pst[jc][:], s_ps[:, 0:T], AF.Exp,
                                           bias=mrow[:, 1:2], scale=1.0 / 1024.0,
                                           accum_out=mrow[:, 2:3]),
             reads=stags + [(mrow, 'nm')], writes=[pst[jc], (mrow, 's')])
        p.op('dve', lambda e: e.reciprocal(mrow[:, 3:4], mrow[:, 2:3]),
             reads=[(mrow, 's')], writes=[(mrow, 'r')])
        p.op('dve', lambda e: e.tensor_scalar(pst[jc][:], pst[jc][:], mrow[:, 3:4],
                                              None, OP.mult, OP.bypass),
             reads=[pst[jc], (mrow, 'r')], writes=[pst[jc]])

    p.close_scope('B')
    out_fm = [p.sbuf(f"of{c}", [128, T], BF16) for c in range(8)]
    s_ps = p.psum("s_ps2", [128, 2048], F32, scope='A')
    o_ps = [s_ps]
    for mc in range(8):
        ps = o_ps[0]
        for n0 in range(0, T, NS):
            nn = min(NS, T - n0)
            for jc in range(nj):
                p.op('pe', lambda e: e.matmul(
                    ps[:, n0:n0 + nn], vT[jc][:, mc * 128:(mc + 1) * 128],
                    pst[jc][:, n0:n0 + nn], start=(jc == 0), stop=(jc == nj - 1)),
                    reads=[(vT[jc], 'all'), pst[jc]], writes=[(ps, n0)])
        ptags = [(ps, n0) for n0 in range(0, T, NS)]
        p.op('dve', lambda e: e.tensor_copy(out_fm[mc][:], ps[:, 0:T]),
             reads=ptags, writes=[out_fm[mc]])
    p.close_scope('A')

    store_residual_out(p, ctx, out_fm, x, y, T=T, D=D, f_dt=BF16)




def build_l3_attn_b(T=2048):
    p = Prog("x")
    ctx = Ctx(p)
    x = p.dram_in('x', (T, 1024), F32)
    y = p.dram_out('y', (T, 1024), F32)
    emit_l3_attn_b(p, ctx, x, y, '', T)
    return p

def ref_l3_attn_b(x, W, b):
    xs = x.astype(np.float64)
    mu = xs.mean(-1, keepdims=True)
    z = (xs - mu) / np.sqrt(xs.var(-1, keepdims=True) + 1e-5)
    qkv = z @ W.T.astype(np.float64) + b
    q, k, v = np.split(qkv, 3, axis=-1)
    attn = q @ k.T / k.shape[-1]
    attn = attn - attn.max(axis=0, keepdims=True)
    e = np.exp(attn)
    a = e / e.sum(axis=0, keepdims=True)
    return x + (a @ v).astype(np.float32)



"""L3_ffn: blockwise-FP4 (E2M1) quant-dequant SwiGLU.
w_gate, w_up (2816, 1024) NATURAL; w_down (1024, 2816) NATURAL.
All quantization on device (blocks of 16 along each tensor's last dim).
"""



MID = [0.25, 0.75, 1.25, 1.75, 2.5, 3.5, 5.0]
COEF = [0.5, 0.5, 0.5, 0.5, 1.0, 1.0, 2.0]


def make_quant_scr(p, W, scope=None):
    return dict(
        xa=p.sbuf("q_xa", [128, W], F32, scope=scope),
        sgn=p.sbuf("q_sgn", [128, W], F32, scope=scope),
        sc=p.sbuf("q_sc", [128, W // 16], F32, scope=scope),
        th=p.sbuf("q_th", [128, W // 16], F32, scope=scope),
        acc=p.sbuf("q_acc", [128, W], F32, scope=scope),
        msk=p.sbuf("q_msk", [128, W], F32, scope=scope),
    )


def fp4_quant_tile(p, src_ap, dst_ap, W, scr, reads, writes):
    nb = W // 16
    xa, sgn, sc, th, acc, msk = (scr['xa'], scr['sgn'], scr['sc'], scr['th'],
                                 scr['acc'], scr['msk'])
    p.op('act', lambda e: e.activation(xa[:, 0:W], src_ap, AF.Abs),
         reads=reads, writes=[xa])
    p.op('act', lambda e: e.activation(sgn[:, 0:W], src_ap, AF.Sign),
         reads=reads, writes=[sgn])
    p.op('dve', lambda e: e.tensor_reduce(
        sc[:, 0:nb], xa[:, 0:W].rearrange("p (b s) -> p b s", s=16), AX.X, OP.max),
        reads=[xa], writes=[sc])
    p.op('dve', lambda e: e.tensor_scalar(sc[:, 0:nb], sc[:, 0:nb], 1.0 / 6.0, 1e-12,
                                          OP.mult, OP.max),
         reads=[sc], writes=[sc])
    for k in range(7):
        p.op('dve', (lambda k_: lambda e: e.tensor_scalar(
            th[:, 0:nb], sc[:, 0:nb], MID[k_], None, OP.mult, OP.bypass))(k),
            reads=[sc], writes=[th])
        p.op('dve', lambda e: e.tensor_tensor(
            msk[:, 0:W].rearrange("p (b s) -> p b s", s=16),
            xa[:, 0:W].rearrange("p (b s) -> p b s", s=16),
            th[:, 0:nb].unsqueeze(2).broadcast_to((128, nb, 16)), OP.is_gt),
            reads=[xa, th], writes=[msk])
        if k == 0:
            p.op('dve', (lambda k_: lambda e: e.tensor_scalar(
                acc[:, 0:W], msk[:, 0:W], COEF[k_], None, OP.mult, OP.bypass))(k),
                reads=[msk], writes=[acc])
        else:
            p.op('dve', (lambda k_: lambda e: e.scalar_tensor_tensor(
                acc[:, 0:W], msk[:, 0:W], COEF[k_], acc[:, 0:W], OP.mult, OP.add))(k),
                reads=[msk, acc], writes=[acc])
    p.op('dve', lambda e: e.tensor_tensor(acc[:, 0:W], acc[:, 0:W], sgn[:, 0:W], OP.mult),
         reads=[acc, sgn], writes=[acc])
    p.op('dve', lambda e: e.tensor_tensor(
        dst_ap.rearrange("p (b s) -> p b s", s=16),
        acc[:, 0:W].rearrange("p (b s) -> p b s", s=16),
        sc[:, 0:nb].unsqueeze(2).broadcast_to((128, nb, 16)), OP.mult),
        reads=[acc, sc], writes=writes)


def emit_l3_ffn(p, ctx, x, y, pf='', T=2048):
    D, I = 1024, 2816
    nI = I // 128
    wg = p.dram_in_cached(pf + 'wg', (I, D), F32)
    wu = p.dram_in_cached(pf + 'wu', (I, D), F32)
    wd = p.dram_in_cached(pf + 'wd', (D, I), F32)

    wgq = p.dram_tmp('wgq', (D, I), BF16)
    wuq = p.dram_tmp('wuq', (D, I), BF16)
    wdq = p.dram_tmp('wdq', (I, D), BF16)

    identf = ctx.get_ident(F32)

    def quant_weight(wsrc, wdst, R, C):
        nR, nC = R // 128, C // 128
        SCP = p._uname('_wq')
        p.open_scope(SCP)
        strip = p.sbuf("wq_strip", [128, C], F32, scope=SCP)
        scr = make_quant_scr(p, C, scope=SCP)
        qd = p.sbuf("wq_qd", [128, C], F32, scope=SCP)
        asm = [p.sbuf(f"wq_asm{kc}", [128, R], BF16, scope=SCP) for kc in range(nC)]
        t_ps = p.psum("wq_tps", [128, 128], F32, scope=SCP)
        for s in range(nR):
            p.dma('sp', lambda e: e.dma_start(out=strip[:], in_=wsrc[s * 128:(s + 1) * 128, :]),
                  writes=[strip])
            fp4_quant_tile(p, strip[:], qd[:], C, scr, [strip], [qd])
            for kc in range(nC):
                p.op('pe', lambda e: e.transpose(
                    t_ps[:], qd[:, kc * 128:(kc + 1) * 128], identf[:]),
                    reads=[qd, identf], writes=[t_ps])
                p.op('dve', lambda e: e.tensor_copy(asm[kc][:, s * 128:(s + 1) * 128], t_ps[:]),
                     reads=[t_ps], writes=[(asm[kc], s)])
        for kc in range(nC):
            p.dma('sp', lambda e: e.dma_start(out=wdst[kc * 128:(kc + 1) * 128, :],
                                              in_=asm[kc][:]),
                  reads=[(asm[kc], s) for s in range(nR)])
        p.close_scope(SCP)

    quant_weight(wg, wgq, I, D)
    quant_weight(wu, wuq, I, D)
    quant_weight(wd, wdq, D, I)

    stg = stage_input_ln(p, ctx, x, T=T)
    ln = stg['ln_bf']
    xq = [p.sbuf(f"xq{c}", [128, T], BF16) for c in range(8)]
    p.open_scope('xq')
    xtm = p.sbuf("xtm", [128, D], F32, scope='xq')
    xqtm = p.sbuf("xqtm", [128, D], F32, scope='xq')
    scr = make_quant_scr(p, D, scope='xq')
    t_ps = p.psum("xq_tps", [128, 128], F32, scope='xq')
    t_psb = p.psum("xq_tpsb", [128, 128], BF16, scope='xq')
    nt = T // 128
    for ti in range(nt):
        for c in range(8):
            p.op('pe', lambda e: e.transpose(
                t_psb[:], ln[c][:, ti * 128:(ti + 1) * 128], ctx.get_ident(BF16)[:]),
                reads=[(ln[c], ti), ctx.get_ident(BF16)], writes=[t_psb])
            p.op('dve', lambda e: e.tensor_copy(xtm[:, c * 128:(c + 1) * 128], t_psb[:]),
                 reads=[t_psb], writes=[(xtm, c)])
        fp4_quant_tile(p, xtm[:], xqtm[:], D, scr,
                       [(xtm, c) for c in range(8)], [xqtm])
        for c in range(8):
            p.op('pe', lambda e: e.transpose(
                t_ps[:], xqtm[:, c * 128:(c + 1) * 128], identf[:]),
                reads=[xqtm, identf], writes=[t_ps])
            p.op('dve', lambda e: e.tensor_copy(xq[c][:, ti * 128:(ti + 1) * 128], t_ps[:]),
                 reads=[t_ps], writes=[(xq[c], ti)])
    p.close_scope('xq')

    out_fm = [p.sbuf(f"of{c}", [128, T], BF16) for c in range(8)]
    TB = min(512, T)
    ntb = TB // 128
    for tb0 in range(0, T, TB):
        p.open_scope('blk')
        hch = [p.sbuf(f"h{hc}", [128, TB], BF16, scope='blk') for hc in range(nI)]
        p.open_scope('gu')
        gch = [p.sbuf(f"g{hc}", [128, TB], BF16, scope='gu') for hc in range(nI)]

        def epi_g(mcj, ps, ptags):
            p.op('act', lambda e: e.activation(gch[mcj][:], ps[:, 0:TB], AF.Silu),
                 reads=ptags, writes=[gch[mcj]])

        def epi_u(mcj, ps, ptags):
            p.op('dve', lambda e: e.tensor_tensor(hch[mcj][:], ps[:, 0:TB], gch[mcj][:],
                                                  OP.mult),
                 reads=ptags + [gch[mcj]], writes=[hch[mcj]])

        proj_fm(p, 'pg', wgq, xq, TB, D, I, epilogue=epi_g, x_off=tb0)
        proj_fm(p, 'pu', wuq, xq, TB, D, I, epilogue=epi_u, x_off=tb0)
        p.close_scope('gu')

        p.open_scope('hq')
        htm = p.sbuf("htm", [128, I], F32, scope='hq')
        hqtm = p.sbuf("hqtm", [128, I], F32, scope='hq')
        scrh = make_quant_scr(p, I // 2, scope='hq')
        h_ps = p.psum("hq_tps", [128, 128], F32, scope='hq')
        h_psb = p.psum("hq_tpsb", [128, 128], BF16, scope='hq')
        for sj in range(ntb):
            for hc in range(nI):
                p.op('pe', lambda e: e.transpose(
                    h_psb[:], hch[hc][:, sj * 128:(sj + 1) * 128], ctx.get_ident(BF16)[:]),
                    reads=[hch[hc], ctx.get_ident(BF16)], writes=[h_psb])
                p.op('dve', lambda e: e.tensor_copy(htm[:, hc * 128:(hc + 1) * 128], h_psb[:]),
                     reads=[h_psb], writes=[(htm, hc)])
            fp4_quant_tile(p, htm[:, 0:I // 2], hqtm[:, 0:I // 2], I // 2, scrh,
                           [(htm, hc) for hc in range(nI)], [(hqtm, 0)])
            fp4_quant_tile(p, htm[:, I // 2:I], hqtm[:, I // 2:I], I // 2, scrh,
                           [(htm, hc) for hc in range(nI)], [(hqtm, 1)])
            for hc in range(nI):
                p.op('pe', lambda e: e.transpose(
                    h_ps[:], hqtm[:, hc * 128:(hc + 1) * 128], identf[:]),
                    reads=[(hqtm, 0), (hqtm, 1)], writes=[h_ps])
                p.op('dve', lambda e: e.tensor_copy(hch[hc][:, sj * 128:(sj + 1) * 128], h_ps[:]),
                     reads=[h_ps], writes=[hch[hc]])
        p.close_scope('hq')

        def epi_d(mcj, ps, ptags):
            p.op('dve', (lambda tb0_: lambda e: e.tensor_copy(
                out_fm[mcj][:, tb0_:tb0_ + TB], ps[:, 0:TB]))(tb0),
                reads=ptags, writes=[(out_fm[mcj], tb0)])

        proj_fm(p, 'pd', wdq, hch, TB, I, D, epilogue=epi_d, fetch_w=128)
        p.close_scope('blk')

    store_residual_out(p, ctx, out_fm, x, y, T=T, D=D, f_dt=BF16)




def build_l3_ffn(T=2048):
    p = Prog("x")
    ctx = Ctx(p)
    x = p.dram_in('x', (T, 1024), F32)
    y = p.dram_out('y', (T, 1024), F32)
    emit_l3_ffn(p, ctx, x, y, '', T)
    return p

def fp4_qd_np(x):
    GRID = np.array([0.0, 0.5, 1.0, 1.5, 2.0, 3.0, 4.0, 6.0])
    MIDn = np.array(MID)
    shape = x.shape
    xb = x.reshape(shape[:-1] + (shape[-1] // 16, 16)).astype(np.float64)
    scale = np.maximum(np.abs(xb).max(-1, keepdims=True) / 6.0, 1e-12)
    xn = xb / scale
    idx = np.searchsorted(MIDn, np.abs(xn))
    q = GRID[idx] * np.sign(xn) * scale
    return q.reshape(shape)


def ref_l3_ffn(x, wg, wu, wd):
    xs = x.astype(np.float64)
    mu = xs.mean(-1, keepdims=True)
    z = (xs - mu) / np.sqrt(xs.var(-1, keepdims=True) + 1e-5)
    zq = fp4_qd_np(z)
    gate = zq @ fp4_qd_np(wg.astype(np.float64)).T
    up = zq @ fp4_qd_np(wu.astype(np.float64)).T
    h = gate / (1 + np.exp(-gate)) * up
    hq = fp4_qd_np(h)
    return x + (hq @ fp4_qd_np(wd.astype(np.float64)).T).astype(np.float32)





def _asc(a):
    return np.ascontiguousarray(np.asarray(a))


def _host_layouts(inp):
    W = {}
    W['l0a'] = {'qw': _asc(inp['L0_attn_a_Wq'].T), 'kw': _asc(inp['L0_attn_a_Wk'].T),
                'vw': _asc(inp['L0_attn_a_Wv'].T), 'ow': _asc(inp['L0_attn_a_Wo'].T),
                'qb': col_of(_asc(inp['L0_attn_a_bq'])), 'kb': col_of(_asc(inp['L0_attn_a_bk'])),
                'vb': col_of(_asc(inp['L0_attn_a_bv'])), 'ob': col_of(_asc(inp['L0_attn_a_bo']))}
    keys = _asc(inp['L0_ffn_keys'])
    W['l0f'] = {'wg': _asc(inp['L0_ffn_Wg']), 'wu': _asc(inp['L0_ffn_Wu']),
                'wd': _asc(inp['L0_ffn_Wd']), 'bgc': col_of(_asc(inp['L0_ffn_bg'])),
                'buc': col_of(_asc(inp['L0_ffn_bu'])), 'bdc': col_of(_asc(inp['L0_ffn_bd'])),
                'wq': _asc(inp['L0_ffn_Wq']),
                'keysx': np.concatenate([keys[h, :, 0, :].T for h in range(4)], 0).copy(),
                'keysy': np.concatenate([keys[h, :, 1, :].T for h in range(4)], 0).copy(),
                'det': _asc(inp['L0_ffn_down_embed'].T), 'ue': _asc(inp['L0_ffn_up_embed'])}
    W['l0b'] = {'wq': _asc(inp['L0_attn_b_Wq']), 'wk': _asc(inp['L0_attn_b_Wk']),
                'wv': _asc(inp['L0_attn_b_Wv']), 'wr': _asc(inp['L0_attn_b_Wr']),
                'wo': _asc(inp['L0_attn_b_Wo']), 'bo': col_of(_asc(inp['L0_attn_b_bo'])),
                'gcol': col_of(_asc(inp['L0_attn_b_gamma'])),
                'bcol': col_of(_asc(inp['L0_attn_b_beta'])),
                'ucol': col_of(_asc(inp['L0_attn_b_u_bias']).reshape(-1)),
                'vcol': col_of(_asc(inp['L0_attn_b_v_bias']).reshape(-1)),
                'rpe': _asc(inp['L0_attn_b_rel_pos_emb'])}
    wqkv = _asc(inp['L1_attn_a_W_qkv']).reshape(1024, 3, 1024)
    W['l1a'] = {'wqn': _asc(wqkv[:, 0]), 'wkn': _asc(wqkv[:, 1]), 'wv': _asc(wqkv[:, 2]),
                'wp': _asc(inp['L1_attn_a_W_proj']),
                'bp': col_of(_asc(inp['L1_attn_a_b_proj'])),
                'lq1': _asc(inp['L1_attn_a_lambda_q1']).reshape(1, 32),
                'lk1': _asc(inp['L1_attn_a_lambda_k1']).reshape(1, 32),
                'lq2': _asc(inp['L1_attn_a_lambda_q2']).reshape(1, 32),
                'lk2': _asc(inp['L1_attn_a_lambda_k2']).reshape(1, 32),
                'swc': np.tile(_asc(inp['L1_attn_a_subln_w']), 2).reshape(128, 1).copy()}
    wqkre = np.zeros((1024, 1024), np.float32)
    wqr = _asc(inp['L1_attn_b_w_qk_red'])
    for h in range(8):
        wqkre[:, h * 64:h * 64 + 32] = wqr[:, h * 32:(h + 1) * 32]
        wqkre[:, 512 + h * 64:512 + h * 64 + 32] = wqr[:, 256 + h * 32:256 + (h + 1) * 32]
    W['l1b'] = {'wqkvf': _asc(inp['L1_attn_b_w_qkv_full']), 'wqkre': wqkre,
                'wvr': _asc(inp['L1_attn_b_w_v_red']), 'wp': _asc(inp['L1_attn_b_w_proj'])}
    W['l1f'] = {'wg': _asc(inp['L1_ffn_w_gate'].T), 'wu': _asc(inp['L1_ffn_w_up'].T),
                'wd': _asc(inp['L1_ffn_w_down'].T)}
    W['l2a'] = {'W': _asc(inp['L2_attn_a_W'].T),
                'lgam': col_of(_asc(inp['L2_attn_a_ln_gamma'])),
                'lbet': col_of(_asc(inp['L2_attn_a_ln_beta'])),
                'eta': _asc(inp['L2_attn_a_eta']).reshape(1, 1),
                'llog': _asc(inp['L2_attn_a_lam_logit']).reshape(1, 1)}
    W['l2b'] = make_l2b_inputs(None, _asc(inp['L2_attn_b_norm_w']),
                               _asc(inp['L2_attn_b_norm_b']),
                               _asc(inp['L2_attn_b_in_proj_w']),
                               _asc(inp['L2_attn_b_conv_w']),
                               _asc(inp['L2_attn_b_conv_b']),
                               _asc(inp['L2_attn_b_x_proj_w']),
                               _asc(inp['L2_attn_b_dt_proj_w']),
                               _asc(inp['L2_attn_b_dt_proj_b']),
                               _asc(inp['L2_attn_b_A_log']),
                               _asc(inp['L2_attn_b_D_skip']),
                               _asc(inp['L2_attn_b_out_proj_w']),
                               _asc(inp['L2_attn_b_normf_w']),
                               _asc(inp['L2_attn_b_normf_b']))
    del W['l2b']['x']
    W['l2f'] = {'gw': _asc(inp['L2_ffn_gate_w'].T),
                'gb': _asc(inp['L2_ffn_gate_b']).reshape(1, 8),
                'w1': _asc(inp['L2_ffn_w1']).reshape(8 * 1024, 4096),
                'b1c': np.concatenate([col_of(_asc(inp['L2_ffn_b1'][e])) for e in range(8)], 0),
                'w2': _asc(inp['L2_ffn_w2']).reshape(8 * 4096, 1024),
                'b2c': np.concatenate([col_of(_asc(inp['L2_ffn_b2'][e])) for e in range(8)], 0)}
    W['l3a'] = {'qw': _asc(inp['L3_attn_a_q_w'].T), 'kw': _asc(inp['L3_attn_a_k_w'].T),
                'vw': _asc(inp['L3_attn_a_v_w'].T), 'ow': _asc(inp['L3_attn_a_out_w'].T),
                'qb': col_of(_asc(inp['L3_attn_a_q_b'])), 'kb': col_of(_asc(inp['L3_attn_a_k_b'])),
                'vb': col_of(_asc(inp['L3_attn_a_v_b'])), 'ob': col_of(_asc(inp['L3_attn_a_out_b']))}
    W['l3b'] = {'W': _asc(inp['L3_attn_b_W'].T), 'b': col_of(_asc(inp['L3_attn_b_b']))}
    W['l3f'] = {'wg': _asc(inp['L3_ffn_w_gate']), 'wu': _asc(inp['L3_ffn_w_up']),
                'wd': _asc(inp['L3_ffn_w_down'])}
    return W


_EMITS = {'l0a': emit_l0_attn_a, 'l0f': emit_l0_ffn, 'l0b': emit_l0_attn_b,
          'l1a': emit_l1_attn_a, 'l1b': emit_l1_attn_b, 'l1f': emit_l1_ffn,
          'l2a': emit_l2_attn_a, 'l2b': emit_l2_attn_b, 'l2f': emit_l2_ffn,
          'l3a': emit_l3_attn_a, 'l3b': emit_l3_attn_b, 'l3f': emit_l3_ffn}

_SEQ = ['l0a', 'l0f', 'l0b', 'l0f',
        'l1a', 'l1f', 'l1b', 'l1f',
        'l2a', 'l2f', 'l2b', 'l2f',
        'l3a', 'l3f', 'l3b', 'l3f']

_FUSED = {}

_GROUPS = [['l0a', 'l0f', 'l0b', 'l0f'],
           ['l1a', 'l1f', 'l1b', 'l1f'],
           ['l2a', 'l2f'],
           ['l2b'],
           ['l2f'],
           ['l3a', 'l3f', 'l3b', 'l3f']]


def _build_group(gi, keys, T=2048):
    if gi in _FUSED:
        return _FUSED[gi]
    p = Prog(f"mega{gi}")
    ctx = Ctx(p)
    ctx.get_ident(F32)
    ctx.get_ident(BF16)
    ctx.get_ident_block(64, BF16)
    ctx.get_ones_col(F32)
    ctx.get_ones_col(BF16)
    ctx.cval(1e-5)
    ctx.cval(1e-06 * 1e-05)
    ctx.cval(1.0)
    x0 = p.dram_in('x', (T, 1024), F32)
    xb = [x0] + [p.dram_tmp(f'xb{i}', (T, 1024), F32) for i in range(1, len(keys))]
    xb.append(p.dram_out('y', (T, 1024), F32))
    for i, key in enumerate(keys):
        p.push_default_scope(f'SL{i}')
        _EMITS[key](p, ctx, xb[i], xb[i + 1], pf=key + '_', T=T)
        p.pop_default_scope()
    p.finalize()
    _FUSED[gi] = p
    return p


def kernel(**inputs):
    from concourse.bass_utils import run_bass_kernel_spmd
    x = np.asarray(inputs['L0_attn_a_x']).astype(np.float32)
    _ = np.asarray(inputs.get('L3_attn_a_mask'))
    B = x.shape[0]
    W = _host_layouts(inputs)
    import gc
    xs = [np.ascontiguousarray(x[b]) for b in range(B)]
    for gi, keys in enumerate(_GROUPS):
        pr = _build_group(gi, keys)
        wbg = dict(pr.extra_inputs)
        for key in keys:
            for k, v in W[key].items():
                wbg[key + '_' + k] = v
        in_maps = [dict(wbg, x=xs[b]) for b in range(B)]
        res = run_bass_kernel_spmd(pr.nc, in_maps, core_ids=list(range(B)))
        xs = [np.array(res.results[b]['y'], copy=True) for b in range(B)]
        del res, in_maps, wbg
        _FUSED.pop(gi, None)
        gc.collect()
        try:
            import resource
            print(f"[kernel] group {gi} done rss_gb="
                  f"{resource.getrusage(resource.RUSAGE_SELF).ru_maxrss / 1e6:.1f}")
        except Exception:
            pass
    return np.stack(xs).astype(np.float32)
